# version 08
# baseline (speedup 1.0000x reference)
import contextlib
import math
import numpy as np
import concourse.bass as bass
import concourse.mybir as mybir
from concourse.bass_utils import run_bass_kernel_spmd

F32 = mybir.dt.float32
BF16 = mybir.dt.bfloat16
I32 = mybir.dt.int32
AF = mybir.ActivationFunctionType
ALU = mybir.AluOpType
AX = mybir.AxisListType

NCORES = 8
NA = 4
S = 2048
D = 2048
DC = 16
TOK = 1024
NH = 8
CQ = 1536
CKV = 512
INTER = 5504
IC = 43
ICP = 48
SCALE = 1.0 / math.sqrt(128.0)
NEG = -1.0e30
EPS = 1e-5
FEAT = 6144
N_SUB = 4


class Sem:
    def __init__(self, h):
        self.h = h
        self.n = 0

    def inc(self, instr, k=1):
        instr.then_inc(self.h, k)
        self.n += k
        return self.n



def host_prep(inp):
    C = np.ascontiguousarray
    x = np.asarray(inp["L0_attn_x"], np.float32)
    wdqT = C(np.asarray(inp["L0_attn_w_dq"], np.float32).T)
    wuqT = C(np.asarray(inp["L0_attn_w_uq"], np.float32).T)
    wdkvT = C(np.asarray(inp["L0_attn_w_dkv"], np.float32).T)
    wukvT = C(np.asarray(inp["L0_attn_w_ukv"], np.float32).T)
    woutT = C(np.asarray(inp["L0_attn_w_out"], np.float32).T)
    lut = C(np.asarray(inp["L0_ffn_lut"], np.float32).reshape(1, 16))
    gidxT = C(np.asarray(inp["L0_ffn_gate_idx"], np.int32).T)
    uidxT = C(np.asarray(inp["L0_ffn_up_idx"], np.int32).T)
    didxT = np.asarray(inp["L0_ffn_down_idx"], np.int32).T
    didxT = C(np.concatenate([didxT, np.zeros((ICP * 128 - INTER, D), np.int32)], axis=0))

    winT = C(np.asarray(inp["L1_attn_W_in"], np.float32).T)
    wre = np.empty((D, FEAT), np.float32)
    for j in range(16):
        wre[:, j * 384:j * 384 + 128] = winT[:, j * 128:(j + 1) * 128]
        wre[:, j * 384 + 128:j * 384 + 256] = winT[:, 2048 + j * 128:2048 + (j + 1) * 128]
        wre[:, j * 384 + 256:j * 384 + 384] = winT[:, 4096 + j * 128:4096 + (j + 1) * 128]
    b_in = np.asarray(inp["L1_attn_b_in"], np.float32)
    binT = np.empty((128, 48), np.float32)
    for j in range(16):
        binT[:, 3 * j] = b_in[j * 128:(j + 1) * 128]
        binT[:, 3 * j + 1] = b_in[2048 + j * 128:2048 + (j + 1) * 128]
        binT[:, 3 * j + 2] = b_in[4096 + j * 128:4096 + (j + 1) * 128]
    gammaT = C(np.asarray(inp["L1_attn_gamma"], np.float32).reshape(16, 128).T)
    betaT = C(np.asarray(inp["L1_attn_beta"], np.float32).reshape(16, 128).T)
    wout1T = C(np.asarray(inp["L1_attn_W_out"], np.float32).T)
    boutT = C(np.asarray(inp["L1_attn_b_out"], np.float32).reshape(16, 128).T)
    wfT = C(np.asarray(inp["L1_ffn_weight"], np.float32).T)
    biasT = C(np.asarray(inp["L1_ffn_bias"], np.float32).reshape(16, 128).T)

    maps = []
    for c in range(NCORES):
        b = c // 2
        half = c % 2
        xslice = x[:, 256 * c:256 * c + 256, :]
        xsliceT = C(xslice.transpose(2, 0, 1).reshape(D, TOK))
        maps.append({
            "xTb": C(x[b].T),
            "xsliceT": xsliceT,
            "wdqT": wdqT,
            "wdkvT": wdkvT,
            "wuqT": C(wuqT[:, 1024 * half:1024 * half + 1024]),
            "wukvTk": C(wukvT[:, 1024 * half:1024 * half + 1024]),
            "wukvTv": C(wukvT[:, 2048 + 1024 * half:2048 + 1024 * half + 1024]),
            "woutT": woutT,
            "lutrow": lut,
            "gidxT": C(gidxT[256 * c:256 * c + 256]).reshape(2, 128, INTER),
            "uidxT": C(uidxT[256 * c:256 * c + 256]).reshape(2, 128, INTER),
            "didxT": C(didxT[768 * c:768 * c + 768]).reshape(6, 128, D),
            "winTre": wre,
            "binT": binT,
            "gammaT": gammaT,
            "betaT": betaT,
            "wout1T": wout1T,
            "boutT": boutT,
            "wfT": wfT,
            "biasT": biasT,
        })
    return maps



def emit_ln(nc, blk, PS, sems, xres_v, Bvec, sqbuf, gamma=None, beta=None, out_bf=None):
    r_mu = sqbuf[0][0:1, :]
    r_var = sqbuf[1][0:1, :]
    r_t = Bvec[0:1, 0:1024]
    r_inv = Bvec[0:1, 1024:2048]
    sPe, sVe, sSc = sems["pe"], sems["ve"], sems["sc"]
    ps_ln = _e(PS, nc.psum_tensor("ps_ln_%d" % sPe.n, [128, 2048], F32))
    ps_bc = _e(PS, nc.psum_tensor("ps_bc_%d" % sPe.n, [128, 2048], F32))
    pe0, sc0, ve0 = sPe.n, sSc.n, sVe.n
    rD = 1.0 / float(D)

    @blk.scalar
    def _(sc):
        for k in range(DC):
            if k >= 2:
                sc.wait_ge(sPe.h, pe0 + k - 1)
            i = sc.activation(sqbuf[k % 2][:, :], xres_v[:, k, :], AF.Square)
            sSc.inc(i)

    @blk.tensor
    def _(te):
        for k in range(DC):
            te.wait_ge(sSc.h, sc0 + k + 1)
            for sc_ in range(2):
                te.matmul(ps_ln[0:1, sc_ * 512:sc_ * 512 + 512], onesC_g[:, :],
                          xres_v[:, k, sc_ * 512:sc_ * 512 + 512],
                          start=(k == 0), stop=(k == DC - 1))
                i = te.matmul(ps_ln[0:1, 1024 + sc_ * 512:1024 + sc_ * 512 + 512], onesC_g[:, :],
                              sqbuf[k % 2][:, sc_ * 512:sc_ * 512 + 512],
                              start=(k == 0), stop=(k == DC - 1))
            sPe.inc(i)

    @blk.vector
    def _(v):
        v.wait_ge(sPe.h, pe0 + DC)
        v.tensor_scalar(r_mu, ps_ln[0:1, 0:1024], rD, None, ALU.mult)
        v.tensor_scalar(r_var, ps_ln[0:1, 1024:2048], rD, None, ALU.mult)
        v.drain()
        v.tensor_tensor(r_t, r_mu, r_mu, ALU.mult)
        v.drain()
        v.tensor_tensor(r_var, r_var, r_t, ALU.subtract)
        v.drain()
        i = v.tensor_scalar(r_var, r_var, EPS, None, ALU.add)
        sVe.inc(i)

    @blk.scalar
    def _(sc):
        sc.wait_ge(sVe.h, ve0 + 1)
        i = sc.activation(r_t, r_var, AF.Sqrt)
        sSc.inc(i)

    @blk.vector
    def _(v):
        v.wait_ge(sSc.h, sc0 + DC + 1)
        v.reciprocal(r_inv, r_t)
        v.drain()
        v.tensor_tensor(r_t, r_inv, r_inv, ALU.mult)
        v.drain()
        v.tensor_tensor(r_t, r_t, r_var, ALU.mult)
        v.drain()
        v.tensor_scalar(r_t, r_t, -0.5, 1.5, ALU.mult, ALU.add)
        v.drain()
        i = v.tensor_tensor(r_inv, r_inv, r_t, ALU.mult)
        sVe.inc(i)

    @blk.tensor
    def _(te):
        te.wait_ge(sVe.h, ve0 + 2)
        te.matmul(ps_bc[:, 0:512], onesR_g[:, :], r_mu[:, 0:512], start=True, stop=True)
        te.matmul(ps_bc[:, 512:1024], onesR_g[:, :], r_mu[:, 512:1024], start=True, stop=True)
        te.matmul(ps_bc[:, 1024:1536], onesR_g[:, :], r_inv[:, 0:512], start=True, stop=True)
        i = te.matmul(ps_bc[:, 1536:2048], onesR_g[:, :], r_inv[:, 512:1024], start=True, stop=True)
        sPe.inc(i)

    @blk.vector
    def _(v):
        v.wait_ge(sPe.h, pe0 + DC + 1)
        i = v.tensor_copy(Bvec[:, 0:2048], ps_bc[:, :])
        sVe.inc(i)
        for k in range(DC):
            v.drain()
            v.tensor_tensor(sqbuf[k % 2][:, :], xres_v[:, k, :], Bvec[:, 0:1024], ALU.subtract)
            v.drain()
            if gamma is None:
                i = v.tensor_tensor(xres_v[:, k, :], sqbuf[k % 2][:, :], Bvec[:, 1024:2048], ALU.mult)
            else:
                v.tensor_tensor(sqbuf[k % 2][:, :], sqbuf[k % 2][:, :], Bvec[:, 1024:2048], ALU.mult)
                v.drain()
                i = v.tensor_scalar(xres_v[:, k, :], sqbuf[k % 2][:, :],
                                    gamma[:, k:k + 1], beta[:, k:k + 1], ALU.mult, ALU.add)
            if out_bf is not None:
                v.drain()
                i = v.tensor_copy(out_bf[:, k, :], xres_v[:, k, :])
        sVe.inc(i)


def _e(stack, ctx):
    return stack.enter_context(ctx)


class GatherEmitter:

    def __init__(self, nc, itb, ibb, acc, tmpg, lutB, quarters, sIdx, sGs, sGd, ALU):
        self.quarters = quarters
        self.sIdx = sIdx
        self.sGs = sGs
        self.sGd = sGd
        self.itb_aps = itb
        self.acc_aps = acc
        self.gd0 = sGd.n
        self.ops = []
        nq = len(quarters)

        def mk(fn):
            self.ops.append(fn)

        for qi, (srcap, cols, dstap) in enumerate(quarters):
            def startf(qi=qi, cols=cols):
                def f(v):
                    v.wait_ge(sIdx[qi % 2].h, 16 * (qi // 2 + 1))
                    if qi >= 2:
                        v.wait_ge(sGs[qi % 2].h, 16 * (qi // 2))
                    v.drain()
                    v.tensor_copy(ibb[:, 0:cols], itb[qi % 2][:, 0:cols])
                return f
            mk(startf())
            def p0(qi=qi, cols=cols):
                def f(v):
                    v.drain()
                    v.tensor_scalar(acc[qi % 2][:, 0:cols], ibb[:, 0:cols],
                                    0.0, lutB[:, 0:1], ALU.is_equal, ALU.mult)
                return f
            mk(p0())
            for j in range(1, 16):
                def pj(qi=qi, cols=cols, j=j):
                    def f(v):
                        v.drain()
                        v.tensor_scalar(tmpg[:, 0:cols], ibb[:, 0:cols],
                                        float(j), lutB[:, j:j + 1], ALU.is_equal, ALU.mult)
                        v.drain()
                        i = v.tensor_tensor(acc[qi % 2][:, 0:cols], acc[qi % 2][:, 0:cols],
                                            tmpg[:, 0:cols], ALU.add)
                        if j == 15:
                            i.then_inc(sGd.h, 1)
                            sGd.n += 1
                    return f
                mk(pj())
        self.pos = 0
        self.ops_per_q = len(self.ops) // nq
        self.store_totals = []
        tot = [0, 0]
        for qi in range(nq):
            tot[qi % 2] += 16
            self.store_totals.append(tuple(tot))

    @property
    def done_quarters(self):
        return self.pos // self.ops_per_q

    def emit(self, v, k):
        end = min(self.pos + k, len(self.ops))
        while self.pos < end:
            self.ops[self.pos](v)
            self.pos += 1

    def flush(self, v):
        self.emit(v, len(self.ops))

    def dma_chain(self, eng, q_from, q_to, extra=None):
        N = len(self.quarters)

        def load(q):
            srcap, cols, dstap = self.quarters[q]
            eng.dma_start(self.itb_aps[q % 2][:, 0:cols], srcap).then_inc(self.sIdx[q % 2].h, 16)
            self.sIdx[q % 2].n += 16

        if q_from == 0:
            for q in (0, 1):
                if q < N:
                    load(q)
        for q in range(q_from, q_to):
            if extra and q in extra:
                extra[q](eng)
            eng.wait_ge(self.sGd.h, self.gd0 + q + 1)
            srcap, cols, dstap = self.quarters[q]
            eng.dma_start(dstap, self.acc_aps[q % 2][:, 0:cols]).then_inc(self.sGs[q % 2].h, 16)
            self.sGs[q % 2].n += 16
            nxt = q + 2
            if nxt < N and nxt < q_to + 2:
                load(nxt)


onesC_g = None
onesR_g = None



def build_nc():
    global onesC_g, onesR_g
    nc = bass.Bass("TRN2", target_bir_lowering=False)
    ap = lambda name, shape, dt, kind: nc.dram_tensor(name, shape, dt, kind=kind).ap()
    aps = lambda name, shape, dt: nc.dram_tensor(name, shape, dt, kind="Internal",
                                                 addr_space="Shared").ap()
    EI, EO, IN = "ExternalInput", "ExternalOutput", "Internal"

    xTb = ap("xTb", [D, S], F32, EI)
    xsliceT = ap("xsliceT", [D, TOK], F32, EI)
    wdqT = ap("wdqT", [D, CQ], F32, EI)
    wdkvT = ap("wdkvT", [D, CKV], F32, EI)
    wuqT = ap("wuqT", [CQ, 1024], F32, EI)
    wukvTk = ap("wukvTk", [CKV, 1024], F32, EI)
    wukvTv = ap("wukvTv", [CKV, 1024], F32, EI)
    woutT = ap("woutT", [D, D], F32, EI)
    lutrow = ap("lutrow", [1, 16], F32, EI)
    gidxT = ap("gidxT", [2, 128, INTER], I32, EI)
    uidxT = ap("uidxT", [2, 128, INTER], I32, EI)
    didxT = ap("didxT", [6, 128, D], I32, EI)
    winTre = ap("winTre", [D, FEAT], F32, EI)
    binT = ap("binT", [128, 48], F32, EI)
    gammaT = ap("gammaT", [128, 16], F32, EI)
    betaT = ap("betaT", [128, 16], F32, EI)
    wout1T = ap("wout1T", [D, D], F32, EI)
    boutT = ap("boutT", [128, 16], F32, EI)
    wfT = ap("wfT", [D, D], F32, EI)
    biasT = ap("biasT", [128, 16], F32, EI)

    out = ap("out", [DC, 128, TOK], F32, EO)

    ag_o_in = [ap("ag_o_in0", [128, 6 * S], BF16, IN), ap("ag_o_in1", [128, 2 * S], BF16, IN)]
    ag_o_out = [aps("ag_o_out0", [NCORES, 128, 6 * S], BF16),
                aps("ag_o_out1", [NCORES, 128, 2 * S], BF16)]
    GOFF = 0
    UOFF = 2 * 128 * INTER
    GUP = 4 * 128 * INTER
    DLEN = 6 * 128 * D
    ag_gu_in = ap("ag_gu_in", [GUP], BF16, IN)
    ag_gu_out = aps("ag_gu_out", [NCORES, GUP], BF16)
    ag_dn_in = ap("ag_dn_in", [DLEN], BF16, IN)
    ag_dn_out = aps("ag_dn_out", [NCORES, DLEN], BF16)
    x_sp = [ap(f"x_sp{i}", [128, DC * TOK], F32, IN) for i in range(2)]

    RG = [list(range(NCORES))]
    rank = nc.axis_index(bass.AxisInfo(NCORES, 1))

    with contextlib.ExitStack() as ST:
        onesC = _e(ST, nc.sbuf_tensor("onesC", [128, 1], F32))
        onesR = _e(ST, nc.sbuf_tensor("onesR", [1, 128], F32))
        identF = _e(ST, nc.sbuf_tensor("identF", [128, 128], F32))
        Mneg = _e(ST, nc.sbuf_tensor("Mneg", [128, 128], BF16))
        MnegT = _e(ST, nc.sbuf_tensor("MnegT", [128, 128], BF16))
        pcol = _e(ST, nc.sbuf_tensor("pcol", [128, 1], F32))
        lutB = _e(ST, nc.sbuf_tensor("lutB", [128, 16], F32))
        lutrow_sb = _e(ST, nc.sbuf_tensor("lutrow_sb", [1, 16], F32))
        acols = _e(ST, nc.sbuf_tensor("acols", [128, 48], F32))
        DTS = _e(ST, contextlib.ExitStack())
        DTA = [_e(DTS, nc.sbuf_tensor(f"DTA{m}", [128, 128], F32)) for m in range(2)]
        DTB = [_e(DTS, nc.sbuf_tensor(f"DTB{m}", [128, 128], F32)) for m in range(2)]
        DTC = [_e(DTS, nc.sbuf_tensor(f"DTC{m}", [128, 128], F32)) for m in range(2)]
        onesC_g, onesR_g = onesC, onesR

        sLd = Sem(_e(ST, nc.semaphore("sLd")))
        sLg = Sem(_e(ST, nc.semaphore("sLg")))
        sSt = Sem(_e(ST, nc.semaphore("sSt")))
        sPe = Sem(_e(ST, nc.semaphore("sPe")))
        sVe = Sem(_e(ST, nc.semaphore("sVe")))
        sSc = Sem(_e(ST, nc.semaphore("sSc")))
        sGp = Sem(_e(ST, nc.semaphore("sGp")))
        sCc = Sem(_e(ST, nc.semaphore("sCc")))
        sOd = [Sem(_e(ST, nc.semaphore(f"sOd{i}"))) for i in range(2)]
        sIdx = [Sem(_e(ST, nc.semaphore(f"sIdx{i}"))) for i in range(2)]
        sGs = [Sem(_e(ST, nc.semaphore(f"sGs{i}"))) for i in range(2)]
        sGU = [Sem(_e(ST, nc.semaphore(f"sGU{i}"))) for i in range(4)]
        sWd = [Sem(_e(ST, nc.semaphore(f"sWd{i}"))) for i in range(2)]
        sWc = [Sem(_e(ST, nc.semaphore(f"sWc{i}"))) for i in range(2)]
        sWf = [Sem(_e(ST, nc.semaphore(f"sWf{i}"))) for i in range(2)]
        sWq = [Sem(_e(ST, nc.semaphore(f"sWq{i}"))) for i in range(2)]
        sGd = Sem(_e(ST, nc.semaphore("sGd")))
        sLdh = [Sem(_e(ST, nc.semaphore(f"sLdh{i}"))) for i in range(2)]
        sWo = Sem(_e(ST, nc.semaphore("sWo")))
        sems = {"pe": sPe, "ve": sVe, "sc": sSc}

        with nc.Block() as blk, contextlib.ExitStack() as PS:
            ps16 = _e(PS, nc.psum_tensor("ps16", [128, 16], F32))

            @blk.sync
            def _(s):
                s.dma_start(lutrow_sb[:], lutrow).then_inc(sLd.h, 16)
            sLd.n += 16

            S0S = _e(PS, contextlib.ExitStack())
            frow = _e(S0S, nc.sbuf_tensor("frow", [128, 128], F32))
            firow = _e(S0S, nc.sbuf_tensor("firow", [128, 128], I32))
            iw1 = _e(S0S, nc.sbuf_tensor("iw1", [128, 128], I32))
            iw2 = _e(S0S, nc.sbuf_tensor("iw2", [128, 128], I32))
            fw1 = _e(S0S, nc.sbuf_tensor("fw1", [128, 128], F32))
            fw2 = _e(S0S, nc.sbuf_tensor("fw2", [128, 128], F32))

            @blk.gpsimd
            def _(g):
                g.iota(frow[:], [[1, 128]], base=0, channel_multiplier=0, allow_small_or_imprecise_dtypes=True)
                g.drain()
                g.iota(pcol[:], [[1, 1]], base=0, channel_multiplier=1, allow_small_or_imprecise_dtypes=True)
                g.drain()
                i = g.iota(firow[:], [[1, 128]], base=0, channel_multiplier=0)
                sGp.inc(i)

            @blk.vector
            def _(v):
                v.memset(onesC[:], 1.0)
                v.memset(onesR[:], 1.0)
                v.wait_ge(sGp.h, sGp.n)
                v.tensor_scalar(identF[:], frow[:], pcol[:, :], 1.0, ALU.is_equal, ALU.mult)
                v.tensor_scalar(Mneg[:], frow[:], pcol[:, :], NEG, ALU.is_gt, ALU.mult)
                i = v.tensor_scalar(MnegT[:], frow[:], pcol[:, :], NEG, ALU.is_lt, ALU.mult)
                sVe.inc(i)

            @blk.tensor
            def _(te):
                te.wait_ge(sLd.h, sLd.n)
                te.wait_ge(sVe.h, sVe.n)
                i = te.matmul(ps16[:, :], onesR[:, :], lutrow_sb[:, :], start=True, stop=True)
                sPe.inc(i)

            @blk.vector
            def _(v):
                v.wait_ge(sPe.h, sPe.n)
                i = v.tensor_copy(lutB[:, :], ps16[:, :])
                sVe.inc(i)

            @blk.vector
            def _(v):
                v.wait_ge(sGp.h, sGp.n)
                for m in range(2):
                    v.tensor_scalar(iw1[:], firow[:], 2, None, ALU.arith_shift_right)
                    v.tensor_scalar(iw2[:], firow[:], 1, None, ALU.arith_shift_right)
                    v.drain()
                    v.tensor_scalar(iw1[:], iw1[:], 1, None, ALU.arith_shift_left)
                    v.tensor_scalar(iw2[:], iw2[:], 1, None, ALU.bitwise_and)
                    v.drain()
                    v.tensor_tensor(iw1[:], iw1[:], iw2[:], ALU.add)
                    v.drain()
                    v.tensor_scalar(fw1[:], iw1[:], float(64 * m), None, ALU.add)
                    v.drain()
                    v.tensor_scalar(DTA[m][:], fw1[:], pcol[:, :], 1.0,
                                    ALU.is_equal, ALU.mult)
                    v.drain()
                    v.tensor_scalar(iw1[:], firow[:], 2, None, ALU.arith_shift_right)
                    v.tensor_scalar(iw2[:], firow[:], 1, None, ALU.bitwise_and)
                    v.drain()
                    v.tensor_scalar(iw1[:], iw1[:], 1, None, ALU.arith_shift_left)
                    v.drain()
                    v.tensor_tensor(iw1[:], iw1[:], iw2[:], ALU.add)
                    v.drain()
                    v.tensor_scalar(fw1[:], iw1[:], float(64 * m), None, ALU.add)
                    v.drain()
                    v.tensor_scalar(DTB[m][:], fw1[:], pcol[:, :], 1.0,
                                    ALU.is_equal, ALU.mult)
                    v.drain()
                    v.tensor_scalar(iw1[:], firow[:], 1, None, ALU.arith_shift_right)
                    v.tensor_scalar(iw2[:], firow[:], 1, None, ALU.bitwise_and)
                    v.drain()
                    v.tensor_scalar(iw1[:], iw1[:], 32 * m, None, ALU.subtract)
                    v.drain()
                    v.tensor_scalar(iw1[:], iw1[:], 2, None, ALU.arith_shift_left)
                    v.drain()
                    v.tensor_tensor(iw1[:], iw1[:], iw2[:], ALU.add)
                    v.drain()
                    v.tensor_scalar(fw1[:], iw1[:], 0.0, None, ALU.add)
                    v.drain()
                    v.tensor_scalar(fw2[:], fw1[:], pcol[:, :], -1.0,
                                    ALU.is_equal, ALU.mult)
                    v.tensor_scalar(fw1[:], fw1[:], 2.0, None, ALU.add)
                    v.drain()
                    v.tensor_scalar(fw1[:], fw1[:], pcol[:, :], -1.0,
                                    ALU.is_equal, ALU.mult)
                    v.drain()
                    i = v.tensor_tensor(DTC[m][:], fw1[:], fw2[:], ALU.add)
                    sVe.inc(i)


        SG = _e(ST, contextlib.ExitStack())
        g_itb = [_e(SG, nc.sbuf_tensor(f"g_itb{i}", [128, 688], I32)) for i in range(2)]
        g_ibb = _e(SG, nc.sbuf_tensor("g_ibb", [128, 688], BF16))
        g_acc = [_e(SG, nc.sbuf_tensor(f"g_acc{i}", [128, 688], BF16)) for i in range(2)]
        g_tmp = _e(SG, nc.sbuf_tensor("g_tmp", [128, 688], BF16))
        quarters = []
        for agt, base, idxt, ntile, tcols, bounds in (
                (ag_gu_in, GOFF, gidxT, 2, INTER, [688 * i for i in range(8)] + [INTER]),
                (ag_gu_in, UOFF, uidxT, 2, INTER, [688 * i for i in range(8)] + [INTER]),
                (ag_dn_in, 0, didxT, 6, D, [0, 688, 1376, 2048])):
            for t in range(ntile):
                tb = base + t * 128 * tcols
                tile_dst = agt[tb:tb + 128 * tcols].rearrange("(p c) -> p c", p=128)
                for q in range(len(bounds) - 1):
                    c0_, c1_ = bounds[q], bounds[q + 1]
                    quarters.append((idxt[t][:, c0_:c1_], c1_ - c0_, tile_dst[:, c0_:c1_]))
        GE = GatherEmitter(nc, [t[:] for t in g_itb], g_ibb[:], [a[:] for a in g_acc],
                           g_tmp[:], lutB[:], quarters, sIdx, sGs, sGd, ALU)
        N_GU_Q = 32

        S_attn = _e(ST, contextlib.ExitStack())
        xTbS = _e(S_attn, nc.sbuf_tensor("xTbS", [128, DC * S], BF16))
        wdqS = _e(S_attn, nc.sbuf_tensor("wdqS", [128, DC * CQ], BF16))
        wdkvS = _e(S_attn, nc.sbuf_tensor("wdkvS", [128, DC * CKV], BF16))
        xTbS_v = xTbS[:].rearrange("p (k s) -> p k s", k=DC)
        wdqS_v = wdqS[:].rearrange("p (k j) -> p k j", k=DC)
        wdkvS_v = wdkvS[:].rearrange("p (k j) -> p k j", k=DC)
        qT_v = xTbS[:, 0:NH * S].rearrange("p (h s) -> p h s", h=NH)
        kT_v = xTbS[:, NH * S:2 * NH * S].rearrange("p (h s) -> p h s", h=NH)
        Vt_v = wdqS[:, 0:16 * 1024].rearrange("p (t r) -> p t r", t=16)
        wuk_v = wdqS[:, 16 * 1024:16 * 1024 + 4 * 1024].rearrange("p (k j) -> p k j", k=4)
        wuq_sl = [wdqS[:, 20 * 1024 + i * 1536:20 * 1024 + (i + 1) * 1536]
                  .rearrange("p (k j) -> p k j", k=12) for i in range(2)]
        wuv_v = wdkvS[:, 0:4 * 1024].rearrange("p (k j) -> p k j", k=4)

        S_c = _e(S_attn, contextlib.ExitStack())
        cqT = _e(S_c, nc.sbuf_tensor("cqT", [128, 12 * S], BF16))
        ckvT = _e(S_c, nc.sbuf_tensor("ckvT", [128, 4 * S], BF16))
        cqT_v = cqT[:].rearrange("p (j s) -> p j s", j=12)
        ckvT_v = ckvT[:].rearrange("p (j s) -> p j s", j=4)

        with nc.Block() as blk, contextlib.ExitStack() as PS:
            psA = [_e(PS, nc.psum_tensor(f"psA{i}", [128, 2048], F32)) for i in range(2)]

            lg0a = sLg.n

            @blk.gpsimd
            def _(g):
                xv = xTb.rearrange("(k p) s -> p k s", p=128)
                wv = wdqT.rearrange("(k p) j -> p k j", p=128)
                for k in range(DC):
                    g.dma_start(xTbS_v[:, k, :], xv[:, k, :]).then_inc(sLg.h, 16)
                    g.dma_start(wdqS_v[:, k, :], wv[:, k, :]).then_inc(sLg.h, 16)
                g.dma_start(wdkvS_v, wdkvT.rearrange("(k p) j -> p k j", p=128)).then_inc(sLg.h, 16)
            sLg.n += 33 * 16
            p0, v0 = sPe.n, sVe.n
            jobs = [(cqT_v, wdqS_v, 12), (ckvT_v, wdkvS_v, 4)]

            @blk.tensor
            def _(te):
                it = 0
                for dst, wS, nj in jobs:
                    for j in range(nj):
                        if it >= 2:
                            te.wait_ge(sVe.h, v0 + it - 1)
                        for k in range(DC):
                            if it == 0:
                                te.wait_ge(sLg.h, lg0a + 32 * (k + 1))
                            elif it == 12 and k == 0:
                                te.wait_ge(sLg.h, sLg.n)
                            for sc_ in range(4):
                                i = te.matmul(psA[it % 2][:, sc_ * 512:sc_ * 512 + 512],
                                              wS[:, k, j * 128:j * 128 + 128],
                                              xTbS_v[:, k, sc_ * 512:sc_ * 512 + 512],
                                              start=(k == 0), stop=(k == DC - 1))
                        it += 1
                        sPe.inc(i)

            @blk.vector
            def _(v):
                it = 0
                for dst, wS, nj in jobs:
                    for j in range(nj):
                        v.wait_ge(sPe.h, p0 + it + 1)
                        i = v.tensor_copy(dst[:, j, :], psA[it % 2][:, :])
                        it += 1
                        sVe.inc(i)

        with nc.Block() as blk, contextlib.ExitStack() as PS:
            psA = [_e(PS, nc.psum_tensor(f"psB{i}", [128, 2048], F32)) for i in range(2)]
            p0, v0 = sPe.n, sVe.n

            def a2_sched():
                for h in range(NH):
                    yield ("q", h)
                for h in range(NH):
                    yield ("k", h)
                for tb in range(16):
                    yield ("v", tb)

            pe_at2 = {}
            psim = p0
            for it, _kind in enumerate(a2_sched()):
                psim += 1
                pe_at2[it] = psim

            @blk.gpsimd
            def _(g):
                g.dma_start(wuk_v, wukvTk.rearrange("(k p) j -> p k j", p=128)).then_inc(sLg.h, 16)
                g.dma_start(wuv_v, wukvTv.rearrange("(k p) j -> p k j", p=128)).then_inc(sLg.h, 16)
                for h in range(NH):
                    if h >= 2:
                        g.wait_ge(sPe.h, pe_at2[h - 2])
                    g.dma_start(wuq_sl[h % 2],
                                wuqT[:, h * 128:h * 128 + 128]
                                .rearrange("(k p) j -> p k j", p=128)).then_inc(sWq[h % 2].h, 16)
                    sWq[h % 2].n += 16
            sLg.n += 32
            lg_a2 = sLg.n

            @blk.tensor
            def _(te):
                for it, (kind, idx) in enumerate(a2_sched()):
                    if it >= 2:
                        te.wait_ge(sVe.h, v0 + it - 1)
                    if kind == "q":
                        te.wait_ge(sWq[idx % 2].h, 16 * (idx // 2 + 1))
                        for k in range(12):
                            for sc_ in range(4):
                                i = te.matmul(psA[it % 2][:, sc_ * 512:sc_ * 512 + 512],
                                              wuq_sl[idx % 2][:, k, :],
                                              cqT_v[:, k, sc_ * 512:sc_ * 512 + 512],
                                              start=(k == 0), stop=(k == 11))
                    elif kind == "k":
                        te.wait_ge(sLg.h, lg_a2)
                        for k in range(4):
                            for sc_ in range(4):
                                i = te.matmul(psA[it % 2][:, sc_ * 512:sc_ * 512 + 512],
                                              wuk_v[:, k, idx * 128:idx * 128 + 128],
                                              ckvT_v[:, k, sc_ * 512:sc_ * 512 + 512],
                                              start=(k == 0), stop=(k == 3))
                    else:
                        for k in range(4):
                            for sc_ in range(2):
                                i = te.matmul(psA[it % 2][:, sc_ * 512:sc_ * 512 + 512],
                                              ckvT_v[:, k, idx * 128:idx * 128 + 128],
                                              wuv_v[:, k, sc_ * 512:sc_ * 512 + 512],
                                              start=(k == 0), stop=(k == 3))
                    sPe.inc(i)

            @blk.vector
            def _(v):
                for it, (kind, idx) in enumerate(a2_sched()):
                    v.wait_ge(sPe.h, p0 + it + 1)
                    if kind == "q":
                        i = v.tensor_copy(qT_v[:, idx, :], psA[it % 2][:, :])
                    elif kind == "k":
                        i = v.tensor_copy(kT_v[:, idx, :], psA[it % 2][:, :])
                    else:
                        i = v.tensor_copy(Vt_v[:, idx, :], psA[it % 2][:, 0:1024])
                    sVe.inc(i)
        S_c.close()

        with contextlib.ExitStack() as PH:
            Pscr = [_e(PH, nc.sbuf_tensor(f"Pscr{i}", [128, 2048], BF16)) for i in range(2)]
            negb = [_e(PH, nc.sbuf_tensor(f"negb{i}", [128, 1], F32)) for i in range(2)]
            mcols = _e(PH, nc.sbuf_tensor("mcols", [128, 128], F32))
            lcols = _e(PH, nc.sbuf_tensor("lcols", [128, 128], F32))
            lncols = _e(PH, nc.sbuf_tensor("lncols", [128, 128], F32))
            bcol = _e(PH, nc.sbuf_tensor("bcol", [128, 128], F32))
            Bsb = _e(PH, nc.sbuf_tensor("Bsb", [128, 2048], F32))
            Pin = [_e(PH, nc.sbuf_tensor(f"Pin{i}", [128, 512], F32)) for i in range(2)]
            PTc = [_e(PH, nc.sbuf_tensor(f"PTc{i}", [128, 512], BF16)) for i in range(2)]
            oTh = [_e(PH, nc.sbuf_tensor(f"oTh{i}", [128, 2048], BF16)) for i in range(2)]

            pairs = [(h, qb) for h in range(NH) for qb in range(16)]
            with nc.Block() as blk, contextlib.ExitStack() as PS:
                psS = [_e(PS, nc.psum_tensor(f"psS{i}", [128, 2048], F32)) for i in range(2)]
                p0, v0, c0 = sPe.n, sVe.n, sSc.n

                @blk.tensor
                def _(te):
                    for it, (h, qb) in enumerate(pairs):
                        ext = (qb + 1) * 128
                        if it >= 2:
                            te.wait_ge(sSc.h, c0 + it - 1)
                        for k0 in range(0, ext, 512):
                            w = min(512, ext - k0)
                            i = te.matmul(psS[it % 2][:, k0:k0 + w],
                                          qT_v[:, h, qb * 128:qb * 128 + 128],
                                          kT_v[:, h, k0:k0 + w], start=True, stop=True)
                        sPe.inc(i)

                @blk.vector
                def _(v):
                    for it, (h, qb) in enumerate(pairs):
                        ext = (qb + 1) * 128
                        v.wait_ge(sPe.h, p0 + it + 1)
                        v.tensor_tensor(psS[it % 2][:, qb * 128:qb * 128 + 128],
                                        psS[it % 2][:, qb * 128:qb * 128 + 128], Mneg[:, :], ALU.add)
                        v.drain()
                        v.tensor_reduce(mcols[:, it:it + 1], psS[it % 2][:, 0:ext], AX.X, ALU.max)
                        v.drain()
                        i = v.tensor_scalar(negb[it % 2][:, :], mcols[:, it:it + 1], -SCALE, None, ALU.mult)
                        sVe.inc(i)
                        GE.emit(v, 1)

                @blk.scalar
                def _(sc):
                    for it, (h, qb) in enumerate(pairs):
                        ext = (qb + 1) * 128
                        sc.wait_ge(sVe.h, v0 + it + 1)
                        i = sc.activation(Pscr[it % 2][:, 0:ext], psS[it % 2][:, 0:ext], AF.Exp,
                                          bias=negb[it % 2][:, :], scale=SCALE,
                                          accum_out=lcols[:, it:it + 1])
                        sSc.inc(i)

                B1q = GE.done_quarters

                @blk.sync
                def _(s):
                    GE.dma_chain(s, 0, B1q)

            with nc.Block() as blk:
                @blk.scalar
                def _(sc):
                    sc.drain()
                    i = sc.activation(lncols[:, :], lcols[:, :], AF.Ln)
                    sSc.inc(i)

                @blk.vector
                def _(v):
                    v.wait_ge(sSc.h, sSc.n)
                    i = v.scalar_tensor_tensor(bcol[:, :], mcols[:, :], SCALE, lncols[:, :],
                                               ALU.mult, ALU.add)
                    sVe.inc(i)

            with nc.Block() as blk, contextlib.ExitStack() as PS:
                psO = _e(PS, nc.psum_tensor("psO", [128, 2048], F32))
                psX = _e(PS, nc.psum_tensor("psX", [128, 2048], F32))
                p0, v0, c0, st0 = sPe.n, sVe.n, sSc.n, sSt.n

                def head_chunks(h):
                    ch = []
                    for kb in range(16):
                        q0 = kb * 128
                        c = q0
                        while c < S:
                            up = min(S, (c // 512 + 1) * 512)
                            bank = c // 512
                            stop = (kb == min(15, 4 * bank + 3))
                            ch.append((kb, c, up - c, c == q0, stop))
                            c = up
                    return ch

                pe_n, ve_n, sc_n, st_n = [p0], [v0], [c0], [st0]
                tr_done_at = {}
                bevac_at = {}
                stt_at = {}
                exp_at = {}
                oevac_at = {}
                odma_at = {}
                vsim = v0
                for h in range(NH):
                    for qb in range(16):
                        vsim += 1
                        tr_done_at[(h, qb)] = vsim
                    vsim += 1
                    bevac_at[h] = vsim
                    for ci in range(len(head_chunks(h))):
                        vsim += 1
                        stt_at[(h, ci)] = vsim
                    vsim += 1
                    oevac_at[h] = vsim
                csim = c0
                for h in range(NH):
                    for ci in range(len(head_chunks(h))):
                        csim += 1
                        exp_at[(h, ci)] = csim
                for h in range(NH):
                    odma_at[h] = 16 * (h // 2 + 1)
                pe_tr_at = {}
                pe_b_at = {}
                pe_st_at = {}
                pe_pv_at = {}
                psim = p0
                for h in range(NH):
                    for qb in range(16):
                        psim += 1
                        pe_tr_at[(h, qb)] = psim
                    psim += 1
                    pe_b_at[h] = psim
                    for ci in range(len(head_chunks(h))):
                        psim += 1
                        pe_st_at[(h, ci)] = psim
                    psim += 1
                    pe_pv_at[h] = psim

                @blk.tensor
                def _(te):
                    for h in range(NH):
                        chunks = head_chunks(h)
                        for qb in range(16):
                            if qb >= 2:
                                te.wait_ge(sVe.h, tr_done_at[(h, qb - 2)])
                            elif h > 0 and qb == 0:
                                te.wait_ge(sVe.h, oevac_at[h - 1])
                            i = te.transpose(psX[0:1, (qb % 2) * 512:(qb % 2) * 512 + 128],
                                             bcol[:, h * 16 + qb:h * 16 + qb + 1], identF[:, :])
                            sPe.inc(i)
                        te.wait_ge(sVe.h, tr_done_at[(h, 15)])
                        for seg in range(4):
                            i = te.matmul(psX[:, seg * 512:seg * 512 + 512], onesR[:, :],
                                          Bsb[0:1, seg * 512:seg * 512 + 512], start=True, stop=True)
                        sPe.inc(i)
                        te.wait_ge(sVe.h, bevac_at[h])
                        if h > 0:
                            te.wait_ge(sVe.h, oevac_at[h - 1])
                        for ci, (kb, q0, w, first, stop) in enumerate(chunks):
                            if ci >= 2:
                                te.wait_ge(sVe.h, stt_at[(h, ci - 2)])
                            i = te.matmul(psX[:, (ci % 2) * 512:(ci % 2) * 512 + w],
                                          kT_v[:, h, kb * 128:kb * 128 + 128],
                                          qT_v[:, h, q0:q0 + w], start=True, stop=True)
                            sPe.inc(i)
                            if ci >= 1:
                                kbp, q0p, wp, firstp, stopp = chunks[ci - 1]
                                te.wait_ge(sSc.h, exp_at[(h, ci - 1)])
                                te.matmul(psO[:, q0p:q0p + wp],
                                          Vt_v[:, kbp, h * 128:h * 128 + 128],
                                          PTc[(ci - 1) % 2][:, 0:wp],
                                          start=(kbp == 0), stop=stopp)
                        kbp, q0p, wp, firstp, stopp = chunks[-1]
                        te.wait_ge(sSc.h, exp_at[(h, len(chunks) - 1)])
                        i = te.matmul(psO[:, q0p:q0p + wp],
                                      Vt_v[:, kbp, h * 128:h * 128 + 128],
                                      PTc[(len(chunks) - 1) % 2][:, 0:wp],
                                      start=(kbp == 0), stop=stopp)
                        sPe.inc(i)

                oth_hook_at = []

                @blk.vector
                def _(v):
                    for h in range(NH):
                        chunks = head_chunks(h)
                        for qb in range(16):
                            v.wait_ge(sPe.h, pe_tr_at[(h, qb)])
                            v.drain()
                            i = v.tensor_copy(Bsb[0:1, qb * 128:qb * 128 + 128],
                                              psX[0:1, (qb % 2) * 512:(qb % 2) * 512 + 128])
                            sVe.inc(i)
                            GE.emit(v, 1)
                        v.wait_ge(sPe.h, pe_b_at[h])
                        v.drain()
                        i = v.tensor_copy(Bsb[:, :], psX[:, :])
                        sVe.inc(i)
                        for ci, (kb, q0, w, first, stop) in enumerate(chunks):
                            v.wait_ge(sPe.h, pe_st_at[(h, ci)])
                            i = v.scalar_tensor_tensor(Pin[ci % 2][:, 0:w],
                                                       psX[:, (ci % 2) * 512:(ci % 2) * 512 + w],
                                                       SCALE, Bsb[:, q0:q0 + w],
                                                       ALU.mult, ALU.subtract)
                            if first:
                                v.drain()
                                i = v.tensor_tensor(Pin[ci % 2][:, 0:128], Pin[ci % 2][:, 0:128],
                                                    MnegT[:, :], ALU.add)
                            sVe.inc(i)
                            GE.emit(v, 2)
                        v.wait_ge(sPe.h, pe_pv_at[h])
                        if h >= 2:
                            v.wait_ge(sOd[h % 2].h, odma_at[h - 2])
                        i = v.tensor_copy(oTh[h % 2][:, :], psO[:, :])
                        sVe.inc(i)
                        oth_hook_at.append(GE.done_quarters + 1)
                        GE.emit(v, 4)

                @blk.scalar
                def _(sc):
                    for h in range(NH):
                        chunks = head_chunks(h)
                        for ci, (kb, q0, w, first, stop) in enumerate(chunks):
                            sc.wait_ge(sVe.h, stt_at[(h, ci)])
                            i = sc.activation(PTc[ci % 2][:, 0:w], Pin[ci % 2][:, 0:w], AF.Exp)
                            sSc.inc(i)

                B2q = GE.done_quarters

                def _emit_oth(e, h):
                    e.wait_ge(sVe.h, oevac_at[h])
                    dstp = ag_o_in[0][:, h * S:(h + 1) * S] if h < 6 else \
                        ag_o_in[1][:, (h - 6) * S:(h - 5) * S]
                    e.dma_start(dstp, oTh[h % 2][:, :]).then_inc(sOd[h % 2].h, 16)
                    sOd[h % 2].n += 16

                @blk.sync
                def _(s):
                    if B2q > B1q:
                        hook_q = {}
                        for h in range(NH):
                            q = min(B2q - 1, max(B1q, oth_hook_at[h]))
                            hook_q.setdefault(q, []).append(h)
                        extra = {q: (lambda hs: (lambda e: [_emit_oth(e, h) for h in hs]))(hs)
                                 for q, hs in hook_q.items()}
                        GE.dma_chain(s, B1q, B2q, extra=extra)
                    else:
                        for h in range(NH):
                            _emit_oth(s, h)
                    s.wait_ge(sOd[0].h, sOd[0].n)
                    s.wait_ge(sOd[1].h, sOd[1].n)

                @blk.gpsimd
                def _(g):
                    g.wait_ge(sOd[0].h, 48)
                    g.wait_ge(sOd[1].h, 48)
                    i = g.collective_compute("AllGather", ALU.bypass, RG,
                                             ins=[ag_o_in[0]], outs=[ag_o_out[0]])
                    sCc.inc(i)
                    g.wait_ge(sOd[0].h, 64)
                    g.wait_ge(sOd[1].h, 64)
                    i = g.collective_compute("AllGather", ALU.bypass, RG,
                                             ins=[ag_o_in[1]], outs=[ag_o_out[1]])
                    sCc.inc(i)

        B2q_g = [GE.done_quarters]
        cc_o_at = {hb: sCc.n - 1 + hb for hb in range(2)}
        S_attn.close()

        with contextlib.ExitStack() as PH:
            xres = _e(PH, nc.sbuf_tensor("xresA", [128, DC * TOK], F32))
            Bvec = _e(PH, nc.sbuf_tensor("BvecA", [128, 2048], F32))
            sqbuf = [_e(PH, nc.sbuf_tensor(f"sqbufA{i}", [128, 1024], F32)) for i in range(2)]
            orcv = _e(PH, nc.sbuf_tensor("orcv", [128, DC * TOK], BF16))
            woutS = _e(PH, nc.sbuf_tensor("woutS", [128, DC * D], BF16))
            xres_v = xres[:].rearrange("p (k t) -> p k t", k=DC)
            orcv_v = orcv[:].rearrange("p (k t) -> p k t", k=DC)
            woutS_v = woutS[:].rearrange("p (k j) -> p k j", k=DC)

            with nc.Block() as blk, contextlib.ExitStack() as PS:
                psA = [_e(PS, nc.psum_tensor(f"psC{i}", [128, 1024], F32)) for i in range(2)]

                @blk.gpsimd
                def _(g):
                    g.dma_start(woutS_v, woutT.rearrange("(k p) j -> p k j", p=128)).then_inc(sLg.h, 16)
                sLg.n += 16

                @blk.sync
                def _(s):
                    s.dma_start(xres_v, xsliceT.rearrange("(k p) t -> p k t", p=128)).then_inc(sLd.h, 16)
                    for hb in range(2):
                        nh = 6 if hb == 0 else 2
                        s.wait_ge(sCc.h, cc_o_at[hb])
                        for r in range(NCORES):
                            half = r % 2
                            a = r // 2
                            src = ag_o_out[hb][r, :, :].rearrange("p (h q) -> p h q", h=nh)
                            src = src[:, :, bass.ds(rank * 256, 256)]
                            dst = orcv_v[:, half * 8 + hb * 6:half * 8 + hb * 6 + nh,
                                         a * 256:a * 256 + 256]
                            s.dma_start(dst, src).then_inc(sLdh[hb].h, 16)
                            sLdh[hb].n += 16
                sLd.n += 16
                ld_all = sLd.n
                lg_all = sLg.n
                p0, v0 = sPe.n, sVe.n
                rounds = [[0, 1, 2, 3, 4, 5, 8, 9, 10, 11, 12, 13], [6, 7, 14, 15]]

                @blk.tensor
                def _(te):
                    te.wait_ge(sLg.h, lg_all)
                    for rd in range(2):
                        te.wait_ge(sLdh[rd].h, sLdh[rd].n)
                        for r in range(DC):
                            it = rd * DC + r
                            if it >= 2:
                                te.wait_ge(sVe.h, v0 + it - 1)
                            for k in rounds[rd]:
                                for sc_ in range(2):
                                    i = te.matmul(psA[it % 2][:, sc_ * 512:sc_ * 512 + 512],
                                                  woutS_v[:, k, r * 128:r * 128 + 128],
                                                  orcv_v[:, k, sc_ * 512:sc_ * 512 + 512],
                                                  start=(k == rounds[rd][0]),
                                                  stop=(k == rounds[rd][-1]))
                            sPe.inc(i)

                @blk.vector
                def _(v):
                    v.wait_ge(sLd.h, ld_all)
                    for rd in range(2):
                        for r in range(DC):
                            it = rd * DC + r
                            v.wait_ge(sPe.h, p0 + it + 1)
                            i = v.tensor_tensor(xres_v[:, r, :], xres_v[:, r, :],
                                                psA[it % 2][:, :], ALU.add)
                            sVe.inc(i)
                            GE.emit(v, 6)

                @blk.vector
                def _(v):
                    GE.flush(v)

                B3q = GE.done_quarters

                @blk.scalar
                def _(sc):
                    if B3q > B2q_g[0]:
                        GE.dma_chain(sc, B2q_g[0], B3q)

                @blk.gpsimd
                def _(g):
                    gu0, gu1 = GE.store_totals[N_GU_Q - 1]
                    g.wait_ge(sGs[0].h, gu0)
                    g.wait_ge(sGs[1].h, gu1)
                    i = g.collective_compute("AllGather", ALU.bypass, RG,
                                             ins=[ag_gu_in], outs=[ag_gu_out])
                    sCc.inc(i)

            cc_gu_at = sCc.n

            with nc.Block() as blk, contextlib.ExitStack() as PS:
                emit_ln(nc, blk, PS, sems, xres_v, Bvec, sqbuf)

            with nc.Block() as blk:
                @blk.sync
                def _(s):
                    s.wait_ge(sVe.h, sVe.n)
                    s.dma_start(x_sp[0], xres[:]).then_inc(sSt.h, 16)
                    sSt.n += 16
                    s.wait_ge(sSt.h, sSt.n)

        SG.close()

        if N_SUB >= 2:
            GB = [(g0, min(g0 + 4, IC)) for g0 in range(0, IC, 4)]
            with contextlib.ExitStack() as SB:
                h_sb = _e(SB, nc.sbuf_tensor("h_sb", [128, IC * TOK], BF16))
                h_v = h_sb[:].rearrange("p (i t) -> p i t", i=IC)

                with contextlib.ExitStack() as PH:
                    x1b = _e(PH, nc.sbuf_tensor("x1b", [128, DC * TOK], BF16))
                    wg = [_e(PH, nc.sbuf_tensor(f"wgs{i}", [128, 16 * 512], BF16)) for i in range(2)]
                    wu = [_e(PH, nc.sbuf_tensor(f"wus{i}", [128, 16 * 512], BF16)) for i in range(2)]
                    x1b_v = x1b[:].rearrange("p (k t) -> p k t", k=DC)
                    wg_v = [w[:].rearrange("p (k c) -> p k c", k=16) for w in wg]
                    wu_v = [w[:].rearrange("p (k c) -> p k c", k=16) for w in wu]
                    with nc.Block() as blk, contextlib.ExitStack() as PS:
                        gps = [_e(PS, nc.psum_tensor(f"gps{i}", [128, 1024], F32)) for i in range(2)]
                        ups = [_e(PS, nc.psum_tensor(f"ups{i}", [128, 1024], F32)) for i in range(2)]

                        @blk.gpsimd
                        def _(g):
                            g.dma_start(x1b_v, x_sp[0].rearrange("p (k t) -> p k t", k=DC)).then_inc(sLg.h, 16)
                            dn0, dn1 = GE.store_totals[-1]
                            g.wait_ge(sGs[0].h, dn0)
                            g.wait_ge(sGs[1].h, dn1)
                            i = g.collective_compute("AllGather", ALU.bypass, RG,
                                                     ins=[ag_dn_in], outs=[ag_dn_out])
                            sCc.inc(i)
                        sLg.n += 16
                        cc_dn_at = sCc.n
                        lg_x1 = sLg.n
                        p0, v0, c0 = sPe.n, sVe.n, sSc.n
                        pe_at = {}
                        psim = p0
                        for gi, (o0, o1) in enumerate(GB):
                            for oc in range(o0, o1):
                                psim += 2
                            pe_at[gi] = psim
                        pe_oc = {}
                        psim = p0
                        for gi, (o0, o1) in enumerate(GB):
                            for oc in range(o0, o1):
                                pe_oc[(oc, "g")] = psim + 1
                                pe_oc[(oc, "u")] = psim + 2
                                psim += 2
                        sc_oc = {}
                        csim = c0
                        for gi, (o0, o1) in enumerate(GB):
                            for oc in range(o0, o1):
                                csim += 1
                                sc_oc[oc] = csim
                        ve_oc = {}
                        vsim = v0
                        for gi, (o0, o1) in enumerate(GB):
                            for oc in range(o0, o1):
                                vsim += 1
                                ve_oc[oc] = vsim

                        @blk.sync
                        def _(s):
                            s.wait_ge(sCc.h, cc_gu_at)
                            for gi, (o0, o1) in enumerate(GB):
                                gsz = o1 - o0
                                if gi >= 2:
                                    s.wait_ge(sPe.h, pe_at[gi - 2])
                                for buf_v, base in ((wg_v[gi % 2], GOFF), (wu_v[gi % 2], UOFF)):
                                    for sub in range(2):
                                        gv = ag_gu_out[:, base + sub * 128 * INTER:
                                                       base + (sub + 1) * 128 * INTER]
                                        gv = gv.rearrange("r (p c) -> r p c", p=128)
                                        gv = gv[:, :, o0 * 128:o0 * 128 + gsz * 128]
                                        s.dma_start(buf_v[:, sub::2, 0:gsz * 128],
                                                    gv.transpose([1, 0, 2])).then_inc(
                                                        sGU[2 * (gi % 2) + sub].h, 16)
                                        sGU[2 * (gi % 2) + sub].n += 16

                        @blk.tensor
                        def _(te):
                            te.wait_ge(sLg.h, lg_x1)
                            te.wait_ge(sGs[0].h, GE.store_totals[-1][0])
                            te.wait_ge(sGs[1].h, GE.store_totals[-1][1])
                            for gi, (o0, o1) in enumerate(GB):
                                gsz = o1 - o0
                                te.wait_ge(sGU[2 * (gi % 2)].h, 32 * (gi // 2 + 1))
                                te.wait_ge(sGU[2 * (gi % 2) + 1].h, 32 * (gi // 2 + 1))
                                for oc in range(o0, o1):
                                    lc = oc - o0
                                    if oc >= 2:
                                        te.wait_ge(sSc.h, sc_oc[oc - 2])
                                    for k in range(DC):
                                        for sc_ in range(2):
                                            i = te.matmul(gps[oc % 2][:, sc_ * 512:sc_ * 512 + 512],
                                                          wg_v[gi % 2][:, k, lc * 128:lc * 128 + 128],
                                                          x1b_v[:, k, sc_ * 512:sc_ * 512 + 512],
                                                          start=(k == 0), stop=(k == DC - 1))
                                    sPe.inc(i)
                                    if oc >= 2:
                                        te.wait_ge(sVe.h, ve_oc[oc - 2])
                                    for k in range(DC):
                                        for sc_ in range(2):
                                            i = te.matmul(ups[oc % 2][:, sc_ * 512:sc_ * 512 + 512],
                                                          wu_v[gi % 2][:, k, lc * 128:lc * 128 + 128],
                                                          x1b_v[:, k, sc_ * 512:sc_ * 512 + 512],
                                                          start=(k == 0), stop=(k == DC - 1))
                                    sPe.inc(i)

                        @blk.scalar
                        def _(sc):
                            for oc in range(IC):
                                sc.wait_ge(sPe.h, pe_oc[(oc, "g")])
                                i = sc.activation(h_v[:, oc, :], gps[oc % 2][:, :], AF.Silu)
                                sSc.inc(i)

                        @blk.vector
                        def _(v):
                            for oc in range(IC):
                                v.wait_ge(sSc.h, sc_oc[oc])
                                v.wait_ge(sPe.h, pe_oc[(oc, "u")])
                                i = v.tensor_tensor(h_v[:, oc, :], h_v[:, oc, :],
                                                    ups[oc % 2][:, :], ALU.mult)
                                sVe.inc(i)

                with contextlib.ExitStack() as PH:
                    xres = _e(PH, nc.sbuf_tensor("xresB", [128, DC * TOK], F32))
                    Bvec = _e(PH, nc.sbuf_tensor("BvecB", [128, 2048], F32))
                    sqbuf = [_e(PH, nc.sbuf_tensor(f"sqbufB{i}", [128, 1024], F32)) for i in range(2)]
                    dsl = [_e(PH, nc.sbuf_tensor(f"dsl{i}", [128, 48 * 128], BF16)) for i in range(2)]
                    xres_v = xres[:].rearrange("p (k t) -> p k t", k=DC)
                    dsl_v = [w[:].rearrange("p (i c) -> p i c", i=48) for w in dsl]
                    with nc.Block() as blk, contextlib.ExitStack() as PS:
                        dps = [_e(PS, nc.psum_tensor(f"dps{i}", [128, 1024], F32)) for i in range(2)]
                        p0, v0 = sPe.n, sVe.n
                        ld0 = sLd.n

                        @blk.sync
                        def _(s):
                            s.dma_start(xres_v, x_sp[0].rearrange("p (k t) -> p k t", k=DC)).then_inc(sLd.h, 16)
                            s.wait_ge(sCc.h, cc_dn_at)
                            for r in range(DC):
                                if r >= 2:
                                    s.wait_ge(sPe.h, p0 + (r - 1))
                                for sub in range(6):
                                    gv = ag_dn_out[:, sub * 128 * D:(sub + 1) * 128 * D]
                                    gv = gv.rearrange("r (p c) -> r p c", p=128)
                                    gv = gv[:, :, r * 128:(r + 1) * 128]
                                    s.dma_start(dsl_v[r % 2][:, sub::6, :],
                                                gv.transpose([1, 0, 2])).then_inc(sWd[r % 2].h, 16)
                                    sWd[r % 2].n += 16
                        sLd.n = ld0 + 16

                        @blk.tensor
                        def _(te):
                            for r in range(DC):
                                te.wait_ge(sWd[r % 2].h, 96 * (r // 2 + 1))
                                if r >= 2:
                                    te.wait_ge(sVe.h, v0 + (r - 1))
                                for i_ in range(IC):
                                    for sc_ in range(2):
                                        i = te.matmul(dps[r % 2][:, sc_ * 512:sc_ * 512 + 512],
                                                      dsl_v[r % 2][:, i_, :],
                                                      h_v[:, i_, sc_ * 512:sc_ * 512 + 512],
                                                      start=(i_ == 0), stop=(i_ == IC - 1))
                                sPe.inc(i)

                        @blk.vector
                        def _(v):
                            for r in range(DC):
                                v.wait_ge(sPe.h, p0 + r + 1)
                                v.wait_ge(sLd.h, ld0 + 16)
                                i = v.tensor_tensor(xres_v[:, r, :], xres_v[:, r, :],
                                                    dps[r % 2][:, :], ALU.add)
                                sVe.inc(i)

                    with nc.Block() as blk, contextlib.ExitStack() as PS:
                        emit_ln(nc, blk, PS, sems, xres_v, Bvec, sqbuf)

                    with nc.Block() as blk:
                        @blk.sync
                        def _(s):
                            s.wait_ge(sVe.h, sVe.n)
                            s.dma_start(x_sp[1], xres[:]).then_inc(sSt.h, 16)
                            sSt.n += 16
                            s.wait_ge(sSt.h, sSt.n)

        if N_SUB >= 3:
            with contextlib.ExitStack() as PH:
                o_ln = _e(PH, nc.sbuf_tensor("o_ln", [128, DC * TOK], F32))
                gmS = _e(PH, nc.sbuf_tensor("gmS", [128, 16], F32))
                btS = _e(PH, nc.sbuf_tensor("btS", [128, 16], F32))
                binT_sb = _e(PH, nc.sbuf_tensor("binT_sb", [128, 48], F32))
                boutT_sb = _e(PH, nc.sbuf_tensor("boutT_sb", [128, 16], F32))
                o_v = o_ln[:].rearrange("p (k t) -> p k t", k=DC)
                PHJ = _e(PH, contextlib.ExitStack())
                x2b = _e(PHJ, nc.sbuf_tensor("x2b", [128, DC * TOK], BF16))
                wfC = [_e(PHJ, nc.sbuf_tensor(f"wfC{i}", [128, D], F32)) for i in range(2)]
                wsl = [_e(PHJ, nc.sbuf_tensor(f"wsl{i}", [128, 16 * 384], BF16)) for i in range(2)]
                ftile = [[_e(PHJ, nc.sbuf_tensor(f"ft{s}_{i}", [128, 1024], F32)) for i in range(2)]
                         for s in range(3)]
                u2 = _e(PHJ, nc.sbuf_tensor("u2", [128, 2048], F32))
                lamf2 = _e(PHJ, nc.sbuf_tensor("lamf2", [128, 2048], F32))
                og2 = _e(PHJ, nc.sbuf_tensor("og2", [128, 2048], F32))
                ssum = _e(PHJ, nc.sbuf_tensor("ssum", [128, 2048], F32))
                rtmp = _e(PHJ, nc.sbuf_tensor("rtmp", [128, 512], F32))
                rst = _e(PHJ, nc.sbuf_tensor("rst", [128, 512], F32))
                x2b_v = x2b[:].rearrange("p (k t) -> p k t", k=DC)
                wsl_v = [w[:].rearrange("p (k c) -> p k c", k=16) for w in wsl]
                u2_v = u2[:].rearrange("p (m t) -> p m t", m=2)
                lam_v = lamf2[:].rearrange("p (m t) -> p m t", m=2)
                og_v = og2[:].rearrange("p (m t) -> p m t", m=2)
                ss_v = ssum[:].rearrange("p (m t) -> p m t", m=2)
                rt_v = rtmp[:].rearrange("p (m t) -> p m t", m=2)
                rs_v = rst[:].rearrange("p (m t) -> p m t", m=2)

                with nc.Block() as blk, contextlib.ExitStack() as PS:
                    fps = _e(PS, nc.psum_tensor("fps", [128, 1024], F32))
                    rps = [_e(PS, nc.psum_tensor(f"rps{i}", [128, 1024], F32)) for i in range(2)]
                    eps = _e(PS, nc.psum_tensor("eps", [128, 1024], F32))
                    lg0 = sLg.n
                    ld0 = sLd.n

                    @blk.gpsimd
                    def _(g):
                        g.dma_start(x2b_v, x_sp[1].rearrange("p (k t) -> p k t", k=DC)).then_inc(sLg.h, 16)
                    sLg.n += 16

                    c0abs = sSc.n

                    @blk.sync
                    def _(s):
                        s.dma_start(gmS[:], gammaT).then_inc(sLd.h, 16)
                        s.dma_start(btS[:], betaT).then_inc(sLd.h, 16)
                        s.dma_start(binT_sb[:], binT).then_inc(sLd.h, 16)
                        s.dma_start(boutT_sb[:], boutT).then_inc(sLd.h, 16)
                        for k in range(DC):
                            if k >= 2:
                                s.wait_ge(sSc.h, c0abs + 4 * (k - 1))
                            gv = wfT[:, :].rearrange("(k p) c -> p k c", p=128)[:, k, :]
                            s.dma_start(wfC[k % 2][:, :], gv).then_inc(sWf[k % 2].h, 16)
                            sWf[k % 2].n += 16
                    sLd.n += 64
                    ldc_all = sLd.n

                    p0, v0, c0 = sPe.n, sVe.n, sSc.n
                    pe_feat, pe_rem, pe_ein = {}, {}, {}
                    psim = p0
                    for j in range(DC):
                        for seg in range(3):
                            psim += 1
                            pe_feat[(j, seg)] = psim
                        if j >= 1:
                            psim += 1
                            pe_ein[j - 1] = psim
                        for r in range(6):
                            psim += 1
                            pe_rem[(j, r)] = psim
                    psim += 1
                    pe_ein[DC - 1] = psim
                    sc_feat = {}
                    csim = c0
                    for j in range(DC):
                        for seg in range(3):
                            csim += 1
                            sc_feat[(j, seg)] = csim
                        csim += 1
                    ve_rem, ve_t, ve_oev = {}, {}, {}
                    vsim = v0
                    for j in range(DC):
                        for r in range(6):
                            vsim += 1
                            ve_rem[(j, r)] = vsim
                        vsim += 1
                        ve_t[j] = vsim
                        vsim += 1
                        ve_oev[j] = vsim

                    @blk.gpsimd
                    def _(g):
                        for j in range(DC):
                            if j >= 2:
                                g.wait_ge(sPe.h, pe_feat[(j - 2, 2)])
                            gv = winTre[:, j * 384:(j + 1) * 384]
                            g.dma_start(wsl_v[j % 2],
                                        gv.rearrange("(k p) c -> p k c", p=128)).then_inc(sWc[j % 2].h, 16)
                            sWc[j % 2].n += 16

                    @blk.tensor
                    def _(te):
                        te.wait_ge(sLg.h, lg0 + 16)
                        for j in range(DC):
                            te.wait_ge(sWc[j % 2].h, 16 * (j // 2 + 1))
                            for seg in range(3):
                                fcnt = 3 * j + seg
                                if fcnt >= 1:
                                    te.wait_ge(sSc.h, sc_feat[((fcnt - 1) // 3, (fcnt - 1) % 3)])
                                for k in range(DC):
                                    for sc_ in range(2):
                                        i = te.matmul(fps[:, sc_ * 512:sc_ * 512 + 512],
                                                      wsl_v[j % 2][:, k, seg * 128:seg * 128 + 128],
                                                      x2b_v[:, k, sc_ * 512:sc_ * 512 + 512],
                                                      start=(k == 0), stop=(k == DC - 1))
                                sPe.inc(i)
                            if j >= 1:
                                te.wait_ge(sVe.h, ve_t[j - 1])
                                if j >= 2:
                                    te.wait_ge(sVe.h, ve_oev[j - 2])
                                for m in range(2):
                                    for sc_ in range(2):
                                        i = te.matmul(eps[:, sc_ * 512:sc_ * 512 + 512],
                                                      DTC[m][:, :],
                                                      u2_v[:, m, sc_ * 512:sc_ * 512 + 512],
                                                      start=(m == 0), stop=(m == 1))
                                sPe.inc(i)
                            rsrc = [(DTA, 2), (DTB, 0), (DTA, 1)]
                            rcnt = 6 * j
                            for which, (Dm, ft) in enumerate(rsrc):
                                te.wait_ge(sSc.h, sc_feat[(j, ft)])
                                for m in range(2):
                                    r = which * 2 + m
                                    if rcnt + r >= 2:
                                        pj, pr = divmod(rcnt + r - 2, 6)
                                        te.wait_ge(sVe.h, ve_rem[(pj, pr)])
                                    for sc_ in range(2):
                                        i = te.matmul(rps[(rcnt + r) % 2][:, sc_ * 512:sc_ * 512 + 512],
                                                      Dm[m][:, :],
                                                      ftile[ft][j % 2][:, sc_ * 512:sc_ * 512 + 512],
                                                      start=True, stop=True)
                                    sPe.inc(i)
                        te.wait_ge(sVe.h, ve_t[DC - 1])
                        te.wait_ge(sVe.h, ve_oev[DC - 2])
                        for m in range(2):
                            for sc_ in range(2):
                                i = te.matmul(eps[:, sc_ * 512:sc_ * 512 + 512],
                                              DTC[m][:, :],
                                              u2_v[:, m, sc_ * 512:sc_ * 512 + 512],
                                              start=(m == 0), stop=(m == 1))
                        sPe.inc(i)

                    @blk.scalar
                    def _(sc):
                        funcs = [AF.Silu, AF.Sigmoid, AF.Sigmoid]
                        sc.wait_ge(sLd.h, ldc_all)
                        for j in range(DC):
                            for seg in range(3):
                                sc.wait_ge(sPe.h, pe_feat[(j, seg)])
                                if j >= 2:
                                    sc.wait_ge(sPe.h, pe_rem[(j - 2, 5)])
                                i = sc.activation(ftile[seg][j % 2][:, :], fps[:, :],
                                                  funcs[seg], bias=binT_sb[:, 3 * j + seg:3 * j + seg + 1])
                                sSc.inc(i)
                            sc.wait_ge(sWf[j % 2].h, 16 * (j // 2 + 1))
                            sc.drain()
                            i = sc.activation(wfC[j % 2][:, :], wfC[j % 2][:, :], AF.Abs,
                                              accum_out=acols[:, j:j + 1])
                            sSc.inc(i)

                    @blk.vector
                    def _(v):
                        for j in range(DC):
                            for r in range(6):
                                which, m = divmod(r, 2)
                                v.wait_ge(sPe.h, pe_rem[(j, r)])
                                v.drain()
                                if which == 0:
                                    i = v.tensor_copy(lam_v[:, m, :], rps[(6 * j + r) % 2][:, :])
                                elif which == 1:
                                    i = v.scalar_tensor_tensor(u2_v[:, m, :], lam_v[:, m, :], 1.0,
                                                               rps[(6 * j + r) % 2][:, :],
                                                               ALU.subtract, ALU.mult)
                                else:
                                    i = v.tensor_copy(og_v[:, m, :], rps[(6 * j + r) % 2][:, :])
                                sVe.inc(i)
                            v.drain()
                            v.tensor_copy(ss_v[:, :, 0:256], u2_v[:, :, 0:256])
                            for a in range(1, 4):
                                v.drain()
                                v.scalar_tensor_tensor(rt_v[:, :, 0:256], lam_v[:, :, a * 256:a * 256 + 256],
                                                       0.0, ss_v[:, :, (a - 1) * 256:a * 256],
                                                       ALU.add, ALU.mult)
                                v.drain()
                                v.tensor_tensor(ss_v[:, :, a * 256:a * 256 + 256], rt_v[:, :, 0:256],
                                                u2_v[:, :, a * 256:a * 256 + 256], ALU.add)
                            v.drain()
                            v.tensor_tensor(ss_v[:, :, 768:1024], ss_v[:, :, 768:1024],
                                            u2_v[:, :, 768:1024], ALU.add)
                            v.drain()
                            v.tensor_copy(rs_v[:, :, 0:256], u2_v[:, :, 768:1024])
                            for a in (2, 1, 0):
                                v.drain()
                                v.scalar_tensor_tensor(rt_v[:, :, 0:256], lam_v[:, :, a * 256:a * 256 + 256],
                                                       0.0, rs_v[:, :, 0:256], ALU.add, ALU.mult)
                                v.drain()
                                v.tensor_tensor(rs_v[:, :, 0:256], rt_v[:, :, 0:256],
                                                u2_v[:, :, a * 256:a * 256 + 256], ALU.add)
                                v.drain()
                                v.tensor_tensor(ss_v[:, :, a * 256:a * 256 + 256],
                                                ss_v[:, :, a * 256:a * 256 + 256],
                                                rs_v[:, :, 0:256], ALU.add)
                            for m in range(2):
                                v.drain()
                                i = v.tensor_tensor(u2_v[:, m, :], ss_v[:, m, :], og_v[:, m, :], ALU.mult)
                            sVe.inc(i)
                            v.wait_ge(sPe.h, pe_ein[j])
                            v.drain()
                            i = v.tensor_copy(o_v[:, j, :], eps[:, :])
                            sVe.inc(i)

                PHJ.close()

                with contextlib.ExitStack() as PH2:
                    Bvec = _e(PH2, nc.sbuf_tensor("BvecC", [128, 2048], F32))
                    sqbuf = [_e(PH2, nc.sbuf_tensor(f"sqbufC{i}", [128, 1024], F32)) for i in range(2)]
                    o_b = _e(PH2, nc.sbuf_tensor("o_b", [128, DC * TOK], BF16))
                    o_b_v = o_b[:].rearrange("p (k t) -> p k t", k=DC)
                    with nc.Block() as blk, contextlib.ExitStack() as PS:
                        emit_ln(nc, blk, PS, sems, o_v, Bvec, sqbuf, gamma=gmS, beta=btS, out_bf=o_b_v)

                    with contextlib.ExitStack() as PH3:
                        xres = _e(PH3, nc.sbuf_tensor("xresC", [128, DC * TOK], F32))
                        wo = [_e(PH3, nc.sbuf_tensor("wo0", [128, 16 * 512], BF16))]
                        xres_v = xres[:].rearrange("p (k t) -> p k t", k=DC)
                        wo_v = [w[:].rearrange("p (k c) -> p k c", k=16) for w in wo]
                        RGC = [(r0, min(r0 + 4, DC)) for r0 in range(0, DC, 4)]
                        with nc.Block() as blk, contextlib.ExitStack() as PS:
                            psA = [_e(PS, nc.psum_tensor(f"psW{i}", [128, 1024], F32)) for i in range(2)]
                            ld0 = sLd.n
                            lg0 = sLg.n
                            p0, v0 = sPe.n, sVe.n
                            pe_r = {}
                            psim = p0
                            for gi, (r0, r1) in enumerate(RGC):
                                for r in range(r0, r1):
                                    psim += 1
                                    pe_r[r] = psim
                            pe_g = {gi: pe_r[r1 - 1] for gi, (r0, r1) in enumerate(RGC)}

                            @blk.sync
                            def _(s):
                                s.dma_start(xres_v, x_sp[1].rearrange("p (k t) -> p k t", k=DC)).then_inc(sLd.h, 16)
                            sLd.n += 16

                            @blk.gpsimd
                            def _(g):
                                for gi, (r0, r1) in enumerate(RGC):
                                    if gi >= 1:
                                        g.wait_ge(sPe.h, pe_g[gi - 1])
                                    gv = wout1T[:, r0 * 128:r1 * 128]
                                    g.dma_start(wo_v[0][:, :, 0:(r1 - r0) * 128],
                                                gv.rearrange("(k p) c -> p k c", p=128)).then_inc(sWo.h, 16)
                                    sWo.n += 16

                            @blk.tensor
                            def _(te):
                                for gi, (r0, r1) in enumerate(RGC):
                                    te.wait_ge(sWo.h, 16 * (gi + 1))
                                    for r in range(r0, r1):
                                        lc = r - r0
                                        if r >= 2:
                                            te.wait_ge(sVe.h, v0 + r - 1)
                                        for k in range(DC):
                                            for sc_ in range(2):
                                                i = te.matmul(psA[r % 2][:, sc_ * 512:sc_ * 512 + 512],
                                                              wo_v[0][:, k, lc * 128:lc * 128 + 128],
                                                              o_b_v[:, k, sc_ * 512:sc_ * 512 + 512],
                                                              start=(k == 0), stop=(k == DC - 1))
                                        sPe.inc(i)

                            @blk.vector
                            def _(v):
                                v.wait_ge(sLd.h, ld0 + 16)
                                for r in range(DC):
                                    v.wait_ge(sPe.h, pe_r[r])
                                    i = v.scalar_tensor_tensor(xres_v[:, r, :], psA[r % 2][:, :],
                                                               boutT_sb[:, r:r + 1], xres_v[:, r, :],
                                                               ALU.add, ALU.add)
                                    sVe.inc(i)

                        with nc.Block() as blk, contextlib.ExitStack() as PS:
                            emit_ln(nc, blk, PS, sems, xres_v, Bvec, sqbuf)

                        with nc.Block() as blk:
                            @blk.sync
                            def _(s):
                                s.wait_ge(sVe.h, sVe.n)
                                s.dma_start(x_sp[0], xres[:]).then_inc(sSt.h, 16)
                                sSt.n += 16
                                s.wait_ge(sSt.h, sSt.n)

        DTS.close()

        if N_SUB >= 4:
            with contextlib.ExitStack() as PH:
                xres = _e(PH, nc.sbuf_tensor("xresD", [128, DC * TOK], F32))
                cT = _e(PH, nc.sbuf_tensor("cT", [128, DC * D], BF16))
                Kq = _e(PH, nc.sbuf_tensor("Kq", [128, DC * TOK], BF16))
                Bvec = _e(PH, nc.sbuf_tensor("BvecD", [128, 2048], F32))
                sqbuf = [_e(PH, nc.sbuf_tensor(f"sqbufD{i}", [128, 1024], F32)) for i in range(2)]
                biasT_sb = _e(PH, nc.sbuf_tensor("biasT_sb", [128, 16], F32))
                ri32 = _e(PH, nc.sbuf_tensor("ri32", [128, 1024], I32))
                rf32 = _e(PH, nc.sbuf_tensor("rf32", [128, 1024], F32))
                qf32 = _e(PH, nc.sbuf_tensor("qf32", [128, 1024], F32))
                wf = [_e(PH, nc.sbuf_tensor(f"wf{i}", [128, D], F32)) for i in range(2)]
                xres_v = xres[:].rearrange("p (k t) -> p k t", k=DC)
                cT_v = cT[:].rearrange("p (k c) -> p k c", k=DC)
                Kq_v = Kq[:].rearrange("p (k t) -> p k t", k=DC)
                amaxrow = Bvec[0:1, 0:1024]
                a_col = acols[:, 16:17]
                alpha_col = acols[:, 17:18]
                half_col = acols[:, 18:19]
                nhalf_col = acols[:, 19:20]

                with nc.Block() as blk, contextlib.ExitStack() as PS:
                    ps_t = _e(PS, nc.psum_tensor("ps_t", [128, 1024], F32))
                    ps_b = _e(PS, nc.psum_tensor("ps_b", [128, 1024], F32))
                    psA = [_e(PS, nc.psum_tensor(f"psD{i}", [128, 1024], F32)) for i in range(2)]
                    p0, v0, c0 = sPe.n, sVe.n, sSc.n
                    ld0 = sLd.n
                    wf0 = (sWf[0].n, sWf[1].n)
                    V_alpha = v0 + 3
                    V_quant = {k: v0 + 3 + 2 * k + 1 for k in range(DC)}
                    V_fold = {k: v0 + 3 + 2 * k + 2 for k in range(DC)}
                    V_red = {b: v0 + 35 + b + 1 for b in range(8)}
                    V_cp = {b: v0 + 43 + b + 1 for b in range(8)}
                    V_scale = v0 + 52
                    V_K = {k: v0 + 52 + k + 1 for k in range(DC)}
                    V_evac = {r: v0 + 68 + r + 1 for r in range(DC)}
                    P_a1, P_a2 = p0 + 1, p0 + 2
                    P_tr = {b: p0 + 2 + b + 1 for b in range(16)}
                    P_bc = p0 + 19
                    P_r = {r: p0 + 19 + r + 1 for r in range(DC)}

                    @blk.sync
                    def _(s):
                        s.dma_start(biasT_sb[:], biasT).then_inc(sLd.h, 16)
                        s.dma_start(xres_v, x_sp[0].rearrange("p (k t) -> p k t", k=DC)).then_inc(sLd.h, 16)
                        for k in range(DC):
                            if k >= 2:
                                s.wait_ge(sVe.h, V_quant[k - 2])
                            gv = wfT[:, :].rearrange("(k p) c -> p k c", p=128)[:, k, :]
                            s.dma_start(wf[k % 2][:, :], gv).then_inc(sWf[k % 2].h, 16)
                            sWf[k % 2].n += 16
                    sLd.n += 32

                    @blk.scalar
                    def _(sc):
                        sc.wait_ge(sLd.h, ld0 + 32)
                        for k in range(DC):
                            if k >= 2:
                                sc.wait_ge(sVe.h, V_fold[k - 2])
                            sc.drain()
                            i = sc.activation(qf32[:, :] if k % 2 == 0 else rf32[:, :],
                                              xres_v[:, k, :], AF.Abs)
                            sSc.inc(i)

                    @blk.tensor
                    def _(te):
                        te.wait_ge(sVe.h, v0 + 1)
                        i = te.matmul(ps_b[0:1, 0:1], onesC[:, :], a_col, start=True, stop=True)
                        sPe.inc(i)
                        te.wait_ge(sVe.h, v0 + 2)
                        i = te.matmul(ps_b[:, 32:33], onesR[:, :], acols[0:1, 20:21],
                                      start=True, stop=True)
                        sPe.inc(i)
                        te.wait_ge(sVe.h, V_fold[DC - 1])
                        for b in range(8):
                            if b >= 2:
                                te.wait_ge(sVe.h, V_red[b - 2])
                            i = te.transpose(ps_t[:, (b % 2) * 512:(b % 2) * 512 + 128],
                                             sqbuf[0][:, b * 128:b * 128 + 128], identF[:, :])
                            sPe.inc(i)
                        for b in range(8):
                            te.wait_ge(sVe.h, V_red[7])
                            if b >= 2:
                                te.wait_ge(sVe.h, V_cp[b - 2])
                            i = te.transpose(ps_t[0:1, (b % 2) * 512 + 256:(b % 2) * 512 + 384],
                                             acols[:, 24 + b:25 + b], identF[:, :])
                            sPe.inc(i)
                        te.wait_ge(sVe.h, V_cp[7])
                        te.matmul(ps_b[:, 0:512], onesR[:, :], amaxrow[0:1, 0:512],
                                  start=True, stop=True)
                        i = te.matmul(ps_b[:, 512:1024], onesR[:, :], amaxrow[0:1, 512:1024],
                                      start=True, stop=True)
                        sPe.inc(i)
                        for r in range(DC):
                            if r >= 2:
                                te.wait_ge(sVe.h, V_evac[r - 2])
                            for k in range(DC):
                                if r == 0:
                                    te.wait_ge(sVe.h, V_K[k])
                                for sc_ in range(2):
                                    i = te.matmul(psA[r % 2][:, sc_ * 512:sc_ * 512 + 512],
                                                  cT_v[:, k, r * 128:r * 128 + 128],
                                                  Kq_v[:, k, sc_ * 512:sc_ * 512 + 512],
                                                  start=(k == 0), stop=(k == DC - 1))
                            sPe.inc(i)

                    @blk.vector
                    def _(v):
                        i = v.tensor_reduce(a_col, acols[:, 0:16], AX.X, ALU.add)
                        sVe.inc(i)
                        v.wait_ge(sPe.h, P_a1)
                        v.drain()
                        i = v.tensor_copy(acols[0:1, 20:21], ps_b[0:1, 0:1])
                        sVe.inc(i)
                        v.wait_ge(sPe.h, P_a2)
                        v.drain()
                        v.tensor_copy(alpha_col, ps_b[:, 32:33])
                        v.drain()
                        v.tensor_scalar(alpha_col, alpha_col, 1.0 / float(D * D), EPS,
                                        ALU.mult, ALU.max)
                        v.drain()
                        v.tensor_scalar(half_col, alpha_col, 0.5, None, ALU.mult)
                        i = v.tensor_scalar(nhalf_col, alpha_col, -0.5, None, ALU.mult)
                        sVe.inc(i)
                        for k in range(DC):
                            v.wait_ge(sWf[k % 2].h, wf0[k % 2] + 16 * (k // 2 + 1))
                            v.drain()
                            v.tensor_scalar(sqbuf[1][:, :], wf[k % 2][:, 0:1024],
                                            nhalf_col, None, ALU.is_le)
                            v.drain()
                            v.scalar_tensor_tensor(cT_v[:, k, 0:1024], wf[k % 2][:, 0:1024],
                                                   half_col, sqbuf[1][:, :], ALU.is_ge, ALU.subtract)
                            v.drain()
                            v.tensor_scalar(sqbuf[1][:, :], wf[k % 2][:, 1024:2048],
                                            nhalf_col, None, ALU.is_le)
                            v.drain()
                            i = v.scalar_tensor_tensor(cT_v[:, k, 1024:2048], wf[k % 2][:, 1024:2048],
                                                       half_col, sqbuf[1][:, :], ALU.is_ge, ALU.subtract)
                            sVe.inc(i)
                            v.wait_ge(sSc.h, c0 + k + 1)
                            v.drain()
                            if k == 0:
                                i = v.tensor_copy(sqbuf[0][:, :], qf32[:, :])
                            else:
                                i = v.tensor_tensor(sqbuf[0][:, :], sqbuf[0][:, :],
                                                    qf32[:, :] if k % 2 == 0 else rf32[:, :], ALU.max)
                            sVe.inc(i)
                        for b in range(8):
                            v.wait_ge(sPe.h, P_tr[b])
                            v.drain()
                            i = v.tensor_reduce(acols[:, 24 + b:25 + b],
                                                ps_t[:, (b % 2) * 512:(b % 2) * 512 + 128],
                                                AX.X, ALU.max)
                            sVe.inc(i)
                        for b in range(8):
                            v.wait_ge(sPe.h, P_tr[8 + b])
                            v.drain()
                            i = v.tensor_copy(amaxrow[0:1, b * 128:b * 128 + 128],
                                              ps_t[0:1, (b % 2) * 512 + 256:(b % 2) * 512 + 384])
                            sVe.inc(i)
                        v.wait_ge(sPe.h, P_bc)
                        v.drain()
                        v.tensor_copy(sqbuf[1][:, :], ps_b[:, 0:1024])
                        v.drain()
                        v.tensor_scalar(Bvec[:, 1024:2048], sqbuf[1][:, :], EPS, 1.0 / 127.0,
                                        ALU.max, ALU.mult)
                        v.drain()
                        v.reciprocal(Bvec[:, 0:1024], Bvec[:, 1024:2048])
                        v.drain()
                        i = v.tensor_scalar(Bvec[:, 1024:2048], Bvec[:, 1024:2048], alpha_col,
                                            None, ALU.mult)
                        sVe.inc(i)
                        for k in range(DC):
                            v.drain()
                            v.tensor_tensor(qf32[:, :], xres_v[:, k, :], Bvec[:, 0:1024], ALU.mult)
                            v.drain()
                            v.tensor_copy(ri32[:, :], qf32[:, :])
                            v.drain()
                            v.tensor_copy(rf32[:, :], ri32[:, :])
                            v.drain()
                            v.tensor_tensor(qf32[:, :], qf32[:, :], rf32[:, :], ALU.subtract)
                            v.drain()
                            v.tensor_scalar(sqbuf[0][:, :], qf32[:, :], 0.5, None, ALU.is_ge)
                            v.drain()
                            v.scalar_tensor_tensor(sqbuf[1][:, :], qf32[:, :], -0.5, sqbuf[0][:, :],
                                                   ALU.is_le, ALU.subtract)
                            v.drain()
                            i = v.tensor_tensor(Kq_v[:, k, :], rf32[:, :], sqbuf[1][:, :], ALU.subtract)
                            sVe.inc(i)
                        for r in range(DC):
                            v.wait_ge(sPe.h, P_r[r])
                            v.drain()
                            v.tensor_tensor(qf32[:, :], psA[r % 2][:, :], Bvec[:, 1024:2048], ALU.mult)
                            v.drain()
                            i = v.scalar_tensor_tensor(xres_v[:, r, :], qf32[:, :],
                                                       biasT_sb[:, r:r + 1], xres_v[:, r, :],
                                                       ALU.add, ALU.add)
                            sVe.inc(i)

                with nc.Block() as blk, contextlib.ExitStack() as PS:
                    emit_ln(nc, blk, PS, sems, xres_v, Bvec, sqbuf)

                with nc.Block() as blk:
                    @blk.sync
                    def _(s):
                        s.wait_ge(sVe.h, sVe.n)
                        s.dma_start(x_sp[1], xres[:]).then_inc(sSt.h, 16)
                        sSt.n += 16
                        s.wait_ge(sSt.h, sSt.n)

        last = x_sp[0] if N_SUB == 1 else (x_sp[1] if N_SUB == 2 else x_sp[(N_SUB - 1) % 2])
        with nc.Block() as blk:
            @blk.sync
            def _(s):
                s.dma_start(out.transpose([1, 0, 2]),
                            last.rearrange("p (k t) -> p k t", k=DC)).then_inc(sSt.h, 16)
                sSt.n += 16
                s.wait_ge(sSt.h, sSt.n)

    return nc


_NC_CACHE = None


def kernel(**inputs):
    global _NC_CACHE
    maps = host_prep(inputs)
    if _NC_CACHE is None:
        _NC_CACHE = build_nc()
    nc = _NC_CACHE
    res = run_bass_kernel_spmd(nc, maps, core_ids=list(range(NCORES)))
    full = np.empty((NA, S, D), np.float32)
    for c in range(NCORES):
        shard = np.asarray(res.results[c]["out"], np.float32).reshape(D, NA, 256)
        full[:, 256 * c:256 * c + 256, :] = shard.transpose(1, 2, 0)
    return full


# version 09
# speedup vs baseline: 1.1047x; 1.1047x over previous
import contextlib
import math
import numpy as np
import concourse.bass as bass
import concourse.mybir as mybir
from concourse.bass_utils import run_bass_kernel_spmd

F32 = mybir.dt.float32
BF16 = mybir.dt.bfloat16
I32 = mybir.dt.int32
AF = mybir.ActivationFunctionType
ALU = mybir.AluOpType
AX = mybir.AxisListType

NCORES = 8
NA = 4
S = 2048
D = 2048
DC = 16
TOK = 1024
NH = 8
CQ = 1536
CKV = 512
INTER = 5504
IC = 43
ICP = 48
SCALE = 1.0 / math.sqrt(128.0)
NEG = -1.0e30
EPS = 1e-5
FEAT = 6144
N_SUB = 4


class Sem:
    def __init__(self, h):
        self.h = h
        self.n = 0

    def inc(self, instr, k=1):
        instr.then_inc(self.h, k)
        self.n += k
        return self.n



def host_prep(inp):
    C = np.ascontiguousarray
    x = np.asarray(inp["L0_attn_x"], np.float32)
    wdqT = C(np.asarray(inp["L0_attn_w_dq"], np.float32).T)
    wuqT = C(np.asarray(inp["L0_attn_w_uq"], np.float32).T)
    wdkvT = C(np.asarray(inp["L0_attn_w_dkv"], np.float32).T)
    wukvT = C(np.asarray(inp["L0_attn_w_ukv"], np.float32).T)
    woutT = C(np.asarray(inp["L0_attn_w_out"], np.float32).T)
    lut = C(np.asarray(inp["L0_ffn_lut"], np.float32).reshape(1, 16))
    gidxT = C(np.asarray(inp["L0_ffn_gate_idx"], np.int32).T)
    uidxT = C(np.asarray(inp["L0_ffn_up_idx"], np.int32).T)
    didxT = np.asarray(inp["L0_ffn_down_idx"], np.int32).T
    didxT = C(np.concatenate([didxT, np.zeros((ICP * 128 - INTER, D), np.int32)], axis=0))

    winT = C(np.asarray(inp["L1_attn_W_in"], np.float32).T)
    wre = np.empty((D, FEAT), np.float32)
    for j in range(16):
        wre[:, j * 384:j * 384 + 128] = winT[:, j * 128:(j + 1) * 128]
        wre[:, j * 384 + 128:j * 384 + 256] = winT[:, 2048 + j * 128:2048 + (j + 1) * 128]
        wre[:, j * 384 + 256:j * 384 + 384] = winT[:, 4096 + j * 128:4096 + (j + 1) * 128]
    b_in = np.asarray(inp["L1_attn_b_in"], np.float32)
    binT = np.empty((128, 48), np.float32)
    for j in range(16):
        binT[:, 3 * j] = b_in[j * 128:(j + 1) * 128]
        binT[:, 3 * j + 1] = b_in[2048 + j * 128:2048 + (j + 1) * 128]
        binT[:, 3 * j + 2] = b_in[4096 + j * 128:4096 + (j + 1) * 128]
    gammaT = C(np.asarray(inp["L1_attn_gamma"], np.float32).reshape(16, 128).T)
    betaT = C(np.asarray(inp["L1_attn_beta"], np.float32).reshape(16, 128).T)
    wout1T = C(np.asarray(inp["L1_attn_W_out"], np.float32).T)
    boutT = C(np.asarray(inp["L1_attn_b_out"], np.float32).reshape(16, 128).T)
    wfT = C(np.asarray(inp["L1_ffn_weight"], np.float32).T)
    biasT = C(np.asarray(inp["L1_ffn_bias"], np.float32).reshape(16, 128).T)

    maps = []
    for c in range(NCORES):
        b = c // 2
        half = c % 2
        xslice = x[:, 256 * c:256 * c + 256, :]
        xsliceT = C(xslice.transpose(2, 0, 1).reshape(D, TOK))
        maps.append({
            "xTb": C(x[b].T),
            "xsliceT": xsliceT,
            "wdqT": wdqT,
            "wdkvT": wdkvT,
            "wuqT": C(wuqT[:, 1024 * half:1024 * half + 1024]),
            "wukvTk": C(wukvT[:, 1024 * half:1024 * half + 1024]),
            "wukvTv": C(wukvT[:, 2048 + 1024 * half:2048 + 1024 * half + 1024]),
            "woutT": woutT,
            "lutrow": lut,
            "gidxT": C(gidxT[256 * c:256 * c + 256]).reshape(2, 128, INTER),
            "uidxT": C(uidxT[256 * c:256 * c + 256]).reshape(2, 128, INTER),
            "didxT": C(didxT[768 * c:768 * c + 768]).reshape(6, 128, D),
            "winTre": wre,
            "binT": binT,
            "gammaT": gammaT,
            "betaT": betaT,
            "wout1T": wout1T,
            "boutT": boutT,
            "wfT": wfT,
            "biasT": biasT,
        })
    return maps



def emit_ln(nc, blk, PS, sems, xres_v, Bvec, sqbuf, gamma=None, beta=None, out_bf=None):
    r_mu = sqbuf[0][0:1, :]
    r_var = sqbuf[1][0:1, :]
    r_t = Bvec[0:1, 0:1024]
    r_inv = Bvec[0:1, 1024:2048]
    sPe, sVe, sSc = sems["pe"], sems["ve"], sems["sc"]
    ps_ln = _e(PS, nc.psum_tensor("ps_ln_%d" % sPe.n, [128, 2048], F32))
    ps_bc = _e(PS, nc.psum_tensor("ps_bc_%d" % sPe.n, [128, 2048], F32))
    pe0, sc0, ve0 = sPe.n, sSc.n, sVe.n
    rD = 1.0 / float(D)

    @blk.scalar
    def _(sc):
        for k in range(DC):
            if k >= 2:
                sc.wait_ge(sPe.h, pe0 + k - 1)
            i = sc.activation(sqbuf[k % 2][:, :], xres_v[:, k, :], AF.Square)
            sSc.inc(i)

    @blk.tensor
    def _(te):
        for k in range(DC):
            te.wait_ge(sSc.h, sc0 + k + 1)
            for sc_ in range(2):
                te.matmul(ps_ln[0:1, sc_ * 512:sc_ * 512 + 512], onesC_g[:, :],
                          xres_v[:, k, sc_ * 512:sc_ * 512 + 512],
                          start=(k == 0), stop=(k == DC - 1))
                i = te.matmul(ps_ln[0:1, 1024 + sc_ * 512:1024 + sc_ * 512 + 512], onesC_g[:, :],
                              sqbuf[k % 2][:, sc_ * 512:sc_ * 512 + 512],
                              start=(k == 0), stop=(k == DC - 1))
            sPe.inc(i)

    @blk.vector
    def _(v):
        v.wait_ge(sPe.h, pe0 + DC)
        v.tensor_scalar(r_mu, ps_ln[0:1, 0:1024], rD, None, ALU.mult)
        v.tensor_scalar(r_var, ps_ln[0:1, 1024:2048], rD, None, ALU.mult)
        v.drain()
        v.tensor_tensor(r_t, r_mu, r_mu, ALU.mult)
        v.drain()
        v.tensor_tensor(r_var, r_var, r_t, ALU.subtract)
        v.drain()
        i = v.tensor_scalar(r_var, r_var, EPS, None, ALU.add)
        sVe.inc(i)

    @blk.scalar
    def _(sc):
        sc.wait_ge(sVe.h, ve0 + 1)
        i = sc.activation(r_t, r_var, AF.Sqrt)
        sSc.inc(i)

    @blk.vector
    def _(v):
        v.wait_ge(sSc.h, sc0 + DC + 1)
        v.reciprocal(r_inv, r_t)
        v.drain()
        v.tensor_tensor(r_t, r_inv, r_inv, ALU.mult)
        v.drain()
        v.tensor_tensor(r_t, r_t, r_var, ALU.mult)
        v.drain()
        v.tensor_scalar(r_t, r_t, -0.5, 1.5, ALU.mult, ALU.add)
        v.drain()
        i = v.tensor_tensor(r_inv, r_inv, r_t, ALU.mult)
        sVe.inc(i)

    @blk.tensor
    def _(te):
        te.wait_ge(sVe.h, ve0 + 2)
        te.matmul(ps_bc[:, 0:512], onesR_g[:, :], r_mu[:, 0:512], start=True, stop=True)
        te.matmul(ps_bc[:, 512:1024], onesR_g[:, :], r_mu[:, 512:1024], start=True, stop=True)
        te.matmul(ps_bc[:, 1024:1536], onesR_g[:, :], r_inv[:, 0:512], start=True, stop=True)
        i = te.matmul(ps_bc[:, 1536:2048], onesR_g[:, :], r_inv[:, 512:1024], start=True, stop=True)
        sPe.inc(i)

    @blk.vector
    def _(v):
        v.wait_ge(sPe.h, pe0 + DC + 1)
        i = v.tensor_copy(Bvec[:, 0:2048], ps_bc[:, :])
        sVe.inc(i)
        for k in range(DC):
            v.drain()
            v.tensor_tensor(sqbuf[k % 2][:, :], xres_v[:, k, :], Bvec[:, 0:1024], ALU.subtract)
            v.drain()
            if gamma is None:
                i = v.tensor_tensor(xres_v[:, k, :], sqbuf[k % 2][:, :], Bvec[:, 1024:2048], ALU.mult)
            else:
                v.tensor_tensor(sqbuf[k % 2][:, :], sqbuf[k % 2][:, :], Bvec[:, 1024:2048], ALU.mult)
                v.drain()
                i = v.tensor_scalar(xres_v[:, k, :], sqbuf[k % 2][:, :],
                                    gamma[:, k:k + 1], beta[:, k:k + 1], ALU.mult, ALU.add)
            if out_bf is not None:
                v.drain()
                i = v.tensor_copy(out_bf[:, k, :], xres_v[:, k, :])
        sVe.inc(i)


def _e(stack, ctx):
    return stack.enter_context(ctx)


class GatherEmitter:

    def __init__(self, nc, itb, ibb, acc, tmpg, lutB, quarters, sIdx, sGs, sGd, ALU):
        self.quarters = quarters
        self.sIdx = sIdx
        self.sGs = sGs
        self.sGd = sGd
        self.itb_aps = itb
        self.acc_aps = acc
        self.gd0 = sGd.n
        self.ops = []
        nq = len(quarters)

        def mk(fn):
            self.ops.append(fn)

        for qi, (srcap, cols, dstap) in enumerate(quarters):
            def startf(qi=qi, cols=cols):
                def f(v):
                    v.wait_ge(sIdx[qi % 2].h, 16 * (qi // 2 + 1))
                    if qi >= 2:
                        v.wait_ge(sGs[qi % 2].h, 16 * (qi // 2))
                    v.drain()
                    v.tensor_copy(ibb[:, 0:cols], itb[qi % 2][:, 0:cols])
                return f
            mk(startf())
            def p0(qi=qi, cols=cols):
                def f(v):
                    v.drain()
                    v.tensor_scalar(acc[qi % 2][:, 0:cols], ibb[:, 0:cols],
                                    0.0, lutB[:, 0:1], ALU.is_equal, ALU.mult)
                return f
            mk(p0())
            for j in range(1, 16):
                def pj(qi=qi, cols=cols, j=j):
                    def f(v):
                        v.drain()
                        v.tensor_scalar(tmpg[:, 0:cols], ibb[:, 0:cols],
                                        float(j), lutB[:, j:j + 1], ALU.is_equal, ALU.mult)
                        v.drain()
                        i = v.tensor_tensor(acc[qi % 2][:, 0:cols], acc[qi % 2][:, 0:cols],
                                            tmpg[:, 0:cols], ALU.add)
                        if j == 15:
                            i.then_inc(sGd.h, 1)
                            sGd.n += 1
                    return f
                mk(pj())
        self.pos = 0
        self.ops_per_q = len(self.ops) // nq
        self.store_totals = []
        tot = [0, 0]
        for qi in range(nq):
            tot[qi % 2] += 16
            self.store_totals.append(tuple(tot))

    @property
    def done_quarters(self):
        return self.pos // self.ops_per_q

    def emit(self, v, k):
        end = min(self.pos + k, len(self.ops))
        while self.pos < end:
            self.ops[self.pos](v)
            self.pos += 1

    def flush(self, v):
        self.emit(v, len(self.ops))

    def dma_chain(self, eng, q_from, q_to, extra=None):
        N = len(self.quarters)

        def load(q):
            srcap, cols, dstap = self.quarters[q]
            eng.dma_start(self.itb_aps[q % 2][:, 0:cols], srcap).then_inc(self.sIdx[q % 2].h, 16)
            self.sIdx[q % 2].n += 16

        if q_from == 0:
            for q in (0, 1):
                if q < N:
                    load(q)
        for q in range(q_from, q_to):
            if extra and q in extra:
                extra[q](eng)
            eng.wait_ge(self.sGd.h, self.gd0 + q + 1)
            srcap, cols, dstap = self.quarters[q]
            eng.dma_start(dstap, self.acc_aps[q % 2][:, 0:cols]).then_inc(self.sGs[q % 2].h, 16)
            self.sGs[q % 2].n += 16
            nxt = q + 2
            if nxt < N and nxt < q_to + 2:
                load(nxt)


onesC_g = None
onesR_g = None



def build_nc():
    global onesC_g, onesR_g
    nc = bass.Bass("TRN2", target_bir_lowering=False)
    ap = lambda name, shape, dt, kind: nc.dram_tensor(name, shape, dt, kind=kind).ap()
    aps = lambda name, shape, dt: nc.dram_tensor(name, shape, dt, kind="Internal",
                                                 addr_space="Shared").ap()
    EI, EO, IN = "ExternalInput", "ExternalOutput", "Internal"

    xTb = ap("xTb", [D, S], F32, EI)
    xsliceT = ap("xsliceT", [D, TOK], F32, EI)
    wdqT = ap("wdqT", [D, CQ], F32, EI)
    wdkvT = ap("wdkvT", [D, CKV], F32, EI)
    wuqT = ap("wuqT", [CQ, 1024], F32, EI)
    wukvTk = ap("wukvTk", [CKV, 1024], F32, EI)
    wukvTv = ap("wukvTv", [CKV, 1024], F32, EI)
    woutT = ap("woutT", [D, D], F32, EI)
    lutrow = ap("lutrow", [1, 16], F32, EI)
    gidxT = ap("gidxT", [2, 128, INTER], I32, EI)
    uidxT = ap("uidxT", [2, 128, INTER], I32, EI)
    didxT = ap("didxT", [6, 128, D], I32, EI)
    winTre = ap("winTre", [D, FEAT], F32, EI)
    binT = ap("binT", [128, 48], F32, EI)
    gammaT = ap("gammaT", [128, 16], F32, EI)
    betaT = ap("betaT", [128, 16], F32, EI)
    wout1T = ap("wout1T", [D, D], F32, EI)
    boutT = ap("boutT", [128, 16], F32, EI)
    wfT = ap("wfT", [D, D], F32, EI)
    biasT = ap("biasT", [128, 16], F32, EI)

    out = ap("out", [DC, 128, TOK], F32, EO)

    ag_o_in = [ap("ag_o_in0", [128, 6 * S], BF16, IN), ap("ag_o_in1", [128, 2 * S], BF16, IN)]
    ag_o_out = [aps("ag_o_out0", [NCORES, 128, 6 * S], BF16),
                aps("ag_o_out1", [NCORES, 128, 2 * S], BF16)]
    GOFF = 0
    UOFF = 2 * 128 * INTER
    GUP = 4 * 128 * INTER
    DLEN = 6 * 128 * D
    ag_gu_in = ap("ag_gu_in", [GUP], BF16, IN)
    ag_gu_out = aps("ag_gu_out", [NCORES, GUP], BF16)
    ag_dn_in = ap("ag_dn_in", [DLEN], BF16, IN)
    ag_dn_out = aps("ag_dn_out", [NCORES, DLEN], BF16)
    x_sp = [ap(f"x_sp{i}", [128, DC * TOK], F32, IN) for i in range(2)]

    RG = [list(range(NCORES))]
    rank = nc.axis_index(bass.AxisInfo(NCORES, 1))

    with contextlib.ExitStack() as ST:
        onesC = _e(ST, nc.sbuf_tensor("onesC", [128, 1], F32))
        onesR = _e(ST, nc.sbuf_tensor("onesR", [1, 128], F32))
        identF = _e(ST, nc.sbuf_tensor("identF", [128, 128], F32))
        Mneg = _e(ST, nc.sbuf_tensor("Mneg", [128, 128], BF16))
        MnegT = _e(ST, nc.sbuf_tensor("MnegT", [128, 128], BF16))
        pcol = _e(ST, nc.sbuf_tensor("pcol", [128, 1], F32))
        lutB = _e(ST, nc.sbuf_tensor("lutB", [128, 16], F32))
        lutrow_sb = _e(ST, nc.sbuf_tensor("lutrow_sb", [1, 16], F32))
        acols = _e(ST, nc.sbuf_tensor("acols", [128, 48], F32))
        DTS = _e(ST, contextlib.ExitStack())
        DTA = [_e(DTS, nc.sbuf_tensor(f"DTA{m}", [128, 128], F32)) for m in range(2)]
        DTB = [_e(DTS, nc.sbuf_tensor(f"DTB{m}", [128, 128], F32)) for m in range(2)]
        DTC = [_e(DTS, nc.sbuf_tensor(f"DTC{m}", [128, 128], F32)) for m in range(2)]
        onesC_g, onesR_g = onesC, onesR

        sLd = Sem(_e(ST, nc.semaphore("sLd")))
        sLg = Sem(_e(ST, nc.semaphore("sLg")))
        sSt = Sem(_e(ST, nc.semaphore("sSt")))
        sPe = Sem(_e(ST, nc.semaphore("sPe")))
        sVe = Sem(_e(ST, nc.semaphore("sVe")))
        sSc = Sem(_e(ST, nc.semaphore("sSc")))
        sGp = Sem(_e(ST, nc.semaphore("sGp")))
        sCc = Sem(_e(ST, nc.semaphore("sCc")))
        sOd = [Sem(_e(ST, nc.semaphore(f"sOd{i}"))) for i in range(2)]
        sIdx = [Sem(_e(ST, nc.semaphore(f"sIdx{i}"))) for i in range(2)]
        sGs = [Sem(_e(ST, nc.semaphore(f"sGs{i}"))) for i in range(2)]
        sGU = [Sem(_e(ST, nc.semaphore(f"sGU{i}"))) for i in range(4)]
        sWd = [Sem(_e(ST, nc.semaphore(f"sWd{i}"))) for i in range(2)]
        sWc = [Sem(_e(ST, nc.semaphore(f"sWc{i}"))) for i in range(2)]
        sWf = [Sem(_e(ST, nc.semaphore(f"sWf{i}"))) for i in range(2)]
        sWq = [Sem(_e(ST, nc.semaphore(f"sWq{i}"))) for i in range(2)]
        sGd = Sem(_e(ST, nc.semaphore("sGd")))
        sLdh = [Sem(_e(ST, nc.semaphore(f"sLdh{i}"))) for i in range(2)]
        sWo = Sem(_e(ST, nc.semaphore("sWo")))
        sems = {"pe": sPe, "ve": sVe, "sc": sSc}

        with nc.Block() as blk, contextlib.ExitStack() as PS:
            ps16 = _e(PS, nc.psum_tensor("ps16", [128, 16], F32))

            @blk.sync
            def _(s):
                s.dma_start(lutrow_sb[:], lutrow).then_inc(sLd.h, 16)
            sLd.n += 16

            S0S = _e(PS, contextlib.ExitStack())
            frow = _e(S0S, nc.sbuf_tensor("frow", [128, 128], F32))
            firow = _e(S0S, nc.sbuf_tensor("firow", [128, 128], I32))
            iw1 = _e(S0S, nc.sbuf_tensor("iw1", [128, 128], I32))
            iw2 = _e(S0S, nc.sbuf_tensor("iw2", [128, 128], I32))
            fw1 = _e(S0S, nc.sbuf_tensor("fw1", [128, 128], F32))
            fw2 = _e(S0S, nc.sbuf_tensor("fw2", [128, 128], F32))

            @blk.gpsimd
            def _(g):
                g.iota(frow[:], [[1, 128]], base=0, channel_multiplier=0, allow_small_or_imprecise_dtypes=True)
                g.drain()
                g.iota(pcol[:], [[1, 1]], base=0, channel_multiplier=1, allow_small_or_imprecise_dtypes=True)
                g.drain()
                i = g.iota(firow[:], [[1, 128]], base=0, channel_multiplier=0)
                sGp.inc(i)

            @blk.vector
            def _(v):
                v.memset(onesC[:], 1.0)
                v.memset(onesR[:], 1.0)
                v.wait_ge(sGp.h, sGp.n)
                v.tensor_scalar(identF[:], frow[:], pcol[:, :], 1.0, ALU.is_equal, ALU.mult)
                v.tensor_scalar(Mneg[:], frow[:], pcol[:, :], NEG, ALU.is_gt, ALU.mult)
                i = v.tensor_scalar(MnegT[:], frow[:], pcol[:, :], NEG, ALU.is_lt, ALU.mult)
                sVe.inc(i)

            @blk.tensor
            def _(te):
                te.wait_ge(sLd.h, sLd.n)
                te.wait_ge(sVe.h, sVe.n)
                i = te.matmul(ps16[:, :], onesR[:, :], lutrow_sb[:, :], start=True, stop=True)
                sPe.inc(i)

            @blk.vector
            def _(v):
                v.wait_ge(sPe.h, sPe.n)
                i = v.tensor_copy(lutB[:, :], ps16[:, :])
                sVe.inc(i)

            @blk.vector
            def _(v):
                v.wait_ge(sGp.h, sGp.n)
                for m in range(2):
                    v.tensor_scalar(iw1[:], firow[:], 2, None, ALU.arith_shift_right)
                    v.tensor_scalar(iw2[:], firow[:], 1, None, ALU.arith_shift_right)
                    v.drain()
                    v.tensor_scalar(iw1[:], iw1[:], 1, None, ALU.arith_shift_left)
                    v.tensor_scalar(iw2[:], iw2[:], 1, None, ALU.bitwise_and)
                    v.drain()
                    v.tensor_tensor(iw1[:], iw1[:], iw2[:], ALU.add)
                    v.drain()
                    v.tensor_scalar(fw1[:], iw1[:], float(64 * m), None, ALU.add)
                    v.drain()
                    v.tensor_scalar(DTA[m][:], fw1[:], pcol[:, :], 1.0,
                                    ALU.is_equal, ALU.mult)
                    v.drain()
                    v.tensor_scalar(iw1[:], firow[:], 2, None, ALU.arith_shift_right)
                    v.tensor_scalar(iw2[:], firow[:], 1, None, ALU.bitwise_and)
                    v.drain()
                    v.tensor_scalar(iw1[:], iw1[:], 1, None, ALU.arith_shift_left)
                    v.drain()
                    v.tensor_tensor(iw1[:], iw1[:], iw2[:], ALU.add)
                    v.drain()
                    v.tensor_scalar(fw1[:], iw1[:], float(64 * m), None, ALU.add)
                    v.drain()
                    v.tensor_scalar(DTB[m][:], fw1[:], pcol[:, :], 1.0,
                                    ALU.is_equal, ALU.mult)
                    v.drain()
                    v.tensor_scalar(iw1[:], firow[:], 1, None, ALU.arith_shift_right)
                    v.tensor_scalar(iw2[:], firow[:], 1, None, ALU.bitwise_and)
                    v.drain()
                    v.tensor_scalar(iw1[:], iw1[:], 32 * m, None, ALU.subtract)
                    v.drain()
                    v.tensor_scalar(iw1[:], iw1[:], 2, None, ALU.arith_shift_left)
                    v.drain()
                    v.tensor_tensor(iw1[:], iw1[:], iw2[:], ALU.add)
                    v.drain()
                    v.tensor_scalar(fw1[:], iw1[:], 0.0, None, ALU.add)
                    v.drain()
                    v.tensor_scalar(fw2[:], fw1[:], pcol[:, :], -1.0,
                                    ALU.is_equal, ALU.mult)
                    v.tensor_scalar(fw1[:], fw1[:], 2.0, None, ALU.add)
                    v.drain()
                    v.tensor_scalar(fw1[:], fw1[:], pcol[:, :], -1.0,
                                    ALU.is_equal, ALU.mult)
                    v.drain()
                    i = v.tensor_tensor(DTC[m][:], fw1[:], fw2[:], ALU.add)
                    sVe.inc(i)


        SG = _e(ST, contextlib.ExitStack())
        g_itb = [_e(SG, nc.sbuf_tensor(f"g_itb{i}", [128, 688], I32)) for i in range(2)]
        g_ibb = _e(SG, nc.sbuf_tensor("g_ibb", [128, 688], BF16))
        g_acc = [_e(SG, nc.sbuf_tensor(f"g_acc{i}", [128, 688], BF16)) for i in range(2)]
        g_tmp = _e(SG, nc.sbuf_tensor("g_tmp", [128, 688], BF16))
        quarters = []
        for agt, base, idxt, ntile, tcols, bounds in (
                (ag_gu_in, GOFF, gidxT, 2, INTER, [688 * i for i in range(8)] + [INTER]),
                (ag_gu_in, UOFF, uidxT, 2, INTER, [688 * i for i in range(8)] + [INTER]),
                (ag_dn_in, 0, didxT, 6, D, [0, 688, 1376, 2048])):
            for t in range(ntile):
                tb = base + t * 128 * tcols
                tile_dst = agt[tb:tb + 128 * tcols].rearrange("(p c) -> p c", p=128)
                for q in range(len(bounds) - 1):
                    c0_, c1_ = bounds[q], bounds[q + 1]
                    quarters.append((idxt[t][:, c0_:c1_], c1_ - c0_, tile_dst[:, c0_:c1_]))
        GE = GatherEmitter(nc, [t[:] for t in g_itb], g_ibb[:], [a[:] for a in g_acc],
                           g_tmp[:], lutB[:], quarters, sIdx, sGs, sGd, ALU)
        N_GU_Q = 32

        S_attn = _e(ST, contextlib.ExitStack())
        xTbS = _e(S_attn, nc.sbuf_tensor("xTbS", [128, DC * S], BF16))
        wdqS = _e(S_attn, nc.sbuf_tensor("wdqS", [128, DC * CQ], BF16))
        wdkvS = _e(S_attn, nc.sbuf_tensor("wdkvS", [128, DC * CKV], BF16))
        xTbS_v = xTbS[:].rearrange("p (k s) -> p k s", k=DC)
        wdqS_v = wdqS[:].rearrange("p (k j) -> p k j", k=DC)
        wdkvS_v = wdkvS[:].rearrange("p (k j) -> p k j", k=DC)
        qT_v = xTbS[:, 0:NH * S].rearrange("p (h s) -> p h s", h=NH)
        kT_v = xTbS[:, NH * S:2 * NH * S].rearrange("p (h s) -> p h s", h=NH)
        Vt_v = wdqS[:, 0:16 * 1024].rearrange("p (t r) -> p t r", t=16)
        wuk_v = wdqS[:, 16 * 1024:16 * 1024 + 4 * 1024].rearrange("p (k j) -> p k j", k=4)
        wuq_sl = [wdqS[:, 20 * 1024 + i * 1536:20 * 1024 + (i + 1) * 1536]
                  .rearrange("p (k j) -> p k j", k=12) for i in range(2)]
        wuv_v = wdkvS[:, 0:4 * 1024].rearrange("p (k j) -> p k j", k=4)

        S_c = _e(S_attn, contextlib.ExitStack())
        cqT = _e(S_c, nc.sbuf_tensor("cqT", [128, 12 * S], BF16))
        ckvT = _e(S_c, nc.sbuf_tensor("ckvT", [128, 4 * S], BF16))
        cqT_v = cqT[:].rearrange("p (j s) -> p j s", j=12)
        ckvT_v = ckvT[:].rearrange("p (j s) -> p j s", j=4)

        with nc.Block() as blk, contextlib.ExitStack() as PS:
            psA = [_e(PS, nc.psum_tensor(f"psA{i}", [128, 2048], F32)) for i in range(2)]

            lg0a = sLg.n

            @blk.gpsimd
            def _(g):
                xv = xTb.rearrange("(k p) s -> p k s", p=128)
                wv = wdqT.rearrange("(k p) j -> p k j", p=128)
                for k in range(DC):
                    if k:
                        g.wait_ge(sLg.h, lg0a + 32 * k)
                    g.dma_start(xTbS_v[:, k, :], xv[:, k, :]).then_inc(sLg.h, 16)
                    g.dma_start(wdqS_v[:, k, :], wv[:, k, :]).then_inc(sLg.h, 16)
                g.wait_ge(sLg.h, lg0a + 32 * DC)
                g.dma_start(wdkvS_v, wdkvT.rearrange("(k p) j -> p k j", p=128)).then_inc(sLg.h, 16)
            sLg.n += 33 * 16
            p0, v0 = sPe.n, sVe.n
            jobs = [(cqT_v, wdqS_v, 12), (ckvT_v, wdkvS_v, 4)]

            @blk.tensor
            def _(te):
                it = 0
                for dst, wS, nj in jobs:
                    for j in range(nj):
                        if it >= 2:
                            te.wait_ge(sVe.h, v0 + it - 1)
                        for k in range(DC):
                            if it == 0:
                                te.wait_ge(sLg.h, lg0a + 32 * (k + 1))
                            elif it == 12 and k == 0:
                                te.wait_ge(sLg.h, sLg.n)
                            for sc_ in range(4):
                                i = te.matmul(psA[it % 2][:, sc_ * 512:sc_ * 512 + 512],
                                              wS[:, k, j * 128:j * 128 + 128],
                                              xTbS_v[:, k, sc_ * 512:sc_ * 512 + 512],
                                              start=(k == 0), stop=(k == DC - 1))
                        it += 1
                        sPe.inc(i)

            @blk.vector
            def _(v):
                it = 0
                for dst, wS, nj in jobs:
                    for j in range(nj):
                        v.wait_ge(sPe.h, p0 + it + 1)
                        i = v.tensor_copy(dst[:, j, :], psA[it % 2][:, :])
                        it += 1
                        sVe.inc(i)

        with nc.Block() as blk, contextlib.ExitStack() as PS:
            psA = [_e(PS, nc.psum_tensor(f"psB{i}", [128, 2048], F32)) for i in range(2)]
            p0, v0 = sPe.n, sVe.n

            def a2_sched():
                for h in range(NH):
                    yield ("q", h)
                for h in range(NH):
                    yield ("k", h)
                for tb in range(16):
                    yield ("v", tb)

            pe_at2 = {}
            psim = p0
            for it, _kind in enumerate(a2_sched()):
                psim += 1
                pe_at2[it] = psim

            @blk.gpsimd
            def _(g):
                g.dma_start(wuk_v, wukvTk.rearrange("(k p) j -> p k j", p=128)).then_inc(sLg.h, 16)
                g.dma_start(wuv_v, wukvTv.rearrange("(k p) j -> p k j", p=128)).then_inc(sLg.h, 16)
                for h in range(NH):
                    if h >= 2:
                        g.wait_ge(sPe.h, pe_at2[h - 2])
                    g.dma_start(wuq_sl[h % 2],
                                wuqT[:, h * 128:h * 128 + 128]
                                .rearrange("(k p) j -> p k j", p=128)).then_inc(sWq[h % 2].h, 16)
                    sWq[h % 2].n += 16
            sLg.n += 32
            lg_a2 = sLg.n

            @blk.tensor
            def _(te):
                for it, (kind, idx) in enumerate(a2_sched()):
                    if it >= 2:
                        te.wait_ge(sVe.h, v0 + it - 1)
                    if kind == "q":
                        te.wait_ge(sWq[idx % 2].h, 16 * (idx // 2 + 1))
                        for k in range(12):
                            for sc_ in range(4):
                                i = te.matmul(psA[it % 2][:, sc_ * 512:sc_ * 512 + 512],
                                              wuq_sl[idx % 2][:, k, :],
                                              cqT_v[:, k, sc_ * 512:sc_ * 512 + 512],
                                              start=(k == 0), stop=(k == 11))
                    elif kind == "k":
                        te.wait_ge(sLg.h, lg_a2)
                        for k in range(4):
                            for sc_ in range(4):
                                i = te.matmul(psA[it % 2][:, sc_ * 512:sc_ * 512 + 512],
                                              wuk_v[:, k, idx * 128:idx * 128 + 128],
                                              ckvT_v[:, k, sc_ * 512:sc_ * 512 + 512],
                                              start=(k == 0), stop=(k == 3))
                    else:
                        for k in range(4):
                            for sc_ in range(2):
                                i = te.matmul(psA[it % 2][:, sc_ * 512:sc_ * 512 + 512],
                                              ckvT_v[:, k, idx * 128:idx * 128 + 128],
                                              wuv_v[:, k, sc_ * 512:sc_ * 512 + 512],
                                              start=(k == 0), stop=(k == 3))
                    sPe.inc(i)

            @blk.vector
            def _(v):
                for it, (kind, idx) in enumerate(a2_sched()):
                    v.wait_ge(sPe.h, p0 + it + 1)
                    if kind == "q":
                        i = v.tensor_copy(qT_v[:, idx, :], psA[it % 2][:, :])
                    elif kind == "k":
                        i = v.tensor_copy(kT_v[:, idx, :], psA[it % 2][:, :])
                    else:
                        i = v.tensor_copy(Vt_v[:, idx, :], psA[it % 2][:, 0:1024])
                    sVe.inc(i)
        S_c.close()

        with contextlib.ExitStack() as PH:
            Pscr = [_e(PH, nc.sbuf_tensor(f"Pscr{i}", [128, 2048], BF16)) for i in range(2)]
            negb = [_e(PH, nc.sbuf_tensor(f"negb{i}", [128, 1], F32)) for i in range(2)]
            mcols = _e(PH, nc.sbuf_tensor("mcols", [128, 128], F32))
            lcols = _e(PH, nc.sbuf_tensor("lcols", [128, 128], F32))
            lncols = _e(PH, nc.sbuf_tensor("lncols", [128, 128], F32))
            bcol = _e(PH, nc.sbuf_tensor("bcol", [128, 128], F32))
            Bsb = _e(PH, nc.sbuf_tensor("Bsb", [128, 2048], F32))
            Pin = [_e(PH, nc.sbuf_tensor(f"Pin{i}", [128, 512], F32)) for i in range(2)]
            PTc = [_e(PH, nc.sbuf_tensor(f"PTc{i}", [128, 512], BF16)) for i in range(2)]
            oTh = [_e(PH, nc.sbuf_tensor(f"oTh{i}", [128, 2048], BF16)) for i in range(2)]

            pairs = [(h, qb) for h in range(NH) for qb in range(16)]
            with nc.Block() as blk, contextlib.ExitStack() as PS:
                psS = [_e(PS, nc.psum_tensor(f"psS{i}", [128, 2048], F32)) for i in range(2)]
                p0, v0, c0 = sPe.n, sVe.n, sSc.n

                @blk.tensor
                def _(te):
                    for it, (h, qb) in enumerate(pairs):
                        ext = (qb + 1) * 128
                        if it >= 2:
                            te.wait_ge(sSc.h, c0 + it - 1)
                        for k0 in range(0, ext, 512):
                            w = min(512, ext - k0)
                            i = te.matmul(psS[it % 2][:, k0:k0 + w],
                                          qT_v[:, h, qb * 128:qb * 128 + 128],
                                          kT_v[:, h, k0:k0 + w], start=True, stop=True)
                        sPe.inc(i)

                @blk.vector
                def _(v):
                    for it, (h, qb) in enumerate(pairs):
                        ext = (qb + 1) * 128
                        v.wait_ge(sPe.h, p0 + it + 1)
                        v.tensor_tensor(psS[it % 2][:, qb * 128:qb * 128 + 128],
                                        psS[it % 2][:, qb * 128:qb * 128 + 128], Mneg[:, :], ALU.add)
                        v.drain()
                        v.tensor_reduce(mcols[:, it:it + 1], psS[it % 2][:, 0:ext], AX.X, ALU.max)
                        v.drain()
                        i = v.tensor_scalar(negb[it % 2][:, :], mcols[:, it:it + 1], -SCALE, None, ALU.mult)
                        sVe.inc(i)
                        GE.emit(v, 1)

                @blk.scalar
                def _(sc):
                    for it, (h, qb) in enumerate(pairs):
                        ext = (qb + 1) * 128
                        sc.wait_ge(sVe.h, v0 + it + 1)
                        i = sc.activation(Pscr[it % 2][:, 0:ext], psS[it % 2][:, 0:ext], AF.Exp,
                                          bias=negb[it % 2][:, :], scale=SCALE,
                                          accum_out=lcols[:, it:it + 1])
                        sSc.inc(i)

                B1q = GE.done_quarters

                @blk.sync
                def _(s):
                    GE.dma_chain(s, 0, B1q)

            with nc.Block() as blk:
                @blk.scalar
                def _(sc):
                    sc.drain()
                    i = sc.activation(lncols[:, :], lcols[:, :], AF.Ln)
                    sSc.inc(i)

                @blk.vector
                def _(v):
                    v.wait_ge(sSc.h, sSc.n)
                    i = v.scalar_tensor_tensor(bcol[:, :], mcols[:, :], SCALE, lncols[:, :],
                                               ALU.mult, ALU.add)
                    sVe.inc(i)

            with nc.Block() as blk, contextlib.ExitStack() as PS:
                psO = _e(PS, nc.psum_tensor("psO", [128, 2048], F32))
                psX = _e(PS, nc.psum_tensor("psX", [128, 2048], F32))
                p0, v0, c0, st0 = sPe.n, sVe.n, sSc.n, sSt.n

                def head_chunks(h):
                    ch = []
                    for kb in range(16):
                        q0 = kb * 128
                        c = q0
                        while c < S:
                            up = min(S, (c // 512 + 1) * 512)
                            bank = c // 512
                            stop = (kb == min(15, 4 * bank + 3))
                            ch.append((kb, c, up - c, c == q0, stop))
                            c = up
                    return ch

                pe_n, ve_n, sc_n, st_n = [p0], [v0], [c0], [st0]
                tr_done_at = {}
                bevac_at = {}
                stt_at = {}
                exp_at = {}
                oevac_at = {}
                odma_at = {}
                vsim = v0
                for h in range(NH):
                    for qb in range(16):
                        vsim += 1
                        tr_done_at[(h, qb)] = vsim
                    vsim += 1
                    bevac_at[h] = vsim
                    for ci in range(len(head_chunks(h))):
                        vsim += 1
                        stt_at[(h, ci)] = vsim
                    vsim += 1
                    oevac_at[h] = vsim
                csim = c0
                for h in range(NH):
                    for ci in range(len(head_chunks(h))):
                        csim += 1
                        exp_at[(h, ci)] = csim
                for h in range(NH):
                    odma_at[h] = 16 * (h // 2 + 1)
                pe_tr_at = {}
                pe_b_at = {}
                pe_st_at = {}
                pe_pv_at = {}
                psim = p0
                for h in range(NH):
                    for qb in range(16):
                        psim += 1
                        pe_tr_at[(h, qb)] = psim
                    psim += 1
                    pe_b_at[h] = psim
                    for ci in range(len(head_chunks(h))):
                        psim += 1
                        pe_st_at[(h, ci)] = psim
                    psim += 1
                    pe_pv_at[h] = psim

                @blk.tensor
                def _(te):
                    for h in range(NH):
                        chunks = head_chunks(h)
                        for qb in range(16):
                            if qb >= 2:
                                te.wait_ge(sVe.h, tr_done_at[(h, qb - 2)])
                            elif h > 0 and qb == 0:
                                te.wait_ge(sVe.h, oevac_at[h - 1])
                            i = te.transpose(psX[0:1, (qb % 2) * 512:(qb % 2) * 512 + 128],
                                             bcol[:, h * 16 + qb:h * 16 + qb + 1], identF[:, :])
                            sPe.inc(i)
                        te.wait_ge(sVe.h, tr_done_at[(h, 15)])
                        for seg in range(4):
                            i = te.matmul(psX[:, seg * 512:seg * 512 + 512], onesR[:, :],
                                          Bsb[0:1, seg * 512:seg * 512 + 512], start=True, stop=True)
                        sPe.inc(i)
                        te.wait_ge(sVe.h, bevac_at[h])
                        if h > 0:
                            te.wait_ge(sVe.h, oevac_at[h - 1])
                        for ci, (kb, q0, w, first, stop) in enumerate(chunks):
                            if ci >= 2:
                                te.wait_ge(sVe.h, stt_at[(h, ci - 2)])
                            i = te.matmul(psX[:, (ci % 2) * 512:(ci % 2) * 512 + w],
                                          kT_v[:, h, kb * 128:kb * 128 + 128],
                                          qT_v[:, h, q0:q0 + w], start=True, stop=True)
                            sPe.inc(i)
                            if ci >= 1:
                                kbp, q0p, wp, firstp, stopp = chunks[ci - 1]
                                te.wait_ge(sSc.h, exp_at[(h, ci - 1)])
                                te.matmul(psO[:, q0p:q0p + wp],
                                          Vt_v[:, kbp, h * 128:h * 128 + 128],
                                          PTc[(ci - 1) % 2][:, 0:wp],
                                          start=(kbp == 0), stop=stopp)
                        kbp, q0p, wp, firstp, stopp = chunks[-1]
                        te.wait_ge(sSc.h, exp_at[(h, len(chunks) - 1)])
                        i = te.matmul(psO[:, q0p:q0p + wp],
                                      Vt_v[:, kbp, h * 128:h * 128 + 128],
                                      PTc[(len(chunks) - 1) % 2][:, 0:wp],
                                      start=(kbp == 0), stop=stopp)
                        sPe.inc(i)

                oth_hook_at = []

                @blk.vector
                def _(v):
                    for h in range(NH):
                        chunks = head_chunks(h)
                        for qb in range(16):
                            v.wait_ge(sPe.h, pe_tr_at[(h, qb)])
                            v.drain()
                            i = v.tensor_copy(Bsb[0:1, qb * 128:qb * 128 + 128],
                                              psX[0:1, (qb % 2) * 512:(qb % 2) * 512 + 128])
                            sVe.inc(i)
                            GE.emit(v, 1)
                        v.wait_ge(sPe.h, pe_b_at[h])
                        v.drain()
                        i = v.tensor_copy(Bsb[:, :], psX[:, :])
                        sVe.inc(i)
                        for ci, (kb, q0, w, first, stop) in enumerate(chunks):
                            v.wait_ge(sPe.h, pe_st_at[(h, ci)])
                            i = v.scalar_tensor_tensor(Pin[ci % 2][:, 0:w],
                                                       psX[:, (ci % 2) * 512:(ci % 2) * 512 + w],
                                                       SCALE, Bsb[:, q0:q0 + w],
                                                       ALU.mult, ALU.subtract)
                            if first:
                                v.drain()
                                i = v.tensor_tensor(Pin[ci % 2][:, 0:128], Pin[ci % 2][:, 0:128],
                                                    MnegT[:, :], ALU.add)
                            sVe.inc(i)
                            GE.emit(v, 2)
                        v.wait_ge(sPe.h, pe_pv_at[h])
                        if h >= 2:
                            v.wait_ge(sOd[h % 2].h, odma_at[h - 2])
                        i = v.tensor_copy(oTh[h % 2][:, :], psO[:, :])
                        sVe.inc(i)
                        oth_hook_at.append(GE.done_quarters + 1)
                        GE.emit(v, 4)

                @blk.scalar
                def _(sc):
                    for h in range(NH):
                        chunks = head_chunks(h)
                        for ci, (kb, q0, w, first, stop) in enumerate(chunks):
                            sc.wait_ge(sVe.h, stt_at[(h, ci)])
                            i = sc.activation(PTc[ci % 2][:, 0:w], Pin[ci % 2][:, 0:w], AF.Exp)
                            sSc.inc(i)

                B2q = GE.done_quarters

                def _emit_oth(e, h):
                    e.wait_ge(sVe.h, oevac_at[h])
                    dstp = ag_o_in[0][:, h * S:(h + 1) * S] if h < 6 else \
                        ag_o_in[1][:, (h - 6) * S:(h - 5) * S]
                    e.dma_start(dstp, oTh[h % 2][:, :]).then_inc(sOd[h % 2].h, 16)
                    sOd[h % 2].n += 16

                @blk.sync
                def _(s):
                    if B2q > B1q:
                        hook_q = {}
                        for h in range(NH):
                            q = min(B2q - 1, max(B1q, oth_hook_at[h]))
                            hook_q.setdefault(q, []).append(h)
                        extra = {q: (lambda hs: (lambda e: [_emit_oth(e, h) for h in hs]))(hs)
                                 for q, hs in hook_q.items()}
                        GE.dma_chain(s, B1q, B2q, extra=extra)
                    else:
                        for h in range(NH):
                            _emit_oth(s, h)
                    s.wait_ge(sOd[0].h, sOd[0].n)
                    s.wait_ge(sOd[1].h, sOd[1].n)

                @blk.gpsimd
                def _(g):
                    g.wait_ge(sOd[0].h, 48)
                    g.wait_ge(sOd[1].h, 48)
                    i = g.collective_compute("AllGather", ALU.bypass, RG,
                                             ins=[ag_o_in[0]], outs=[ag_o_out[0]])
                    sCc.inc(i)
                    g.wait_ge(sOd[0].h, 64)
                    g.wait_ge(sOd[1].h, 64)
                    i = g.collective_compute("AllGather", ALU.bypass, RG,
                                             ins=[ag_o_in[1]], outs=[ag_o_out[1]])
                    sCc.inc(i)

        B2q_g = [GE.done_quarters]
        cc_o_at = {hb: sCc.n - 1 + hb for hb in range(2)}
        S_attn.close()

        with contextlib.ExitStack() as PH:
            xres = _e(PH, nc.sbuf_tensor("xresA", [128, DC * TOK], F32))
            Bvec = _e(PH, nc.sbuf_tensor("BvecA", [128, 2048], F32))
            sqbuf = [_e(PH, nc.sbuf_tensor(f"sqbufA{i}", [128, 1024], F32)) for i in range(2)]
            orcv = _e(PH, nc.sbuf_tensor("orcv", [128, DC * TOK], BF16))
            woutS = _e(PH, nc.sbuf_tensor("woutS", [128, DC * D], BF16))
            xres_v = xres[:].rearrange("p (k t) -> p k t", k=DC)
            orcv_v = orcv[:].rearrange("p (k t) -> p k t", k=DC)
            woutS_v = woutS[:].rearrange("p (k j) -> p k j", k=DC)

            with nc.Block() as blk, contextlib.ExitStack() as PS:
                psA = [_e(PS, nc.psum_tensor(f"psC{i}", [128, 1024], F32)) for i in range(2)]

                @blk.gpsimd
                def _(g):
                    g.dma_start(woutS_v, woutT.rearrange("(k p) j -> p k j", p=128)).then_inc(sLg.h, 16)
                sLg.n += 16

                @blk.sync
                def _(s):
                    s.dma_start(xres_v, xsliceT.rearrange("(k p) t -> p k t", p=128)).then_inc(sLd.h, 16)
                    for hb in range(2):
                        nh = 6 if hb == 0 else 2
                        s.wait_ge(sCc.h, cc_o_at[hb])
                        for r in range(NCORES):
                            half = r % 2
                            a = r // 2
                            src = ag_o_out[hb][r, :, :].rearrange("p (h q) -> p h q", h=nh)
                            src = src[:, :, bass.ds(rank * 256, 256)]
                            dst = orcv_v[:, half * 8 + hb * 6:half * 8 + hb * 6 + nh,
                                         a * 256:a * 256 + 256]
                            s.dma_start(dst, src).then_inc(sLdh[hb].h, 16)
                            sLdh[hb].n += 16
                sLd.n += 16
                ld_all = sLd.n
                lg_all = sLg.n
                p0, v0 = sPe.n, sVe.n
                rounds = [[0, 1, 2, 3, 4, 5, 8, 9, 10, 11, 12, 13], [6, 7, 14, 15]]

                @blk.tensor
                def _(te):
                    te.wait_ge(sLg.h, lg_all)
                    for rd in range(2):
                        te.wait_ge(sLdh[rd].h, sLdh[rd].n)
                        for r in range(DC):
                            it = rd * DC + r
                            if it >= 2:
                                te.wait_ge(sVe.h, v0 + it - 1)
                            for k in rounds[rd]:
                                for sc_ in range(2):
                                    i = te.matmul(psA[it % 2][:, sc_ * 512:sc_ * 512 + 512],
                                                  woutS_v[:, k, r * 128:r * 128 + 128],
                                                  orcv_v[:, k, sc_ * 512:sc_ * 512 + 512],
                                                  start=(k == rounds[rd][0]),
                                                  stop=(k == rounds[rd][-1]))
                            sPe.inc(i)

                @blk.vector
                def _(v):
                    v.wait_ge(sLd.h, ld_all)
                    for rd in range(2):
                        for r in range(DC):
                            it = rd * DC + r
                            v.wait_ge(sPe.h, p0 + it + 1)
                            i = v.tensor_tensor(xres_v[:, r, :], xres_v[:, r, :],
                                                psA[it % 2][:, :], ALU.add)
                            sVe.inc(i)
                            GE.emit(v, 6)

                @blk.vector
                def _(v):
                    GE.flush(v)

                B3q = GE.done_quarters

                @blk.scalar
                def _(sc):
                    if B3q > B2q_g[0]:
                        GE.dma_chain(sc, B2q_g[0], B3q)

                @blk.gpsimd
                def _(g):
                    gu0, gu1 = GE.store_totals[N_GU_Q - 1]
                    g.wait_ge(sGs[0].h, gu0)
                    g.wait_ge(sGs[1].h, gu1)
                    i = g.collective_compute("AllGather", ALU.bypass, RG,
                                             ins=[ag_gu_in], outs=[ag_gu_out])
                    sCc.inc(i)

            cc_gu_at = sCc.n

            with nc.Block() as blk, contextlib.ExitStack() as PS:
                emit_ln(nc, blk, PS, sems, xres_v, Bvec, sqbuf)

            with nc.Block() as blk:
                @blk.sync
                def _(s):
                    s.wait_ge(sVe.h, sVe.n)
                    s.dma_start(x_sp[0], xres[:]).then_inc(sSt.h, 16)
                    sSt.n += 16
                    s.wait_ge(sSt.h, sSt.n)

        SG.close()

        if N_SUB >= 2:
            GB = [(g0, min(g0 + 4, IC)) for g0 in range(0, IC, 4)]
            with contextlib.ExitStack() as SB:
                h_sb = _e(SB, nc.sbuf_tensor("h_sb", [128, IC * TOK], BF16))
                h_v = h_sb[:].rearrange("p (i t) -> p i t", i=IC)

                with contextlib.ExitStack() as PH:
                    x1b = _e(PH, nc.sbuf_tensor("x1b", [128, DC * TOK], BF16))
                    wg = [_e(PH, nc.sbuf_tensor(f"wgs{i}", [128, 16 * 512], BF16)) for i in range(2)]
                    wu = [_e(PH, nc.sbuf_tensor(f"wus{i}", [128, 16 * 512], BF16)) for i in range(2)]
                    x1b_v = x1b[:].rearrange("p (k t) -> p k t", k=DC)
                    wg_v = [w[:].rearrange("p (k c) -> p k c", k=16) for w in wg]
                    wu_v = [w[:].rearrange("p (k c) -> p k c", k=16) for w in wu]
                    with nc.Block() as blk, contextlib.ExitStack() as PS:
                        gps = [_e(PS, nc.psum_tensor(f"gps{i}", [128, 1024], F32)) for i in range(2)]
                        ups = [_e(PS, nc.psum_tensor(f"ups{i}", [128, 1024], F32)) for i in range(2)]

                        @blk.gpsimd
                        def _(g):
                            g.dma_start(x1b_v, x_sp[0].rearrange("p (k t) -> p k t", k=DC)).then_inc(sLg.h, 16)
                            dn0, dn1 = GE.store_totals[-1]
                            g.wait_ge(sGs[0].h, dn0)
                            g.wait_ge(sGs[1].h, dn1)
                            i = g.collective_compute("AllGather", ALU.bypass, RG,
                                                     ins=[ag_dn_in], outs=[ag_dn_out])
                            sCc.inc(i)
                        sLg.n += 16
                        cc_dn_at = sCc.n
                        lg_x1 = sLg.n
                        p0, v0, c0 = sPe.n, sVe.n, sSc.n
                        pe_at = {}
                        psim = p0
                        for gi, (o0, o1) in enumerate(GB):
                            for oc in range(o0, o1):
                                psim += 2
                            pe_at[gi] = psim
                        pe_oc = {}
                        psim = p0
                        for gi, (o0, o1) in enumerate(GB):
                            for oc in range(o0, o1):
                                pe_oc[(oc, "g")] = psim + 1
                                pe_oc[(oc, "u")] = psim + 2
                                psim += 2
                        sc_oc = {}
                        csim = c0
                        for gi, (o0, o1) in enumerate(GB):
                            for oc in range(o0, o1):
                                csim += 1
                                sc_oc[oc] = csim
                        ve_oc = {}
                        vsim = v0
                        for gi, (o0, o1) in enumerate(GB):
                            for oc in range(o0, o1):
                                vsim += 1
                                ve_oc[oc] = vsim

                        @blk.sync
                        def _(s):
                            s.wait_ge(sCc.h, cc_gu_at)
                            for gi, (o0, o1) in enumerate(GB):
                                gsz = o1 - o0
                                if gi >= 2:
                                    s.wait_ge(sPe.h, pe_at[gi - 2])
                                for buf_v, base in ((wg_v[gi % 2], GOFF), (wu_v[gi % 2], UOFF)):
                                    for sub in range(2):
                                        gv = ag_gu_out[:, base + sub * 128 * INTER:
                                                       base + (sub + 1) * 128 * INTER]
                                        gv = gv.rearrange("r (p c) -> r p c", p=128)
                                        gv = gv[:, :, o0 * 128:o0 * 128 + gsz * 128]
                                        s.dma_start(buf_v[:, sub::2, 0:gsz * 128],
                                                    gv.transpose([1, 0, 2])).then_inc(
                                                        sGU[2 * (gi % 2) + sub].h, 16)
                                        sGU[2 * (gi % 2) + sub].n += 16

                        @blk.tensor
                        def _(te):
                            te.wait_ge(sLg.h, lg_x1)
                            te.wait_ge(sGs[0].h, GE.store_totals[-1][0])
                            te.wait_ge(sGs[1].h, GE.store_totals[-1][1])
                            for gi, (o0, o1) in enumerate(GB):
                                gsz = o1 - o0
                                te.wait_ge(sGU[2 * (gi % 2)].h, 32 * (gi // 2 + 1))
                                te.wait_ge(sGU[2 * (gi % 2) + 1].h, 32 * (gi // 2 + 1))
                                for oc in range(o0, o1):
                                    lc = oc - o0
                                    if oc >= 2:
                                        te.wait_ge(sSc.h, sc_oc[oc - 2])
                                    for k in range(DC):
                                        for sc_ in range(2):
                                            i = te.matmul(gps[oc % 2][:, sc_ * 512:sc_ * 512 + 512],
                                                          wg_v[gi % 2][:, k, lc * 128:lc * 128 + 128],
                                                          x1b_v[:, k, sc_ * 512:sc_ * 512 + 512],
                                                          start=(k == 0), stop=(k == DC - 1))
                                    sPe.inc(i)
                                    if oc >= 2:
                                        te.wait_ge(sVe.h, ve_oc[oc - 2])
                                    for k in range(DC):
                                        for sc_ in range(2):
                                            i = te.matmul(ups[oc % 2][:, sc_ * 512:sc_ * 512 + 512],
                                                          wu_v[gi % 2][:, k, lc * 128:lc * 128 + 128],
                                                          x1b_v[:, k, sc_ * 512:sc_ * 512 + 512],
                                                          start=(k == 0), stop=(k == DC - 1))
                                    sPe.inc(i)

                        @blk.scalar
                        def _(sc):
                            for oc in range(IC):
                                sc.wait_ge(sPe.h, pe_oc[(oc, "g")])
                                i = sc.activation(h_v[:, oc, :], gps[oc % 2][:, :], AF.Silu)
                                sSc.inc(i)

                        @blk.vector
                        def _(v):
                            for oc in range(IC):
                                v.wait_ge(sSc.h, sc_oc[oc])
                                v.wait_ge(sPe.h, pe_oc[(oc, "u")])
                                i = v.tensor_tensor(h_v[:, oc, :], h_v[:, oc, :],
                                                    ups[oc % 2][:, :], ALU.mult)
                                sVe.inc(i)

                with contextlib.ExitStack() as PH:
                    xres = _e(PH, nc.sbuf_tensor("xresB", [128, DC * TOK], F32))
                    Bvec = _e(PH, nc.sbuf_tensor("BvecB", [128, 2048], F32))
                    sqbuf = [_e(PH, nc.sbuf_tensor(f"sqbufB{i}", [128, 1024], F32)) for i in range(2)]
                    dsl = [_e(PH, nc.sbuf_tensor(f"dsl{i}", [128, 48 * 128], BF16)) for i in range(2)]
                    xres_v = xres[:].rearrange("p (k t) -> p k t", k=DC)
                    dsl_v = [w[:].rearrange("p (i c) -> p i c", i=48) for w in dsl]
                    with nc.Block() as blk, contextlib.ExitStack() as PS:
                        dps = [_e(PS, nc.psum_tensor(f"dps{i}", [128, 1024], F32)) for i in range(2)]
                        p0, v0 = sPe.n, sVe.n
                        ld0 = sLd.n

                        @blk.sync
                        def _(s):
                            s.dma_start(xres_v, x_sp[0].rearrange("p (k t) -> p k t", k=DC)).then_inc(sLd.h, 16)
                            s.wait_ge(sCc.h, cc_dn_at)
                            for r in range(DC):
                                if r >= 2:
                                    s.wait_ge(sPe.h, p0 + (r - 1))
                                for sub in range(6):
                                    gv = ag_dn_out[:, sub * 128 * D:(sub + 1) * 128 * D]
                                    gv = gv.rearrange("r (p c) -> r p c", p=128)
                                    gv = gv[:, :, r * 128:(r + 1) * 128]
                                    s.dma_start(dsl_v[r % 2][:, sub::6, :],
                                                gv.transpose([1, 0, 2])).then_inc(sWd[r % 2].h, 16)
                                    sWd[r % 2].n += 16
                        sLd.n = ld0 + 16

                        @blk.tensor
                        def _(te):
                            for r in range(DC):
                                te.wait_ge(sWd[r % 2].h, 96 * (r // 2 + 1))
                                if r >= 2:
                                    te.wait_ge(sVe.h, v0 + (r - 1))
                                for i_ in range(IC):
                                    for sc_ in range(2):
                                        i = te.matmul(dps[r % 2][:, sc_ * 512:sc_ * 512 + 512],
                                                      dsl_v[r % 2][:, i_, :],
                                                      h_v[:, i_, sc_ * 512:sc_ * 512 + 512],
                                                      start=(i_ == 0), stop=(i_ == IC - 1))
                                sPe.inc(i)

                        @blk.vector
                        def _(v):
                            for r in range(DC):
                                v.wait_ge(sPe.h, p0 + r + 1)
                                v.wait_ge(sLd.h, ld0 + 16)
                                i = v.tensor_tensor(xres_v[:, r, :], xres_v[:, r, :],
                                                    dps[r % 2][:, :], ALU.add)
                                sVe.inc(i)

                    with nc.Block() as blk, contextlib.ExitStack() as PS:
                        emit_ln(nc, blk, PS, sems, xres_v, Bvec, sqbuf)

                    with nc.Block() as blk:
                        @blk.sync
                        def _(s):
                            s.wait_ge(sVe.h, sVe.n)
                            s.dma_start(x_sp[1], xres[:]).then_inc(sSt.h, 16)
                            sSt.n += 16
                            s.wait_ge(sSt.h, sSt.n)

        if N_SUB >= 3:
            with contextlib.ExitStack() as PH:
                o_ln = _e(PH, nc.sbuf_tensor("o_ln", [128, DC * TOK], F32))
                gmS = _e(PH, nc.sbuf_tensor("gmS", [128, 16], F32))
                btS = _e(PH, nc.sbuf_tensor("btS", [128, 16], F32))
                binT_sb = _e(PH, nc.sbuf_tensor("binT_sb", [128, 48], F32))
                boutT_sb = _e(PH, nc.sbuf_tensor("boutT_sb", [128, 16], F32))
                o_v = o_ln[:].rearrange("p (k t) -> p k t", k=DC)
                PHJ = _e(PH, contextlib.ExitStack())
                x2b = _e(PHJ, nc.sbuf_tensor("x2b", [128, DC * TOK], BF16))
                wfC = [_e(PHJ, nc.sbuf_tensor(f"wfC{i}", [128, D], F32)) for i in range(2)]
                wsl = [_e(PHJ, nc.sbuf_tensor(f"wsl{i}", [128, 16 * 384], BF16)) for i in range(2)]
                ftile = [[_e(PHJ, nc.sbuf_tensor(f"ft{s}_{i}", [128, 1024], F32)) for i in range(2)]
                         for s in range(3)]
                u2 = _e(PHJ, nc.sbuf_tensor("u2", [128, 2048], F32))
                lamf2 = _e(PHJ, nc.sbuf_tensor("lamf2", [128, 2048], F32))
                og2 = _e(PHJ, nc.sbuf_tensor("og2", [128, 2048], F32))
                ssum = _e(PHJ, nc.sbuf_tensor("ssum", [128, 2048], F32))
                rtmp = _e(PHJ, nc.sbuf_tensor("rtmp", [128, 512], F32))
                rst = _e(PHJ, nc.sbuf_tensor("rst", [128, 512], F32))
                x2b_v = x2b[:].rearrange("p (k t) -> p k t", k=DC)
                wsl_v = [w[:].rearrange("p (k c) -> p k c", k=16) for w in wsl]
                u2_v = u2[:].rearrange("p (m t) -> p m t", m=2)
                lam_v = lamf2[:].rearrange("p (m t) -> p m t", m=2)
                og_v = og2[:].rearrange("p (m t) -> p m t", m=2)
                ss_v = ssum[:].rearrange("p (m t) -> p m t", m=2)
                rt_v = rtmp[:].rearrange("p (m t) -> p m t", m=2)
                rs_v = rst[:].rearrange("p (m t) -> p m t", m=2)

                with nc.Block() as blk, contextlib.ExitStack() as PS:
                    fps = _e(PS, nc.psum_tensor("fps", [128, 1024], F32))
                    rps = [_e(PS, nc.psum_tensor(f"rps{i}", [128, 1024], F32)) for i in range(2)]
                    eps = _e(PS, nc.psum_tensor("eps", [128, 1024], F32))
                    lg0 = sLg.n
                    ld0 = sLd.n

                    @blk.gpsimd
                    def _(g):
                        g.dma_start(x2b_v, x_sp[1].rearrange("p (k t) -> p k t", k=DC)).then_inc(sLg.h, 16)
                    sLg.n += 16

                    c0abs = sSc.n

                    @blk.sync
                    def _(s):
                        s.dma_start(gmS[:], gammaT).then_inc(sLd.h, 16)
                        s.dma_start(btS[:], betaT).then_inc(sLd.h, 16)
                        s.dma_start(binT_sb[:], binT).then_inc(sLd.h, 16)
                        s.dma_start(boutT_sb[:], boutT).then_inc(sLd.h, 16)
                        for k in range(DC):
                            if k >= 2:
                                s.wait_ge(sSc.h, c0abs + 4 * (k - 1))
                            gv = wfT[:, :].rearrange("(k p) c -> p k c", p=128)[:, k, :]
                            s.dma_start(wfC[k % 2][:, :], gv).then_inc(sWf[k % 2].h, 16)
                            sWf[k % 2].n += 16
                    sLd.n += 64
                    ldc_all = sLd.n

                    p0, v0, c0 = sPe.n, sVe.n, sSc.n
                    pe_feat, pe_rem, pe_ein = {}, {}, {}
                    psim = p0
                    for j in range(DC):
                        for seg in range(3):
                            psim += 1
                            pe_feat[(j, seg)] = psim
                        if j >= 1:
                            psim += 1
                            pe_ein[j - 1] = psim
                        for r in range(6):
                            psim += 1
                            pe_rem[(j, r)] = psim
                    psim += 1
                    pe_ein[DC - 1] = psim
                    sc_feat = {}
                    csim = c0
                    for j in range(DC):
                        for seg in range(3):
                            csim += 1
                            sc_feat[(j, seg)] = csim
                        csim += 1
                    ve_rem, ve_t, ve_oev = {}, {}, {}
                    vsim = v0
                    for j in range(DC):
                        for r in range(6):
                            vsim += 1
                            ve_rem[(j, r)] = vsim
                        vsim += 1
                        ve_t[j] = vsim
                        vsim += 1
                        ve_oev[j] = vsim

                    @blk.gpsimd
                    def _(g):
                        for j in range(DC):
                            if j >= 2:
                                g.wait_ge(sPe.h, pe_feat[(j - 2, 2)])
                            gv = winTre[:, j * 384:(j + 1) * 384]
                            g.dma_start(wsl_v[j % 2],
                                        gv.rearrange("(k p) c -> p k c", p=128)).then_inc(sWc[j % 2].h, 16)
                            sWc[j % 2].n += 16

                    @blk.tensor
                    def _(te):
                        te.wait_ge(sLg.h, lg0 + 16)
                        for j in range(DC):
                            te.wait_ge(sWc[j % 2].h, 16 * (j // 2 + 1))
                            for seg in range(3):
                                fcnt = 3 * j + seg
                                if fcnt >= 1:
                                    te.wait_ge(sSc.h, sc_feat[((fcnt - 1) // 3, (fcnt - 1) % 3)])
                                for k in range(DC):
                                    for sc_ in range(2):
                                        i = te.matmul(fps[:, sc_ * 512:sc_ * 512 + 512],
                                                      wsl_v[j % 2][:, k, seg * 128:seg * 128 + 128],
                                                      x2b_v[:, k, sc_ * 512:sc_ * 512 + 512],
                                                      start=(k == 0), stop=(k == DC - 1))
                                sPe.inc(i)
                            if j >= 1:
                                te.wait_ge(sVe.h, ve_t[j - 1])
                                if j >= 2:
                                    te.wait_ge(sVe.h, ve_oev[j - 2])
                                for m in range(2):
                                    for sc_ in range(2):
                                        i = te.matmul(eps[:, sc_ * 512:sc_ * 512 + 512],
                                                      DTC[m][:, :],
                                                      u2_v[:, m, sc_ * 512:sc_ * 512 + 512],
                                                      start=(m == 0), stop=(m == 1))
                                sPe.inc(i)
                            rsrc = [(DTA, 2), (DTB, 0), (DTA, 1)]
                            rcnt = 6 * j
                            for which, (Dm, ft) in enumerate(rsrc):
                                te.wait_ge(sSc.h, sc_feat[(j, ft)])
                                for m in range(2):
                                    r = which * 2 + m
                                    if rcnt + r >= 2:
                                        pj, pr = divmod(rcnt + r - 2, 6)
                                        te.wait_ge(sVe.h, ve_rem[(pj, pr)])
                                    for sc_ in range(2):
                                        i = te.matmul(rps[(rcnt + r) % 2][:, sc_ * 512:sc_ * 512 + 512],
                                                      Dm[m][:, :],
                                                      ftile[ft][j % 2][:, sc_ * 512:sc_ * 512 + 512],
                                                      start=True, stop=True)
                                    sPe.inc(i)
                        te.wait_ge(sVe.h, ve_t[DC - 1])
                        te.wait_ge(sVe.h, ve_oev[DC - 2])
                        for m in range(2):
                            for sc_ in range(2):
                                i = te.matmul(eps[:, sc_ * 512:sc_ * 512 + 512],
                                              DTC[m][:, :],
                                              u2_v[:, m, sc_ * 512:sc_ * 512 + 512],
                                              start=(m == 0), stop=(m == 1))
                        sPe.inc(i)

                    @blk.scalar
                    def _(sc):
                        funcs = [AF.Silu, AF.Sigmoid, AF.Sigmoid]
                        sc.wait_ge(sLd.h, ldc_all)
                        for j in range(DC):
                            for seg in range(3):
                                sc.wait_ge(sPe.h, pe_feat[(j, seg)])
                                if j >= 2:
                                    sc.wait_ge(sPe.h, pe_rem[(j - 2, 5)])
                                i = sc.activation(ftile[seg][j % 2][:, :], fps[:, :],
                                                  funcs[seg], bias=binT_sb[:, 3 * j + seg:3 * j + seg + 1])
                                sSc.inc(i)
                            sc.wait_ge(sWf[j % 2].h, 16 * (j // 2 + 1))
                            sc.drain()
                            i = sc.activation(wfC[j % 2][:, :], wfC[j % 2][:, :], AF.Abs,
                                              accum_out=acols[:, j:j + 1])
                            sSc.inc(i)

                    @blk.vector
                    def _(v):
                        for j in range(DC):
                            for r in range(6):
                                which, m = divmod(r, 2)
                                v.wait_ge(sPe.h, pe_rem[(j, r)])
                                v.drain()
                                if which == 0:
                                    i = v.tensor_copy(lam_v[:, m, :], rps[(6 * j + r) % 2][:, :])
                                elif which == 1:
                                    i = v.scalar_tensor_tensor(u2_v[:, m, :], lam_v[:, m, :], 1.0,
                                                               rps[(6 * j + r) % 2][:, :],
                                                               ALU.subtract, ALU.mult)
                                else:
                                    i = v.tensor_copy(og_v[:, m, :], rps[(6 * j + r) % 2][:, :])
                                sVe.inc(i)
                            v.drain()
                            v.tensor_copy(ss_v[:, :, 0:256], u2_v[:, :, 0:256])
                            for a in range(1, 4):
                                v.drain()
                                v.scalar_tensor_tensor(rt_v[:, :, 0:256], lam_v[:, :, a * 256:a * 256 + 256],
                                                       0.0, ss_v[:, :, (a - 1) * 256:a * 256],
                                                       ALU.add, ALU.mult)
                                v.drain()
                                v.tensor_tensor(ss_v[:, :, a * 256:a * 256 + 256], rt_v[:, :, 0:256],
                                                u2_v[:, :, a * 256:a * 256 + 256], ALU.add)
                            v.drain()
                            v.tensor_tensor(ss_v[:, :, 768:1024], ss_v[:, :, 768:1024],
                                            u2_v[:, :, 768:1024], ALU.add)
                            v.drain()
                            v.tensor_copy(rs_v[:, :, 0:256], u2_v[:, :, 768:1024])
                            for a in (2, 1, 0):
                                v.drain()
                                v.scalar_tensor_tensor(rt_v[:, :, 0:256], lam_v[:, :, a * 256:a * 256 + 256],
                                                       0.0, rs_v[:, :, 0:256], ALU.add, ALU.mult)
                                v.drain()
                                v.tensor_tensor(rs_v[:, :, 0:256], rt_v[:, :, 0:256],
                                                u2_v[:, :, a * 256:a * 256 + 256], ALU.add)
                                v.drain()
                                v.tensor_tensor(ss_v[:, :, a * 256:a * 256 + 256],
                                                ss_v[:, :, a * 256:a * 256 + 256],
                                                rs_v[:, :, 0:256], ALU.add)
                            for m in range(2):
                                v.drain()
                                i = v.tensor_tensor(u2_v[:, m, :], ss_v[:, m, :], og_v[:, m, :], ALU.mult)
                            sVe.inc(i)
                            v.wait_ge(sPe.h, pe_ein[j])
                            v.drain()
                            i = v.tensor_copy(o_v[:, j, :], eps[:, :])
                            sVe.inc(i)

                PHJ.close()

                with contextlib.ExitStack() as PH2:
                    Bvec = _e(PH2, nc.sbuf_tensor("BvecC", [128, 2048], F32))
                    sqbuf = [_e(PH2, nc.sbuf_tensor(f"sqbufC{i}", [128, 1024], F32)) for i in range(2)]
                    o_b = _e(PH2, nc.sbuf_tensor("o_b", [128, DC * TOK], BF16))
                    o_b_v = o_b[:].rearrange("p (k t) -> p k t", k=DC)
                    with nc.Block() as blk, contextlib.ExitStack() as PS:
                        emit_ln(nc, blk, PS, sems, o_v, Bvec, sqbuf, gamma=gmS, beta=btS, out_bf=o_b_v)

                    with contextlib.ExitStack() as PH3:
                        xres = _e(PH3, nc.sbuf_tensor("xresC", [128, DC * TOK], F32))
                        wo = [_e(PH3, nc.sbuf_tensor("wo0", [128, 16 * 512], BF16))]
                        xres_v = xres[:].rearrange("p (k t) -> p k t", k=DC)
                        wo_v = [w[:].rearrange("p (k c) -> p k c", k=16) for w in wo]
                        RGC = [(r0, min(r0 + 4, DC)) for r0 in range(0, DC, 4)]
                        with nc.Block() as blk, contextlib.ExitStack() as PS:
                            psA = [_e(PS, nc.psum_tensor(f"psW{i}", [128, 1024], F32)) for i in range(2)]
                            ld0 = sLd.n
                            lg0 = sLg.n
                            p0, v0 = sPe.n, sVe.n
                            pe_r = {}
                            psim = p0
                            for gi, (r0, r1) in enumerate(RGC):
                                for r in range(r0, r1):
                                    psim += 1
                                    pe_r[r] = psim
                            pe_g = {gi: pe_r[r1 - 1] for gi, (r0, r1) in enumerate(RGC)}

                            @blk.sync
                            def _(s):
                                s.dma_start(xres_v, x_sp[1].rearrange("p (k t) -> p k t", k=DC)).then_inc(sLd.h, 16)
                            sLd.n += 16

                            @blk.gpsimd
                            def _(g):
                                for gi, (r0, r1) in enumerate(RGC):
                                    if gi >= 1:
                                        g.wait_ge(sPe.h, pe_g[gi - 1])
                                    gv = wout1T[:, r0 * 128:r1 * 128]
                                    g.dma_start(wo_v[0][:, :, 0:(r1 - r0) * 128],
                                                gv.rearrange("(k p) c -> p k c", p=128)).then_inc(sWo.h, 16)
                                    sWo.n += 16

                            @blk.tensor
                            def _(te):
                                for gi, (r0, r1) in enumerate(RGC):
                                    te.wait_ge(sWo.h, 16 * (gi + 1))
                                    for r in range(r0, r1):
                                        lc = r - r0
                                        if r >= 2:
                                            te.wait_ge(sVe.h, v0 + r - 1)
                                        for k in range(DC):
                                            for sc_ in range(2):
                                                i = te.matmul(psA[r % 2][:, sc_ * 512:sc_ * 512 + 512],
                                                              wo_v[0][:, k, lc * 128:lc * 128 + 128],
                                                              o_b_v[:, k, sc_ * 512:sc_ * 512 + 512],
                                                              start=(k == 0), stop=(k == DC - 1))
                                        sPe.inc(i)

                            @blk.vector
                            def _(v):
                                v.wait_ge(sLd.h, ld0 + 16)
                                for r in range(DC):
                                    v.wait_ge(sPe.h, pe_r[r])
                                    i = v.scalar_tensor_tensor(xres_v[:, r, :], psA[r % 2][:, :],
                                                               boutT_sb[:, r:r + 1], xres_v[:, r, :],
                                                               ALU.add, ALU.add)
                                    sVe.inc(i)

                        with nc.Block() as blk, contextlib.ExitStack() as PS:
                            emit_ln(nc, blk, PS, sems, xres_v, Bvec, sqbuf)

                        with nc.Block() as blk:
                            @blk.sync
                            def _(s):
                                s.wait_ge(sVe.h, sVe.n)
                                s.dma_start(x_sp[0], xres[:]).then_inc(sSt.h, 16)
                                sSt.n += 16
                                s.wait_ge(sSt.h, sSt.n)

        DTS.close()

        if N_SUB >= 4:
            with contextlib.ExitStack() as PH:
                xres = _e(PH, nc.sbuf_tensor("xresD", [128, DC * TOK], F32))
                cT = _e(PH, nc.sbuf_tensor("cT", [128, DC * D], BF16))
                Kq = _e(PH, nc.sbuf_tensor("Kq", [128, DC * TOK], BF16))
                Bvec = _e(PH, nc.sbuf_tensor("BvecD", [128, 2048], F32))
                sqbuf = [_e(PH, nc.sbuf_tensor(f"sqbufD{i}", [128, 1024], F32)) for i in range(2)]
                biasT_sb = _e(PH, nc.sbuf_tensor("biasT_sb", [128, 16], F32))
                ri32 = _e(PH, nc.sbuf_tensor("ri32", [128, 1024], I32))
                rf32 = _e(PH, nc.sbuf_tensor("rf32", [128, 1024], F32))
                qf32 = _e(PH, nc.sbuf_tensor("qf32", [128, 1024], F32))
                wf = [_e(PH, nc.sbuf_tensor(f"wf{i}", [128, D], F32)) for i in range(2)]
                xres_v = xres[:].rearrange("p (k t) -> p k t", k=DC)
                cT_v = cT[:].rearrange("p (k c) -> p k c", k=DC)
                Kq_v = Kq[:].rearrange("p (k t) -> p k t", k=DC)
                amaxrow = Bvec[0:1, 0:1024]
                a_col = acols[:, 16:17]
                alpha_col = acols[:, 17:18]
                half_col = acols[:, 18:19]
                nhalf_col = acols[:, 19:20]

                with nc.Block() as blk, contextlib.ExitStack() as PS:
                    ps_t = _e(PS, nc.psum_tensor("ps_t", [128, 1024], F32))
                    ps_b = _e(PS, nc.psum_tensor("ps_b", [128, 1024], F32))
                    psA = [_e(PS, nc.psum_tensor(f"psD{i}", [128, 1024], F32)) for i in range(2)]
                    p0, v0, c0 = sPe.n, sVe.n, sSc.n
                    ld0 = sLd.n
                    wf0 = (sWf[0].n, sWf[1].n)
                    V_alpha = v0 + 3
                    V_quant = {k: v0 + 3 + 2 * k + 1 for k in range(DC)}
                    V_fold = {k: v0 + 3 + 2 * k + 2 for k in range(DC)}
                    V_red = {b: v0 + 35 + b + 1 for b in range(8)}
                    V_cp = {b: v0 + 43 + b + 1 for b in range(8)}
                    V_scale = v0 + 52
                    V_K = {k: v0 + 52 + k + 1 for k in range(DC)}
                    V_evac = {r: v0 + 68 + r + 1 for r in range(DC)}
                    P_a1, P_a2 = p0 + 1, p0 + 2
                    P_tr = {b: p0 + 2 + b + 1 for b in range(16)}
                    P_bc = p0 + 19
                    P_r = {r: p0 + 19 + r + 1 for r in range(DC)}

                    @blk.sync
                    def _(s):
                        s.dma_start(biasT_sb[:], biasT).then_inc(sLd.h, 16)
                        s.dma_start(xres_v, x_sp[0].rearrange("p (k t) -> p k t", k=DC)).then_inc(sLd.h, 16)
                        for k in range(DC):
                            if k >= 2:
                                s.wait_ge(sVe.h, V_quant[k - 2])
                            gv = wfT[:, :].rearrange("(k p) c -> p k c", p=128)[:, k, :]
                            s.dma_start(wf[k % 2][:, :], gv).then_inc(sWf[k % 2].h, 16)
                            sWf[k % 2].n += 16
                    sLd.n += 32

                    @blk.scalar
                    def _(sc):
                        sc.wait_ge(sLd.h, ld0 + 32)
                        for k in range(DC):
                            if k >= 2:
                                sc.wait_ge(sVe.h, V_fold[k - 2])
                            sc.drain()
                            i = sc.activation(qf32[:, :] if k % 2 == 0 else rf32[:, :],
                                              xres_v[:, k, :], AF.Abs)
                            sSc.inc(i)

                    @blk.tensor
                    def _(te):
                        te.wait_ge(sVe.h, v0 + 1)
                        i = te.matmul(ps_b[0:1, 0:1], onesC[:, :], a_col, start=True, stop=True)
                        sPe.inc(i)
                        te.wait_ge(sVe.h, v0 + 2)
                        i = te.matmul(ps_b[:, 32:33], onesR[:, :], acols[0:1, 20:21],
                                      start=True, stop=True)
                        sPe.inc(i)
                        te.wait_ge(sVe.h, V_fold[DC - 1])
                        for b in range(8):
                            if b >= 2:
                                te.wait_ge(sVe.h, V_red[b - 2])
                            i = te.transpose(ps_t[:, (b % 2) * 512:(b % 2) * 512 + 128],
                                             sqbuf[0][:, b * 128:b * 128 + 128], identF[:, :])
                            sPe.inc(i)
                        for b in range(8):
                            te.wait_ge(sVe.h, V_red[7])
                            if b >= 2:
                                te.wait_ge(sVe.h, V_cp[b - 2])
                            i = te.transpose(ps_t[0:1, (b % 2) * 512 + 256:(b % 2) * 512 + 384],
                                             acols[:, 24 + b:25 + b], identF[:, :])
                            sPe.inc(i)
                        te.wait_ge(sVe.h, V_cp[7])
                        te.matmul(ps_b[:, 0:512], onesR[:, :], amaxrow[0:1, 0:512],
                                  start=True, stop=True)
                        i = te.matmul(ps_b[:, 512:1024], onesR[:, :], amaxrow[0:1, 512:1024],
                                      start=True, stop=True)
                        sPe.inc(i)
                        for r in range(DC):
                            if r >= 2:
                                te.wait_ge(sVe.h, V_evac[r - 2])
                            for k in range(DC):
                                if r == 0:
                                    te.wait_ge(sVe.h, V_K[k])
                                for sc_ in range(2):
                                    i = te.matmul(psA[r % 2][:, sc_ * 512:sc_ * 512 + 512],
                                                  cT_v[:, k, r * 128:r * 128 + 128],
                                                  Kq_v[:, k, sc_ * 512:sc_ * 512 + 512],
                                                  start=(k == 0), stop=(k == DC - 1))
                            sPe.inc(i)

                    @blk.vector
                    def _(v):
                        i = v.tensor_reduce(a_col, acols[:, 0:16], AX.X, ALU.add)
                        sVe.inc(i)
                        v.wait_ge(sPe.h, P_a1)
                        v.drain()
                        i = v.tensor_copy(acols[0:1, 20:21], ps_b[0:1, 0:1])
                        sVe.inc(i)
                        v.wait_ge(sPe.h, P_a2)
                        v.drain()
                        v.tensor_copy(alpha_col, ps_b[:, 32:33])
                        v.drain()
                        v.tensor_scalar(alpha_col, alpha_col, 1.0 / float(D * D), EPS,
                                        ALU.mult, ALU.max)
                        v.drain()
                        v.tensor_scalar(half_col, alpha_col, 0.5, None, ALU.mult)
                        i = v.tensor_scalar(nhalf_col, alpha_col, -0.5, None, ALU.mult)
                        sVe.inc(i)
                        for k in range(DC):
                            v.wait_ge(sWf[k % 2].h, wf0[k % 2] + 16 * (k // 2 + 1))
                            v.drain()
                            v.tensor_scalar(sqbuf[1][:, :], wf[k % 2][:, 0:1024],
                                            nhalf_col, None, ALU.is_le)
                            v.drain()
                            v.scalar_tensor_tensor(cT_v[:, k, 0:1024], wf[k % 2][:, 0:1024],
                                                   half_col, sqbuf[1][:, :], ALU.is_ge, ALU.subtract)
                            v.drain()
                            v.tensor_scalar(sqbuf[1][:, :], wf[k % 2][:, 1024:2048],
                                            nhalf_col, None, ALU.is_le)
                            v.drain()
                            i = v.scalar_tensor_tensor(cT_v[:, k, 1024:2048], wf[k % 2][:, 1024:2048],
                                                       half_col, sqbuf[1][:, :], ALU.is_ge, ALU.subtract)
                            sVe.inc(i)
                            v.wait_ge(sSc.h, c0 + k + 1)
                            v.drain()
                            if k == 0:
                                i = v.tensor_copy(sqbuf[0][:, :], qf32[:, :])
                            else:
                                i = v.tensor_tensor(sqbuf[0][:, :], sqbuf[0][:, :],
                                                    qf32[:, :] if k % 2 == 0 else rf32[:, :], ALU.max)
                            sVe.inc(i)
                        for b in range(8):
                            v.wait_ge(sPe.h, P_tr[b])
                            v.drain()
                            i = v.tensor_reduce(acols[:, 24 + b:25 + b],
                                                ps_t[:, (b % 2) * 512:(b % 2) * 512 + 128],
                                                AX.X, ALU.max)
                            sVe.inc(i)
                        for b in range(8):
                            v.wait_ge(sPe.h, P_tr[8 + b])
                            v.drain()
                            i = v.tensor_copy(amaxrow[0:1, b * 128:b * 128 + 128],
                                              ps_t[0:1, (b % 2) * 512 + 256:(b % 2) * 512 + 384])
                            sVe.inc(i)
                        v.wait_ge(sPe.h, P_bc)
                        v.drain()
                        v.tensor_copy(sqbuf[1][:, :], ps_b[:, 0:1024])
                        v.drain()
                        v.tensor_scalar(Bvec[:, 1024:2048], sqbuf[1][:, :], EPS, 1.0 / 127.0,
                                        ALU.max, ALU.mult)
                        v.drain()
                        v.reciprocal(Bvec[:, 0:1024], Bvec[:, 1024:2048])
                        v.drain()
                        i = v.tensor_scalar(Bvec[:, 1024:2048], Bvec[:, 1024:2048], alpha_col,
                                            None, ALU.mult)
                        sVe.inc(i)
                        for k in range(DC):
                            v.drain()
                            v.tensor_tensor(qf32[:, :], xres_v[:, k, :], Bvec[:, 0:1024], ALU.mult)
                            v.drain()
                            v.tensor_copy(ri32[:, :], qf32[:, :])
                            v.drain()
                            v.tensor_copy(rf32[:, :], ri32[:, :])
                            v.drain()
                            v.tensor_tensor(qf32[:, :], qf32[:, :], rf32[:, :], ALU.subtract)
                            v.drain()
                            v.tensor_scalar(sqbuf[0][:, :], qf32[:, :], 0.5, None, ALU.is_ge)
                            v.drain()
                            v.scalar_tensor_tensor(sqbuf[1][:, :], qf32[:, :], -0.5, sqbuf[0][:, :],
                                                   ALU.is_le, ALU.subtract)
                            v.drain()
                            i = v.tensor_tensor(Kq_v[:, k, :], rf32[:, :], sqbuf[1][:, :], ALU.subtract)
                            sVe.inc(i)
                        for r in range(DC):
                            v.wait_ge(sPe.h, P_r[r])
                            v.drain()
                            v.tensor_tensor(qf32[:, :], psA[r % 2][:, :], Bvec[:, 1024:2048], ALU.mult)
                            v.drain()
                            i = v.scalar_tensor_tensor(xres_v[:, r, :], qf32[:, :],
                                                       biasT_sb[:, r:r + 1], xres_v[:, r, :],
                                                       ALU.add, ALU.add)
                            sVe.inc(i)

                with nc.Block() as blk, contextlib.ExitStack() as PS:
                    emit_ln(nc, blk, PS, sems, xres_v, Bvec, sqbuf)

                with nc.Block() as blk:
                    @blk.sync
                    def _(s):
                        s.wait_ge(sVe.h, sVe.n)
                        s.dma_start(x_sp[1], xres[:]).then_inc(sSt.h, 16)
                        sSt.n += 16
                        s.wait_ge(sSt.h, sSt.n)

        last = x_sp[0] if N_SUB == 1 else (x_sp[1] if N_SUB == 2 else x_sp[(N_SUB - 1) % 2])
        with nc.Block() as blk:
            @blk.sync
            def _(s):
                s.dma_start(out.transpose([1, 0, 2]),
                            last.rearrange("p (k t) -> p k t", k=DC)).then_inc(sSt.h, 16)
                sSt.n += 16
                s.wait_ge(sSt.h, sSt.n)

    return nc


_NC_CACHE = None


def kernel(**inputs):
    global _NC_CACHE
    maps = host_prep(inputs)
    if _NC_CACHE is None:
        _NC_CACHE = build_nc()
    nc = _NC_CACHE
    res = run_bass_kernel_spmd(nc, maps, core_ids=list(range(NCORES)))
    full = np.empty((NA, S, D), np.float32)
    for c in range(NCORES):
        shard = np.asarray(res.results[c]["out"], np.float32).reshape(D, NA, 256)
        full[:, 256 * c:256 * c + 256, :] = shard.transpose(1, 2, 0)
    return full


# version 10 (final)
# speedup vs baseline: 1.1113x; 1.0060x over previous
import contextlib
import math
import numpy as np
import concourse.bass as bass
import concourse.mybir as mybir
from concourse.bass_utils import run_bass_kernel_spmd

F32 = mybir.dt.float32
BF16 = mybir.dt.bfloat16
I32 = mybir.dt.int32
AF = mybir.ActivationFunctionType
ALU = mybir.AluOpType
AX = mybir.AxisListType

NCORES = 8
NA = 4
S = 2048
D = 2048
DC = 16
TOK = 1024
NH = 8
CQ = 1536
CKV = 512
INTER = 5504
IC = 43
ICP = 48
SCALE = 1.0 / math.sqrt(128.0)
NEG = -1.0e30
EPS = 1e-5
FEAT = 6144
N_SUB = 4


class Sem:
    def __init__(self, h):
        self.h = h
        self.n = 0

    def inc(self, instr, k=1):
        instr.then_inc(self.h, k)
        self.n += k
        return self.n



def host_prep(inp):
    C = np.ascontiguousarray
    x = np.asarray(inp["L0_attn_x"], np.float32)
    wdqT = C(np.asarray(inp["L0_attn_w_dq"], np.float32).T)
    wuqT = C(np.asarray(inp["L0_attn_w_uq"], np.float32).T)
    wdkvT = C(np.asarray(inp["L0_attn_w_dkv"], np.float32).T)
    wukvT = C(np.asarray(inp["L0_attn_w_ukv"], np.float32).T)
    woutT = C(np.asarray(inp["L0_attn_w_out"], np.float32).T)
    lut = C(np.asarray(inp["L0_ffn_lut"], np.float32).reshape(1, 16))
    gidxT = C(np.asarray(inp["L0_ffn_gate_idx"], np.int32).T)
    uidxT = C(np.asarray(inp["L0_ffn_up_idx"], np.int32).T)
    didxT = np.asarray(inp["L0_ffn_down_idx"], np.int32).T
    didxT = C(np.concatenate([didxT, np.zeros((ICP * 128 - INTER, D), np.int32)], axis=0))

    winT = C(np.asarray(inp["L1_attn_W_in"], np.float32).T)
    wre = np.empty((D, FEAT), np.float32)
    for j in range(16):
        wre[:, j * 384:j * 384 + 128] = winT[:, j * 128:(j + 1) * 128]
        wre[:, j * 384 + 128:j * 384 + 256] = winT[:, 2048 + j * 128:2048 + (j + 1) * 128]
        wre[:, j * 384 + 256:j * 384 + 384] = winT[:, 4096 + j * 128:4096 + (j + 1) * 128]
    b_in = np.asarray(inp["L1_attn_b_in"], np.float32)
    binT = np.empty((128, 48), np.float32)
    for j in range(16):
        binT[:, 3 * j] = b_in[j * 128:(j + 1) * 128]
        binT[:, 3 * j + 1] = b_in[2048 + j * 128:2048 + (j + 1) * 128]
        binT[:, 3 * j + 2] = b_in[4096 + j * 128:4096 + (j + 1) * 128]
    gammaT = C(np.asarray(inp["L1_attn_gamma"], np.float32).reshape(16, 128).T)
    betaT = C(np.asarray(inp["L1_attn_beta"], np.float32).reshape(16, 128).T)
    wout1T = C(np.asarray(inp["L1_attn_W_out"], np.float32).T)
    boutT = C(np.asarray(inp["L1_attn_b_out"], np.float32).reshape(16, 128).T)
    wfT = C(np.asarray(inp["L1_ffn_weight"], np.float32).T)
    biasT = C(np.asarray(inp["L1_ffn_bias"], np.float32).reshape(16, 128).T)

    maps = []
    for c in range(NCORES):
        b = c // 2
        half = c % 2
        xslice = x[:, 256 * c:256 * c + 256, :]
        xsliceT = C(xslice.transpose(2, 0, 1).reshape(D, TOK))
        maps.append({
            "xTb": C(x[b].T),
            "xsliceT": xsliceT,
            "wdqT": wdqT,
            "wdkvT": wdkvT,
            "wuqT": C(wuqT[:, 1024 * half:1024 * half + 1024]),
            "wukvTk": C(wukvT[:, 1024 * half:1024 * half + 1024]),
            "wukvTv": C(wukvT[:, 2048 + 1024 * half:2048 + 1024 * half + 1024]),
            "woutT": woutT,
            "lutrow": lut,
            "gidxT": C(gidxT[256 * c:256 * c + 256]).reshape(2, 128, INTER),
            "uidxT": C(uidxT[256 * c:256 * c + 256]).reshape(2, 128, INTER),
            "didxT": C(didxT[768 * c:768 * c + 768]).reshape(6, 128, D),
            "winTre": wre,
            "binT": binT,
            "gammaT": gammaT,
            "betaT": betaT,
            "wout1T": wout1T,
            "boutT": boutT,
            "wfT": wfT,
            "biasT": biasT,
        })
    return maps



def emit_ln(nc, blk, PS, sems, xres_v, Bvec, sqbuf, gamma=None, beta=None, out_bf=None,
            spill_to=None, spill_sem=None):
    r_mu = sqbuf[0][0:1, :]
    r_var = sqbuf[1][0:1, :]
    r_t = Bvec[0:1, 0:1024]
    r_inv = Bvec[0:1, 1024:2048]
    sPe, sVe, sSc = sems["pe"], sems["ve"], sems["sc"]
    ps_ln = _e(PS, nc.psum_tensor("ps_ln_%d" % sPe.n, [128, 2048], F32))
    ps_bc = _e(PS, nc.psum_tensor("ps_bc_%d" % sPe.n, [128, 2048], F32))
    pe0, sc0, ve0 = sPe.n, sSc.n, sVe.n
    rD = 1.0 / float(D)

    @blk.scalar
    def _(sc):
        for k in range(DC):
            if k >= 2:
                sc.wait_ge(sPe.h, pe0 + k - 1)
            i = sc.activation(sqbuf[k % 2][:, :], xres_v[:, k, :], AF.Square)
            sSc.inc(i)

    @blk.tensor
    def _(te):
        for k in range(DC):
            te.wait_ge(sSc.h, sc0 + k + 1)
            for sc_ in range(2):
                te.matmul(ps_ln[0:1, sc_ * 512:sc_ * 512 + 512], onesC_g[:, :],
                          xres_v[:, k, sc_ * 512:sc_ * 512 + 512],
                          start=(k == 0), stop=(k == DC - 1))
                i = te.matmul(ps_ln[0:1, 1024 + sc_ * 512:1024 + sc_ * 512 + 512], onesC_g[:, :],
                              sqbuf[k % 2][:, sc_ * 512:sc_ * 512 + 512],
                              start=(k == 0), stop=(k == DC - 1))
            sPe.inc(i)

    @blk.vector
    def _(v):
        v.wait_ge(sPe.h, pe0 + DC)
        v.tensor_scalar(r_mu, ps_ln[0:1, 0:1024], rD, None, ALU.mult)
        v.tensor_scalar(r_var, ps_ln[0:1, 1024:2048], rD, None, ALU.mult)
        v.drain()
        v.tensor_tensor(r_t, r_mu, r_mu, ALU.mult)
        v.drain()
        v.tensor_tensor(r_var, r_var, r_t, ALU.subtract)
        v.drain()
        i = v.tensor_scalar(r_var, r_var, EPS, None, ALU.add)
        sVe.inc(i)

    @blk.scalar
    def _(sc):
        sc.wait_ge(sVe.h, ve0 + 1)
        i = sc.activation(r_t, r_var, AF.Sqrt)
        sSc.inc(i)

    @blk.vector
    def _(v):
        v.wait_ge(sSc.h, sc0 + DC + 1)
        v.reciprocal(r_inv, r_t)
        v.drain()
        v.tensor_tensor(r_t, r_inv, r_inv, ALU.mult)
        v.drain()
        v.tensor_tensor(r_t, r_t, r_var, ALU.mult)
        v.drain()
        v.tensor_scalar(r_t, r_t, -0.5, 1.5, ALU.mult, ALU.add)
        v.drain()
        i = v.tensor_tensor(r_inv, r_inv, r_t, ALU.mult)
        sVe.inc(i)

    @blk.tensor
    def _(te):
        te.wait_ge(sVe.h, ve0 + 2)
        te.matmul(ps_bc[:, 0:512], onesR_g[:, :], r_mu[:, 0:512], start=True, stop=True)
        te.matmul(ps_bc[:, 512:1024], onesR_g[:, :], r_mu[:, 512:1024], start=True, stop=True)
        te.matmul(ps_bc[:, 1024:1536], onesR_g[:, :], r_inv[:, 0:512], start=True, stop=True)
        i = te.matmul(ps_bc[:, 1536:2048], onesR_g[:, :], r_inv[:, 512:1024], start=True, stop=True)
        sPe.inc(i)

    Vn = {}

    @blk.vector
    def _(v):
        v.wait_ge(sPe.h, pe0 + DC + 1)
        i = v.tensor_copy(Bvec[:, 0:2048], ps_bc[:, :])
        sVe.inc(i)
        for k in range(DC):
            v.drain()
            v.tensor_tensor(sqbuf[k % 2][:, :], xres_v[:, k, :], Bvec[:, 0:1024], ALU.subtract)
            v.drain()
            if gamma is None:
                i = v.tensor_tensor(xres_v[:, k, :], sqbuf[k % 2][:, :], Bvec[:, 1024:2048], ALU.mult)
            else:
                v.tensor_tensor(sqbuf[k % 2][:, :], sqbuf[k % 2][:, :], Bvec[:, 1024:2048], ALU.mult)
                v.drain()
                i = v.tensor_scalar(xres_v[:, k, :], sqbuf[k % 2][:, :],
                                    gamma[:, k:k + 1], beta[:, k:k + 1], ALU.mult, ALU.add)
            if out_bf is not None:
                v.drain()
                i = v.tensor_copy(out_bf[:, k, :], xres_v[:, k, :])
            if spill_to is not None:
                sVe.inc(i)
                Vn[k] = sVe.n
        if spill_to is None:
            sVe.inc(i)

    if spill_to is not None:
        sp0 = spill_sem.n

        @blk.sync
        def _(s):
            for k in range(DC):
                s.wait_ge(sVe.h, Vn[k])
                if k:
                    s.wait_ge(spill_sem.h, sp0 + 16 * k)
                s.dma_start(spill_to[:, k, :], xres_v[:, k, :]).then_inc(spill_sem.h, 16)
                spill_sem.n += 16


def _e(stack, ctx):
    return stack.enter_context(ctx)


class GatherEmitter:

    def __init__(self, nc, itb, ibb, acc, tmpg, lutB, quarters, sIdx, sGs, sGd, ALU):
        self.quarters = quarters
        self.sIdx = sIdx
        self.sGs = sGs
        self.sGd = sGd
        self.itb_aps = itb
        self.acc_aps = acc
        self.gd0 = sGd.n
        self.ops = []
        nq = len(quarters)

        def mk(fn):
            self.ops.append(fn)

        for qi, (srcap, cols, dstap) in enumerate(quarters):
            def startf(qi=qi, cols=cols):
                def f(v):
                    v.wait_ge(sIdx[qi % 2].h, 16 * (qi // 2 + 1))
                    if qi >= 2:
                        v.wait_ge(sGs[qi % 2].h, 16 * (qi // 2))
                    v.drain()
                    v.tensor_copy(ibb[:, 0:cols], itb[qi % 2][:, 0:cols])
                return f
            mk(startf())
            def p0(qi=qi, cols=cols):
                def f(v):
                    v.drain()
                    v.tensor_scalar(acc[qi % 2][:, 0:cols], ibb[:, 0:cols],
                                    0.0, lutB[:, 0:1], ALU.is_equal, ALU.mult)
                return f
            mk(p0())
            for j in range(1, 16):
                def pj(qi=qi, cols=cols, j=j):
                    def f(v):
                        v.drain()
                        v.tensor_scalar(tmpg[:, 0:cols], ibb[:, 0:cols],
                                        float(j), lutB[:, j:j + 1], ALU.is_equal, ALU.mult)
                        v.drain()
                        i = v.tensor_tensor(acc[qi % 2][:, 0:cols], acc[qi % 2][:, 0:cols],
                                            tmpg[:, 0:cols], ALU.add)
                        if j == 15:
                            i.then_inc(sGd.h, 1)
                            sGd.n += 1
                    return f
                mk(pj())
        self.pos = 0
        self.ops_per_q = len(self.ops) // nq
        self.store_totals = []
        tot = [0, 0]
        for qi in range(nq):
            tot[qi % 2] += 16
            self.store_totals.append(tuple(tot))

    @property
    def done_quarters(self):
        return self.pos // self.ops_per_q

    def emit(self, v, k):
        end = min(self.pos + k, len(self.ops))
        while self.pos < end:
            self.ops[self.pos](v)
            self.pos += 1

    def flush(self, v):
        self.emit(v, len(self.ops))

    def dma_chain(self, eng, q_from, q_to, extra=None):
        N = len(self.quarters)

        def load(q):
            srcap, cols, dstap = self.quarters[q]
            eng.dma_start(self.itb_aps[q % 2][:, 0:cols], srcap).then_inc(self.sIdx[q % 2].h, 16)
            self.sIdx[q % 2].n += 16

        if q_from == 0:
            for q in (0, 1):
                if q < N:
                    load(q)
        for q in range(q_from, q_to):
            if extra and q in extra:
                extra[q](eng)
            eng.wait_ge(self.sGd.h, self.gd0 + q + 1)
            srcap, cols, dstap = self.quarters[q]
            eng.dma_start(dstap, self.acc_aps[q % 2][:, 0:cols]).then_inc(self.sGs[q % 2].h, 16)
            self.sGs[q % 2].n += 16
            nxt = q + 2
            if nxt < N and nxt < q_to + 2:
                load(nxt)


onesC_g = None
onesR_g = None



def build_nc():
    global onesC_g, onesR_g
    nc = bass.Bass("TRN2", target_bir_lowering=False)
    ap = lambda name, shape, dt, kind: nc.dram_tensor(name, shape, dt, kind=kind).ap()
    aps = lambda name, shape, dt: nc.dram_tensor(name, shape, dt, kind="Internal",
                                                 addr_space="Shared").ap()
    EI, EO, IN = "ExternalInput", "ExternalOutput", "Internal"

    xTb = ap("xTb", [D, S], F32, EI)
    xsliceT = ap("xsliceT", [D, TOK], F32, EI)
    wdqT = ap("wdqT", [D, CQ], F32, EI)
    wdkvT = ap("wdkvT", [D, CKV], F32, EI)
    wuqT = ap("wuqT", [CQ, 1024], F32, EI)
    wukvTk = ap("wukvTk", [CKV, 1024], F32, EI)
    wukvTv = ap("wukvTv", [CKV, 1024], F32, EI)
    woutT = ap("woutT", [D, D], F32, EI)
    lutrow = ap("lutrow", [1, 16], F32, EI)
    gidxT = ap("gidxT", [2, 128, INTER], I32, EI)
    uidxT = ap("uidxT", [2, 128, INTER], I32, EI)
    didxT = ap("didxT", [6, 128, D], I32, EI)
    winTre = ap("winTre", [D, FEAT], F32, EI)
    binT = ap("binT", [128, 48], F32, EI)
    gammaT = ap("gammaT", [128, 16], F32, EI)
    betaT = ap("betaT", [128, 16], F32, EI)
    wout1T = ap("wout1T", [D, D], F32, EI)
    boutT = ap("boutT", [128, 16], F32, EI)
    wfT = ap("wfT", [D, D], F32, EI)
    biasT = ap("biasT", [128, 16], F32, EI)

    out = ap("out", [DC, 128, TOK], F32, EO)

    ag_o_in = [ap("ag_o_in0", [128, 6 * S], BF16, IN), ap("ag_o_in1", [128, 2 * S], BF16, IN)]
    ag_o_out = [aps("ag_o_out0", [NCORES, 128, 6 * S], BF16),
                aps("ag_o_out1", [NCORES, 128, 2 * S], BF16)]
    GOFF = 0
    UOFF = 2 * 128 * INTER
    GUP = 4 * 128 * INTER
    DLEN = 6 * 128 * D
    ag_gu_in = ap("ag_gu_in", [GUP], BF16, IN)
    ag_gu_out = aps("ag_gu_out", [NCORES, GUP], BF16)
    ag_dn_in = ap("ag_dn_in", [DLEN], BF16, IN)
    ag_dn_out = aps("ag_dn_out", [NCORES, DLEN], BF16)
    x_sp = [ap(f"x_sp{i}", [128, DC * TOK], F32, IN) for i in range(2)]

    RG = [list(range(NCORES))]
    rank = nc.axis_index(bass.AxisInfo(NCORES, 1))

    with contextlib.ExitStack() as ST:
        onesC = _e(ST, nc.sbuf_tensor("onesC", [128, 1], F32))
        onesR = _e(ST, nc.sbuf_tensor("onesR", [1, 128], F32))
        identF = _e(ST, nc.sbuf_tensor("identF", [128, 128], F32))
        Mneg = _e(ST, nc.sbuf_tensor("Mneg", [128, 128], BF16))
        MnegT = _e(ST, nc.sbuf_tensor("MnegT", [128, 128], BF16))
        pcol = _e(ST, nc.sbuf_tensor("pcol", [128, 1], F32))
        lutB = _e(ST, nc.sbuf_tensor("lutB", [128, 16], F32))
        lutrow_sb = _e(ST, nc.sbuf_tensor("lutrow_sb", [1, 16], F32))
        acols = _e(ST, nc.sbuf_tensor("acols", [128, 48], F32))
        DTS = _e(ST, contextlib.ExitStack())
        DTA = [_e(DTS, nc.sbuf_tensor(f"DTA{m}", [128, 128], F32)) for m in range(2)]
        DTB = [_e(DTS, nc.sbuf_tensor(f"DTB{m}", [128, 128], F32)) for m in range(2)]
        DTC = [_e(DTS, nc.sbuf_tensor(f"DTC{m}", [128, 128], F32)) for m in range(2)]
        onesC_g, onesR_g = onesC, onesR

        sLd = Sem(_e(ST, nc.semaphore("sLd")))
        sLg = Sem(_e(ST, nc.semaphore("sLg")))
        sSt = Sem(_e(ST, nc.semaphore("sSt")))
        sPe = Sem(_e(ST, nc.semaphore("sPe")))
        sVe = Sem(_e(ST, nc.semaphore("sVe")))
        sSc = Sem(_e(ST, nc.semaphore("sSc")))
        sGp = Sem(_e(ST, nc.semaphore("sGp")))
        sCc = Sem(_e(ST, nc.semaphore("sCc")))
        sOd = [Sem(_e(ST, nc.semaphore(f"sOd{i}"))) for i in range(2)]
        sIdx = [Sem(_e(ST, nc.semaphore(f"sIdx{i}"))) for i in range(2)]
        sGs = [Sem(_e(ST, nc.semaphore(f"sGs{i}"))) for i in range(2)]
        sGU = [Sem(_e(ST, nc.semaphore(f"sGU{i}"))) for i in range(4)]
        sWd = [Sem(_e(ST, nc.semaphore(f"sWd{i}"))) for i in range(2)]
        sWc = [Sem(_e(ST, nc.semaphore(f"sWc{i}"))) for i in range(2)]
        sWf = [Sem(_e(ST, nc.semaphore(f"sWf{i}"))) for i in range(2)]
        sWq = [Sem(_e(ST, nc.semaphore(f"sWq{i}"))) for i in range(2)]
        sGd = Sem(_e(ST, nc.semaphore("sGd")))
        sLdh = [Sem(_e(ST, nc.semaphore(f"sLdh{i}"))) for i in range(2)]
        sWo = Sem(_e(ST, nc.semaphore("sWo")))
        sSpB = Sem(_e(ST, nc.semaphore("sSpB")))
        sSpC = Sem(_e(ST, nc.semaphore("sSpC")))
        sSpF = Sem(_e(ST, nc.semaphore("sSpF")))
        sLx = Sem(_e(ST, nc.semaphore("sLx")))
        sems = {"pe": sPe, "ve": sVe, "sc": sSc}

        with nc.Block() as blk, contextlib.ExitStack() as PS:
            ps16 = _e(PS, nc.psum_tensor("ps16", [128, 16], F32))

            @blk.sync
            def _(s):
                s.dma_start(lutrow_sb[:], lutrow).then_inc(sLd.h, 16)
            sLd.n += 16

            S0S = _e(PS, contextlib.ExitStack())
            frow = _e(S0S, nc.sbuf_tensor("frow", [128, 128], F32))
            firow = _e(S0S, nc.sbuf_tensor("firow", [128, 128], I32))
            iw1 = _e(S0S, nc.sbuf_tensor("iw1", [128, 128], I32))
            iw2 = _e(S0S, nc.sbuf_tensor("iw2", [128, 128], I32))
            fw1 = _e(S0S, nc.sbuf_tensor("fw1", [128, 128], F32))
            fw2 = _e(S0S, nc.sbuf_tensor("fw2", [128, 128], F32))

            @blk.gpsimd
            def _(g):
                g.iota(frow[:], [[1, 128]], base=0, channel_multiplier=0, allow_small_or_imprecise_dtypes=True)
                g.drain()
                g.iota(pcol[:], [[1, 1]], base=0, channel_multiplier=1, allow_small_or_imprecise_dtypes=True)
                g.drain()
                i = g.iota(firow[:], [[1, 128]], base=0, channel_multiplier=0)
                sGp.inc(i)

            @blk.vector
            def _(v):
                v.memset(onesC[:], 1.0)
                v.memset(onesR[:], 1.0)
                v.wait_ge(sGp.h, sGp.n)
                v.tensor_scalar(identF[:], frow[:], pcol[:, :], 1.0, ALU.is_equal, ALU.mult)
                v.tensor_scalar(Mneg[:], frow[:], pcol[:, :], NEG, ALU.is_gt, ALU.mult)
                i = v.tensor_scalar(MnegT[:], frow[:], pcol[:, :], NEG, ALU.is_lt, ALU.mult)
                sVe.inc(i)

            @blk.tensor
            def _(te):
                te.wait_ge(sLd.h, sLd.n)
                te.wait_ge(sVe.h, sVe.n)
                i = te.matmul(ps16[:, :], onesR[:, :], lutrow_sb[:, :], start=True, stop=True)
                sPe.inc(i)

            @blk.vector
            def _(v):
                v.wait_ge(sPe.h, sPe.n)
                i = v.tensor_copy(lutB[:, :], ps16[:, :])
                sVe.inc(i)

            @blk.vector
            def _(v):
                v.wait_ge(sGp.h, sGp.n)
                for m in range(2):
                    v.tensor_scalar(iw1[:], firow[:], 2, None, ALU.arith_shift_right)
                    v.tensor_scalar(iw2[:], firow[:], 1, None, ALU.arith_shift_right)
                    v.drain()
                    v.tensor_scalar(iw1[:], iw1[:], 1, None, ALU.arith_shift_left)
                    v.tensor_scalar(iw2[:], iw2[:], 1, None, ALU.bitwise_and)
                    v.drain()
                    v.tensor_tensor(iw1[:], iw1[:], iw2[:], ALU.add)
                    v.drain()
                    v.tensor_scalar(fw1[:], iw1[:], float(64 * m), None, ALU.add)
                    v.drain()
                    v.tensor_scalar(DTA[m][:], fw1[:], pcol[:, :], 1.0,
                                    ALU.is_equal, ALU.mult)
                    v.drain()
                    v.tensor_scalar(iw1[:], firow[:], 2, None, ALU.arith_shift_right)
                    v.tensor_scalar(iw2[:], firow[:], 1, None, ALU.bitwise_and)
                    v.drain()
                    v.tensor_scalar(iw1[:], iw1[:], 1, None, ALU.arith_shift_left)
                    v.drain()
                    v.tensor_tensor(iw1[:], iw1[:], iw2[:], ALU.add)
                    v.drain()
                    v.tensor_scalar(fw1[:], iw1[:], float(64 * m), None, ALU.add)
                    v.drain()
                    v.tensor_scalar(DTB[m][:], fw1[:], pcol[:, :], 1.0,
                                    ALU.is_equal, ALU.mult)
                    v.drain()
                    v.tensor_scalar(iw1[:], firow[:], 1, None, ALU.arith_shift_right)
                    v.tensor_scalar(iw2[:], firow[:], 1, None, ALU.bitwise_and)
                    v.drain()
                    v.tensor_scalar(iw1[:], iw1[:], 32 * m, None, ALU.subtract)
                    v.drain()
                    v.tensor_scalar(iw1[:], iw1[:], 2, None, ALU.arith_shift_left)
                    v.drain()
                    v.tensor_tensor(iw1[:], iw1[:], iw2[:], ALU.add)
                    v.drain()
                    v.tensor_scalar(fw1[:], iw1[:], 0.0, None, ALU.add)
                    v.drain()
                    v.tensor_scalar(fw2[:], fw1[:], pcol[:, :], -1.0,
                                    ALU.is_equal, ALU.mult)
                    v.tensor_scalar(fw1[:], fw1[:], 2.0, None, ALU.add)
                    v.drain()
                    v.tensor_scalar(fw1[:], fw1[:], pcol[:, :], -1.0,
                                    ALU.is_equal, ALU.mult)
                    v.drain()
                    i = v.tensor_tensor(DTC[m][:], fw1[:], fw2[:], ALU.add)
                    sVe.inc(i)


        SG = _e(ST, contextlib.ExitStack())
        g_itb = [_e(SG, nc.sbuf_tensor(f"g_itb{i}", [128, 688], I32)) for i in range(2)]
        g_ibb = _e(SG, nc.sbuf_tensor("g_ibb", [128, 688], BF16))
        g_acc = [_e(SG, nc.sbuf_tensor(f"g_acc{i}", [128, 688], BF16)) for i in range(2)]
        g_tmp = _e(SG, nc.sbuf_tensor("g_tmp", [128, 688], BF16))
        quarters = []
        for agt, base, idxt, ntile, tcols, bounds in (
                (ag_gu_in, GOFF, gidxT, 2, INTER, [688 * i for i in range(8)] + [INTER]),
                (ag_gu_in, UOFF, uidxT, 2, INTER, [688 * i for i in range(8)] + [INTER]),
                (ag_dn_in, 0, didxT, 6, D, [0, 688, 1376, 2048])):
            for t in range(ntile):
                tb = base + t * 128 * tcols
                tile_dst = agt[tb:tb + 128 * tcols].rearrange("(p c) -> p c", p=128)
                for q in range(len(bounds) - 1):
                    c0_, c1_ = bounds[q], bounds[q + 1]
                    quarters.append((idxt[t][:, c0_:c1_], c1_ - c0_, tile_dst[:, c0_:c1_]))
        GE = GatherEmitter(nc, [t[:] for t in g_itb], g_ibb[:], [a[:] for a in g_acc],
                           g_tmp[:], lutB[:], quarters, sIdx, sGs, sGd, ALU)
        N_GU_Q = 32

        S_attn = _e(ST, contextlib.ExitStack())
        xTbS = _e(S_attn, nc.sbuf_tensor("xTbS", [128, DC * S], BF16))
        wdqS = _e(S_attn, nc.sbuf_tensor("wdqS", [128, DC * CQ], BF16))
        wdkvS = _e(S_attn, nc.sbuf_tensor("wdkvS", [128, DC * CKV], BF16))
        xTbS_v = xTbS[:].rearrange("p (k s) -> p k s", k=DC)
        wdqS_v = wdqS[:].rearrange("p (k j) -> p k j", k=DC)
        wdkvS_v = wdkvS[:].rearrange("p (k j) -> p k j", k=DC)
        qT_v = xTbS[:, 0:NH * S].rearrange("p (h s) -> p h s", h=NH)
        kT_v = xTbS[:, NH * S:2 * NH * S].rearrange("p (h s) -> p h s", h=NH)
        Vt_v = wdqS[:, 0:16 * 1024].rearrange("p (t r) -> p t r", t=16)
        wuk_v = wdqS[:, 16 * 1024:16 * 1024 + 4 * 1024].rearrange("p (k j) -> p k j", k=4)
        wuq_sl = [wdqS[:, 20 * 1024 + i * 1536:20 * 1024 + (i + 1) * 1536]
                  .rearrange("p (k j) -> p k j", k=12) for i in range(2)]
        wuv_v = wdkvS[:, 0:4 * 1024].rearrange("p (k j) -> p k j", k=4)

        S_c = _e(S_attn, contextlib.ExitStack())
        cqT = _e(S_c, nc.sbuf_tensor("cqT", [128, 12 * S], BF16))
        ckvT = _e(S_c, nc.sbuf_tensor("ckvT", [128, 4 * S], BF16))
        cqT_v = cqT[:].rearrange("p (j s) -> p j s", j=12)
        ckvT_v = ckvT[:].rearrange("p (j s) -> p j s", j=4)

        with nc.Block() as blk, contextlib.ExitStack() as PS:
            psA = [_e(PS, nc.psum_tensor(f"psA{i}", [128, 2048], F32)) for i in range(2)]

            lg0a = sLg.n

            @blk.gpsimd
            def _(g):
                xv = xTb.rearrange("(k p) s -> p k s", p=128)
                wv = wdqT.rearrange("(k p) j -> p k j", p=128)
                for k in range(DC):
                    if k:
                        g.wait_ge(sLg.h, lg0a + 32 * k)
                    g.dma_start(xTbS_v[:, k, :], xv[:, k, :]).then_inc(sLg.h, 16)
                    g.dma_start(wdqS_v[:, k, :], wv[:, k, :]).then_inc(sLg.h, 16)
                g.wait_ge(sLg.h, lg0a + 32 * DC)
                g.dma_start(wdkvS_v, wdkvT.rearrange("(k p) j -> p k j", p=128)).then_inc(sLg.h, 16)
            sLg.n += 33 * 16
            p0, v0 = sPe.n, sVe.n
            jobs = [(cqT_v, wdqS_v, 12), (ckvT_v, wdkvS_v, 4)]

            @blk.tensor
            def _(te):
                it = 0
                for dst, wS, nj in jobs:
                    for j in range(nj):
                        if it >= 2:
                            te.wait_ge(sVe.h, v0 + it - 1)
                        for k in range(DC):
                            if it == 0:
                                te.wait_ge(sLg.h, lg0a + 32 * (k + 1))
                            elif it == 12 and k == 0:
                                te.wait_ge(sLg.h, sLg.n)
                            for sc_ in range(4):
                                i = te.matmul(psA[it % 2][:, sc_ * 512:sc_ * 512 + 512],
                                              wS[:, k, j * 128:j * 128 + 128],
                                              xTbS_v[:, k, sc_ * 512:sc_ * 512 + 512],
                                              start=(k == 0), stop=(k == DC - 1))
                        it += 1
                        sPe.inc(i)

            @blk.vector
            def _(v):
                it = 0
                for dst, wS, nj in jobs:
                    for j in range(nj):
                        v.wait_ge(sPe.h, p0 + it + 1)
                        i = v.tensor_copy(dst[:, j, :], psA[it % 2][:, :])
                        it += 1
                        sVe.inc(i)

        with nc.Block() as blk, contextlib.ExitStack() as PS:
            psA = [_e(PS, nc.psum_tensor(f"psB{i}", [128, 2048], F32)) for i in range(2)]
            p0, v0 = sPe.n, sVe.n

            def a2_sched():
                for h in range(NH):
                    yield ("q", h)
                for h in range(NH):
                    yield ("k", h)
                for tb in range(16):
                    yield ("v", tb)

            pe_at2 = {}
            psim = p0
            for it, _kind in enumerate(a2_sched()):
                psim += 1
                pe_at2[it] = psim

            @blk.gpsimd
            def _(g):
                g.dma_start(wuk_v, wukvTk.rearrange("(k p) j -> p k j", p=128)).then_inc(sLg.h, 16)
                g.dma_start(wuv_v, wukvTv.rearrange("(k p) j -> p k j", p=128)).then_inc(sLg.h, 16)
                for h in range(NH):
                    if h >= 2:
                        g.wait_ge(sPe.h, pe_at2[h - 2])
                    g.dma_start(wuq_sl[h % 2],
                                wuqT[:, h * 128:h * 128 + 128]
                                .rearrange("(k p) j -> p k j", p=128)).then_inc(sWq[h % 2].h, 16)
                    sWq[h % 2].n += 16
            sLg.n += 32
            lg_a2 = sLg.n

            @blk.tensor
            def _(te):
                for it, (kind, idx) in enumerate(a2_sched()):
                    if it >= 2:
                        te.wait_ge(sVe.h, v0 + it - 1)
                    if kind == "q":
                        te.wait_ge(sWq[idx % 2].h, 16 * (idx // 2 + 1))
                        for k in range(12):
                            for sc_ in range(4):
                                i = te.matmul(psA[it % 2][:, sc_ * 512:sc_ * 512 + 512],
                                              wuq_sl[idx % 2][:, k, :],
                                              cqT_v[:, k, sc_ * 512:sc_ * 512 + 512],
                                              start=(k == 0), stop=(k == 11))
                    elif kind == "k":
                        te.wait_ge(sLg.h, lg_a2)
                        for k in range(4):
                            for sc_ in range(4):
                                i = te.matmul(psA[it % 2][:, sc_ * 512:sc_ * 512 + 512],
                                              wuk_v[:, k, idx * 128:idx * 128 + 128],
                                              ckvT_v[:, k, sc_ * 512:sc_ * 512 + 512],
                                              start=(k == 0), stop=(k == 3))
                    else:
                        for k in range(4):
                            for sc_ in range(2):
                                i = te.matmul(psA[it % 2][:, sc_ * 512:sc_ * 512 + 512],
                                              ckvT_v[:, k, idx * 128:idx * 128 + 128],
                                              wuv_v[:, k, sc_ * 512:sc_ * 512 + 512],
                                              start=(k == 0), stop=(k == 3))
                    sPe.inc(i)

            @blk.vector
            def _(v):
                for it, (kind, idx) in enumerate(a2_sched()):
                    v.wait_ge(sPe.h, p0 + it + 1)
                    if kind == "q":
                        i = v.tensor_copy(qT_v[:, idx, :], psA[it % 2][:, :])
                    elif kind == "k":
                        i = v.tensor_copy(kT_v[:, idx, :], psA[it % 2][:, :])
                    else:
                        i = v.tensor_copy(Vt_v[:, idx, :], psA[it % 2][:, 0:1024])
                    sVe.inc(i)
        S_c.close()

        with contextlib.ExitStack() as PH:
            Pscr = [_e(PH, nc.sbuf_tensor(f"Pscr{i}", [128, 2048], BF16)) for i in range(2)]
            negb = [_e(PH, nc.sbuf_tensor(f"negb{i}", [128, 1], F32)) for i in range(2)]
            mcols = _e(PH, nc.sbuf_tensor("mcols", [128, 128], F32))
            lcols = _e(PH, nc.sbuf_tensor("lcols", [128, 128], F32))
            lncols = _e(PH, nc.sbuf_tensor("lncols", [128, 128], F32))
            bcol = _e(PH, nc.sbuf_tensor("bcol", [128, 128], F32))
            Bsb = _e(PH, nc.sbuf_tensor("Bsb", [128, 2048], F32))
            Pin = [_e(PH, nc.sbuf_tensor(f"Pin{i}", [128, 512], F32)) for i in range(2)]
            PTc = [_e(PH, nc.sbuf_tensor(f"PTc{i}", [128, 512], BF16)) for i in range(2)]
            oTh = [_e(PH, nc.sbuf_tensor(f"oTh{i}", [128, 2048], BF16)) for i in range(2)]

            pairs = [(h, qb) for h in range(NH) for qb in range(16)]
            with nc.Block() as blk, contextlib.ExitStack() as PS:
                psS = [_e(PS, nc.psum_tensor(f"psS{i}", [128, 2048], F32)) for i in range(2)]
                p0, v0, c0 = sPe.n, sVe.n, sSc.n

                @blk.tensor
                def _(te):
                    for it, (h, qb) in enumerate(pairs):
                        ext = (qb + 1) * 128
                        if it >= 2:
                            te.wait_ge(sSc.h, c0 + it - 1)
                        for k0 in range(0, ext, 512):
                            w = min(512, ext - k0)
                            i = te.matmul(psS[it % 2][:, k0:k0 + w],
                                          qT_v[:, h, qb * 128:qb * 128 + 128],
                                          kT_v[:, h, k0:k0 + w], start=True, stop=True)
                        sPe.inc(i)

                @blk.vector
                def _(v):
                    for it, (h, qb) in enumerate(pairs):
                        ext = (qb + 1) * 128
                        v.wait_ge(sPe.h, p0 + it + 1)
                        v.tensor_tensor(psS[it % 2][:, qb * 128:qb * 128 + 128],
                                        psS[it % 2][:, qb * 128:qb * 128 + 128], Mneg[:, :], ALU.add)
                        v.drain()
                        v.tensor_reduce(mcols[:, it:it + 1], psS[it % 2][:, 0:ext], AX.X, ALU.max)
                        v.drain()
                        i = v.tensor_scalar(negb[it % 2][:, :], mcols[:, it:it + 1], -SCALE, None, ALU.mult)
                        sVe.inc(i)
                        GE.emit(v, 1)

                @blk.scalar
                def _(sc):
                    for it, (h, qb) in enumerate(pairs):
                        ext = (qb + 1) * 128
                        sc.wait_ge(sVe.h, v0 + it + 1)
                        i = sc.activation(Pscr[it % 2][:, 0:ext], psS[it % 2][:, 0:ext], AF.Exp,
                                          bias=negb[it % 2][:, :], scale=SCALE,
                                          accum_out=lcols[:, it:it + 1])
                        sSc.inc(i)

                B1q = GE.done_quarters

                @blk.sync
                def _(s):
                    GE.dma_chain(s, 0, B1q)

            with nc.Block() as blk:
                @blk.scalar
                def _(sc):
                    sc.drain()
                    i = sc.activation(lncols[:, :], lcols[:, :], AF.Ln)
                    sSc.inc(i)

                @blk.vector
                def _(v):
                    v.wait_ge(sSc.h, sSc.n)
                    i = v.scalar_tensor_tensor(bcol[:, :], mcols[:, :], SCALE, lncols[:, :],
                                               ALU.mult, ALU.add)
                    sVe.inc(i)

            with nc.Block() as blk, contextlib.ExitStack() as PS:
                psO = _e(PS, nc.psum_tensor("psO", [128, 2048], F32))
                psX = _e(PS, nc.psum_tensor("psX", [128, 2048], F32))
                p0, v0, c0, st0 = sPe.n, sVe.n, sSc.n, sSt.n

                def head_chunks(h):
                    ch = []
                    for kb in range(16):
                        q0 = kb * 128
                        c = q0
                        while c < S:
                            up = min(S, (c // 512 + 1) * 512)
                            bank = c // 512
                            stop = (kb == min(15, 4 * bank + 3))
                            ch.append((kb, c, up - c, c == q0, stop))
                            c = up
                    return ch

                pe_n, ve_n, sc_n, st_n = [p0], [v0], [c0], [st0]
                tr_done_at = {}
                bevac_at = {}
                stt_at = {}
                exp_at = {}
                oevac_at = {}
                odma_at = {}
                vsim = v0
                for h in range(NH):
                    for qb in range(16):
                        vsim += 1
                        tr_done_at[(h, qb)] = vsim
                    vsim += 1
                    bevac_at[h] = vsim
                    for ci in range(len(head_chunks(h))):
                        vsim += 1
                        stt_at[(h, ci)] = vsim
                    vsim += 1
                    oevac_at[h] = vsim
                csim = c0
                for h in range(NH):
                    for ci in range(len(head_chunks(h))):
                        csim += 1
                        exp_at[(h, ci)] = csim
                for h in range(NH):
                    odma_at[h] = 16 * (h // 2 + 1)
                pe_tr_at = {}
                pe_b_at = {}
                pe_st_at = {}
                pe_pv_at = {}
                psim = p0
                for h in range(NH):
                    for qb in range(16):
                        psim += 1
                        pe_tr_at[(h, qb)] = psim
                    psim += 1
                    pe_b_at[h] = psim
                    for ci in range(len(head_chunks(h))):
                        psim += 1
                        pe_st_at[(h, ci)] = psim
                    psim += 1
                    pe_pv_at[h] = psim

                @blk.tensor
                def _(te):
                    for h in range(NH):
                        chunks = head_chunks(h)
                        for qb in range(16):
                            if qb >= 2:
                                te.wait_ge(sVe.h, tr_done_at[(h, qb - 2)])
                            elif h > 0 and qb == 0:
                                te.wait_ge(sVe.h, oevac_at[h - 1])
                            i = te.transpose(psX[0:1, (qb % 2) * 512:(qb % 2) * 512 + 128],
                                             bcol[:, h * 16 + qb:h * 16 + qb + 1], identF[:, :])
                            sPe.inc(i)
                        te.wait_ge(sVe.h, tr_done_at[(h, 15)])
                        for seg in range(4):
                            i = te.matmul(psX[:, seg * 512:seg * 512 + 512], onesR[:, :],
                                          Bsb[0:1, seg * 512:seg * 512 + 512], start=True, stop=True)
                        sPe.inc(i)
                        te.wait_ge(sVe.h, bevac_at[h])
                        if h > 0:
                            te.wait_ge(sVe.h, oevac_at[h - 1])
                        for ci, (kb, q0, w, first, stop) in enumerate(chunks):
                            if ci >= 2:
                                te.wait_ge(sVe.h, stt_at[(h, ci - 2)])
                            i = te.matmul(psX[:, (ci % 2) * 512:(ci % 2) * 512 + w],
                                          kT_v[:, h, kb * 128:kb * 128 + 128],
                                          qT_v[:, h, q0:q0 + w], start=True, stop=True)
                            sPe.inc(i)
                            if ci >= 1:
                                kbp, q0p, wp, firstp, stopp = chunks[ci - 1]
                                te.wait_ge(sSc.h, exp_at[(h, ci - 1)])
                                te.matmul(psO[:, q0p:q0p + wp],
                                          Vt_v[:, kbp, h * 128:h * 128 + 128],
                                          PTc[(ci - 1) % 2][:, 0:wp],
                                          start=(kbp == 0), stop=stopp)
                        kbp, q0p, wp, firstp, stopp = chunks[-1]
                        te.wait_ge(sSc.h, exp_at[(h, len(chunks) - 1)])
                        i = te.matmul(psO[:, q0p:q0p + wp],
                                      Vt_v[:, kbp, h * 128:h * 128 + 128],
                                      PTc[(len(chunks) - 1) % 2][:, 0:wp],
                                      start=(kbp == 0), stop=stopp)
                        sPe.inc(i)

                oth_hook_at = []

                @blk.vector
                def _(v):
                    for h in range(NH):
                        chunks = head_chunks(h)
                        for qb in range(16):
                            v.wait_ge(sPe.h, pe_tr_at[(h, qb)])
                            v.drain()
                            i = v.tensor_copy(Bsb[0:1, qb * 128:qb * 128 + 128],
                                              psX[0:1, (qb % 2) * 512:(qb % 2) * 512 + 128])
                            sVe.inc(i)
                            GE.emit(v, 1)
                        v.wait_ge(sPe.h, pe_b_at[h])
                        v.drain()
                        i = v.tensor_copy(Bsb[:, :], psX[:, :])
                        sVe.inc(i)
                        for ci, (kb, q0, w, first, stop) in enumerate(chunks):
                            v.wait_ge(sPe.h, pe_st_at[(h, ci)])
                            i = v.scalar_tensor_tensor(Pin[ci % 2][:, 0:w],
                                                       psX[:, (ci % 2) * 512:(ci % 2) * 512 + w],
                                                       SCALE, Bsb[:, q0:q0 + w],
                                                       ALU.mult, ALU.subtract)
                            if first:
                                v.drain()
                                i = v.tensor_tensor(Pin[ci % 2][:, 0:128], Pin[ci % 2][:, 0:128],
                                                    MnegT[:, :], ALU.add)
                            sVe.inc(i)
                            GE.emit(v, 2)
                        v.wait_ge(sPe.h, pe_pv_at[h])
                        if h >= 2:
                            v.wait_ge(sOd[h % 2].h, odma_at[h - 2])
                        i = v.tensor_copy(oTh[h % 2][:, :], psO[:, :])
                        sVe.inc(i)
                        oth_hook_at.append(GE.done_quarters + 1)
                        GE.emit(v, 4)

                @blk.scalar
                def _(sc):
                    for h in range(NH):
                        chunks = head_chunks(h)
                        for ci, (kb, q0, w, first, stop) in enumerate(chunks):
                            sc.wait_ge(sVe.h, stt_at[(h, ci)])
                            i = sc.activation(PTc[ci % 2][:, 0:w], Pin[ci % 2][:, 0:w], AF.Exp)
                            sSc.inc(i)

                B2q = GE.done_quarters

                def _emit_oth(e, h):
                    e.wait_ge(sVe.h, oevac_at[h])
                    dstp = ag_o_in[0][:, h * S:(h + 1) * S] if h < 6 else \
                        ag_o_in[1][:, (h - 6) * S:(h - 5) * S]
                    e.dma_start(dstp, oTh[h % 2][:, :]).then_inc(sOd[h % 2].h, 16)
                    sOd[h % 2].n += 16

                @blk.sync
                def _(s):
                    if B2q > B1q:
                        hook_q = {}
                        for h in range(NH):
                            q = min(B2q - 1, max(B1q, oth_hook_at[h]))
                            hook_q.setdefault(q, []).append(h)
                        extra = {q: (lambda hs: (lambda e: [_emit_oth(e, h) for h in hs]))(hs)
                                 for q, hs in hook_q.items()}
                        GE.dma_chain(s, B1q, B2q, extra=extra)
                    else:
                        for h in range(NH):
                            _emit_oth(s, h)
                    s.wait_ge(sOd[0].h, sOd[0].n)
                    s.wait_ge(sOd[1].h, sOd[1].n)

                @blk.gpsimd
                def _(g):
                    g.wait_ge(sOd[0].h, 48)
                    g.wait_ge(sOd[1].h, 48)
                    i = g.collective_compute("AllGather", ALU.bypass, RG,
                                             ins=[ag_o_in[0]], outs=[ag_o_out[0]])
                    sCc.inc(i)
                    g.wait_ge(sOd[0].h, 64)
                    g.wait_ge(sOd[1].h, 64)
                    i = g.collective_compute("AllGather", ALU.bypass, RG,
                                             ins=[ag_o_in[1]], outs=[ag_o_out[1]])
                    sCc.inc(i)

        B2q_g = [GE.done_quarters]
        cc_o_at = {hb: sCc.n - 1 + hb for hb in range(2)}
        S_attn.close()

        with contextlib.ExitStack() as PH:
            xres = _e(PH, nc.sbuf_tensor("xresA", [128, DC * TOK], F32))
            Bvec = _e(PH, nc.sbuf_tensor("BvecA", [128, 2048], F32))
            sqbuf = [_e(PH, nc.sbuf_tensor(f"sqbufA{i}", [128, 1024], F32)) for i in range(2)]
            orcv = _e(PH, nc.sbuf_tensor("orcv", [128, DC * TOK], BF16))
            woutS = _e(PH, nc.sbuf_tensor("woutS", [128, DC * D], BF16))
            xres_v = xres[:].rearrange("p (k t) -> p k t", k=DC)
            orcv_v = orcv[:].rearrange("p (k t) -> p k t", k=DC)
            woutS_v = woutS[:].rearrange("p (k j) -> p k j", k=DC)

            with nc.Block() as blk, contextlib.ExitStack() as PS:
                psA = [_e(PS, nc.psum_tensor(f"psC{i}", [128, 1024], F32)) for i in range(2)]

                @blk.gpsimd
                def _(g):
                    g.dma_start(woutS_v, woutT.rearrange("(k p) j -> p k j", p=128)).then_inc(sLg.h, 16)
                sLg.n += 16

                @blk.sync
                def _(s):
                    s.dma_start(xres_v, xsliceT.rearrange("(k p) t -> p k t", p=128)).then_inc(sLd.h, 16)
                    for hb in range(2):
                        nh = 6 if hb == 0 else 2
                        s.wait_ge(sCc.h, cc_o_at[hb])
                        for r in range(NCORES):
                            half = r % 2
                            a = r // 2
                            src = ag_o_out[hb][r, :, :].rearrange("p (h q) -> p h q", h=nh)
                            src = src[:, :, bass.ds(rank * 256, 256)]
                            dst = orcv_v[:, half * 8 + hb * 6:half * 8 + hb * 6 + nh,
                                         a * 256:a * 256 + 256]
                            s.dma_start(dst, src).then_inc(sLdh[hb].h, 16)
                            sLdh[hb].n += 16
                sLd.n += 16
                ld_all = sLd.n
                lg_all = sLg.n
                p0, v0 = sPe.n, sVe.n
                rounds = [[0, 1, 2, 3, 4, 5, 8, 9, 10, 11, 12, 13], [6, 7, 14, 15]]

                @blk.tensor
                def _(te):
                    te.wait_ge(sLg.h, lg_all)
                    for rd in range(2):
                        te.wait_ge(sLdh[rd].h, sLdh[rd].n)
                        for r in range(DC):
                            it = rd * DC + r
                            if it >= 2:
                                te.wait_ge(sVe.h, v0 + it - 1)
                            for k in rounds[rd]:
                                for sc_ in range(2):
                                    i = te.matmul(psA[it % 2][:, sc_ * 512:sc_ * 512 + 512],
                                                  woutS_v[:, k, r * 128:r * 128 + 128],
                                                  orcv_v[:, k, sc_ * 512:sc_ * 512 + 512],
                                                  start=(k == rounds[rd][0]),
                                                  stop=(k == rounds[rd][-1]))
                            sPe.inc(i)

                @blk.vector
                def _(v):
                    v.wait_ge(sLd.h, ld_all)
                    for rd in range(2):
                        for r in range(DC):
                            it = rd * DC + r
                            v.wait_ge(sPe.h, p0 + it + 1)
                            i = v.tensor_tensor(xres_v[:, r, :], xres_v[:, r, :],
                                                psA[it % 2][:, :], ALU.add)
                            sVe.inc(i)
                            GE.emit(v, 6)

                @blk.vector
                def _(v):
                    GE.flush(v)

                B3q = GE.done_quarters

                @blk.scalar
                def _(sc):
                    if B3q > B2q_g[0]:
                        GE.dma_chain(sc, B2q_g[0], B3q)

                @blk.gpsimd
                def _(g):
                    gu0, gu1 = GE.store_totals[N_GU_Q - 1]
                    g.wait_ge(sGs[0].h, gu0)
                    g.wait_ge(sGs[1].h, gu1)
                    i = g.collective_compute("AllGather", ALU.bypass, RG,
                                             ins=[ag_gu_in], outs=[ag_gu_out])
                    sCc.inc(i)

            cc_gu_at = sCc.n

            with nc.Block() as blk, contextlib.ExitStack() as PS:
                emit_ln(nc, blk, PS, sems, xres_v, Bvec, sqbuf)

            with nc.Block() as blk:
                @blk.sync
                def _(s):
                    s.wait_ge(sVe.h, sVe.n)
                    s.dma_start(x_sp[0], xres[:]).then_inc(sSt.h, 16)
                    sSt.n += 16
                    s.wait_ge(sSt.h, sSt.n)

        SG.close()

        if N_SUB >= 2:
            GB = [(g0, min(g0 + 4, IC)) for g0 in range(0, IC, 4)]
            with contextlib.ExitStack() as SB:
                h_sb = _e(SB, nc.sbuf_tensor("h_sb", [128, IC * TOK], BF16))
                h_v = h_sb[:].rearrange("p (i t) -> p i t", i=IC)

                with contextlib.ExitStack() as PH:
                    x1b = _e(PH, nc.sbuf_tensor("x1b", [128, DC * TOK], BF16))
                    wg = [_e(PH, nc.sbuf_tensor(f"wgs{i}", [128, 16 * 512], BF16)) for i in range(2)]
                    wu = [_e(PH, nc.sbuf_tensor(f"wus{i}", [128, 16 * 512], BF16)) for i in range(2)]
                    x1b_v = x1b[:].rearrange("p (k t) -> p k t", k=DC)
                    wg_v = [w[:].rearrange("p (k c) -> p k c", k=16) for w in wg]
                    wu_v = [w[:].rearrange("p (k c) -> p k c", k=16) for w in wu]
                    with nc.Block() as blk, contextlib.ExitStack() as PS:
                        gps = [_e(PS, nc.psum_tensor(f"gps{i}", [128, 1024], F32)) for i in range(2)]
                        ups = [_e(PS, nc.psum_tensor(f"ups{i}", [128, 1024], F32)) for i in range(2)]

                        @blk.gpsimd
                        def _(g):
                            g.dma_start(x1b_v, x_sp[0].rearrange("p (k t) -> p k t", k=DC)).then_inc(sLg.h, 16)
                            dn0, dn1 = GE.store_totals[-1]
                            g.wait_ge(sGs[0].h, dn0)
                            g.wait_ge(sGs[1].h, dn1)
                            i = g.collective_compute("AllGather", ALU.bypass, RG,
                                                     ins=[ag_dn_in], outs=[ag_dn_out])
                            sCc.inc(i)
                        sLg.n += 16
                        cc_dn_at = sCc.n
                        lg_x1 = sLg.n
                        p0, v0, c0 = sPe.n, sVe.n, sSc.n
                        pe_at = {}
                        psim = p0
                        for gi, (o0, o1) in enumerate(GB):
                            for oc in range(o0, o1):
                                psim += 2
                            pe_at[gi] = psim
                        pe_oc = {}
                        psim = p0
                        for gi, (o0, o1) in enumerate(GB):
                            for oc in range(o0, o1):
                                pe_oc[(oc, "g")] = psim + 1
                                pe_oc[(oc, "u")] = psim + 2
                                psim += 2
                        sc_oc = {}
                        csim = c0
                        for gi, (o0, o1) in enumerate(GB):
                            for oc in range(o0, o1):
                                csim += 1
                                sc_oc[oc] = csim
                        ve_oc = {}
                        vsim = v0
                        for gi, (o0, o1) in enumerate(GB):
                            for oc in range(o0, o1):
                                vsim += 1
                                ve_oc[oc] = vsim

                        @blk.sync
                        def _(s):
                            s.wait_ge(sCc.h, cc_gu_at)
                            for gi, (o0, o1) in enumerate(GB):
                                gsz = o1 - o0
                                if gi >= 2:
                                    s.wait_ge(sPe.h, pe_at[gi - 2])
                                for buf_v, base in ((wg_v[gi % 2], GOFF), (wu_v[gi % 2], UOFF)):
                                    for sub in range(2):
                                        gv = ag_gu_out[:, base + sub * 128 * INTER:
                                                       base + (sub + 1) * 128 * INTER]
                                        gv = gv.rearrange("r (p c) -> r p c", p=128)
                                        gv = gv[:, :, o0 * 128:o0 * 128 + gsz * 128]
                                        s.dma_start(buf_v[:, sub::2, 0:gsz * 128],
                                                    gv.transpose([1, 0, 2])).then_inc(
                                                        sGU[2 * (gi % 2) + sub].h, 16)
                                        sGU[2 * (gi % 2) + sub].n += 16

                        @blk.tensor
                        def _(te):
                            te.wait_ge(sLg.h, lg_x1)
                            te.wait_ge(sGs[0].h, GE.store_totals[-1][0])
                            te.wait_ge(sGs[1].h, GE.store_totals[-1][1])
                            for gi, (o0, o1) in enumerate(GB):
                                gsz = o1 - o0
                                te.wait_ge(sGU[2 * (gi % 2)].h, 32 * (gi // 2 + 1))
                                te.wait_ge(sGU[2 * (gi % 2) + 1].h, 32 * (gi // 2 + 1))
                                for oc in range(o0, o1):
                                    lc = oc - o0
                                    if oc >= 2:
                                        te.wait_ge(sSc.h, sc_oc[oc - 2])
                                    for k in range(DC):
                                        for sc_ in range(2):
                                            i = te.matmul(gps[oc % 2][:, sc_ * 512:sc_ * 512 + 512],
                                                          wg_v[gi % 2][:, k, lc * 128:lc * 128 + 128],
                                                          x1b_v[:, k, sc_ * 512:sc_ * 512 + 512],
                                                          start=(k == 0), stop=(k == DC - 1))
                                    sPe.inc(i)
                                    if oc >= 2:
                                        te.wait_ge(sVe.h, ve_oc[oc - 2])
                                    for k in range(DC):
                                        for sc_ in range(2):
                                            i = te.matmul(ups[oc % 2][:, sc_ * 512:sc_ * 512 + 512],
                                                          wu_v[gi % 2][:, k, lc * 128:lc * 128 + 128],
                                                          x1b_v[:, k, sc_ * 512:sc_ * 512 + 512],
                                                          start=(k == 0), stop=(k == DC - 1))
                                    sPe.inc(i)

                        @blk.scalar
                        def _(sc):
                            for oc in range(IC):
                                sc.wait_ge(sPe.h, pe_oc[(oc, "g")])
                                i = sc.activation(h_v[:, oc, :], gps[oc % 2][:, :], AF.Silu)
                                sSc.inc(i)

                        @blk.vector
                        def _(v):
                            for oc in range(IC):
                                v.wait_ge(sSc.h, sc_oc[oc])
                                v.wait_ge(sPe.h, pe_oc[(oc, "u")])
                                i = v.tensor_tensor(h_v[:, oc, :], h_v[:, oc, :],
                                                    ups[oc % 2][:, :], ALU.mult)
                                sVe.inc(i)

                with contextlib.ExitStack() as PH:
                    xres = _e(PH, nc.sbuf_tensor("xresB", [128, DC * TOK], F32))
                    Bvec = _e(PH, nc.sbuf_tensor("BvecB", [128, 2048], F32))
                    sqbuf = [_e(PH, nc.sbuf_tensor(f"sqbufB{i}", [128, 1024], F32)) for i in range(2)]
                    dsl = [_e(PH, nc.sbuf_tensor(f"dsl{i}", [128, 48 * 128], BF16)) for i in range(2)]
                    xres_v = xres[:].rearrange("p (k t) -> p k t", k=DC)
                    dsl_v = [w[:].rearrange("p (i c) -> p i c", i=48) for w in dsl]
                    with nc.Block() as blk, contextlib.ExitStack() as PS:
                        dps = [_e(PS, nc.psum_tensor(f"dps{i}", [128, 1024], F32)) for i in range(2)]
                        p0, v0 = sPe.n, sVe.n
                        ld0 = sLd.n

                        @blk.sync
                        def _(s):
                            s.dma_start(xres_v, x_sp[0].rearrange("p (k t) -> p k t", k=DC)).then_inc(sLd.h, 16)
                            s.wait_ge(sCc.h, cc_dn_at)
                            for r in range(DC):
                                if r >= 2:
                                    s.wait_ge(sPe.h, p0 + (r - 1))
                                for sub in range(6):
                                    gv = ag_dn_out[:, sub * 128 * D:(sub + 1) * 128 * D]
                                    gv = gv.rearrange("r (p c) -> r p c", p=128)
                                    gv = gv[:, :, r * 128:(r + 1) * 128]
                                    s.dma_start(dsl_v[r % 2][:, sub::6, :],
                                                gv.transpose([1, 0, 2])).then_inc(sWd[r % 2].h, 16)
                                    sWd[r % 2].n += 16
                        sLd.n = ld0 + 16

                        @blk.tensor
                        def _(te):
                            for r in range(DC):
                                te.wait_ge(sWd[r % 2].h, 96 * (r // 2 + 1))
                                if r >= 2:
                                    te.wait_ge(sVe.h, v0 + (r - 1))
                                for i_ in range(IC):
                                    for sc_ in range(2):
                                        i = te.matmul(dps[r % 2][:, sc_ * 512:sc_ * 512 + 512],
                                                      dsl_v[r % 2][:, i_, :],
                                                      h_v[:, i_, sc_ * 512:sc_ * 512 + 512],
                                                      start=(i_ == 0), stop=(i_ == IC - 1))
                                sPe.inc(i)

                        @blk.vector
                        def _(v):
                            for r in range(DC):
                                v.wait_ge(sPe.h, p0 + r + 1)
                                v.wait_ge(sLd.h, ld0 + 16)
                                i = v.tensor_tensor(xres_v[:, r, :], xres_v[:, r, :],
                                                    dps[r % 2][:, :], ALU.add)
                                sVe.inc(i)

                    with nc.Block() as blk, contextlib.ExitStack() as PS:
                        emit_ln(nc, blk, PS, sems, xres_v, Bvec, sqbuf,
                                spill_to=x_sp[1].rearrange("p (k t) -> p k t", k=DC),
                                spill_sem=sSpB)

        if N_SUB >= 3:
            with contextlib.ExitStack() as PH:
                o_ln = _e(PH, nc.sbuf_tensor("o_ln", [128, DC * TOK], F32))
                gmS = _e(PH, nc.sbuf_tensor("gmS", [128, 16], F32))
                btS = _e(PH, nc.sbuf_tensor("btS", [128, 16], F32))
                binT_sb = _e(PH, nc.sbuf_tensor("binT_sb", [128, 48], F32))
                boutT_sb = _e(PH, nc.sbuf_tensor("boutT_sb", [128, 16], F32))
                o_v = o_ln[:].rearrange("p (k t) -> p k t", k=DC)
                PHJ = _e(PH, contextlib.ExitStack())
                x2b = _e(PHJ, nc.sbuf_tensor("x2b", [128, DC * TOK], BF16))
                wfC = [_e(PHJ, nc.sbuf_tensor(f"wfC{i}", [128, D], F32)) for i in range(2)]
                wsl = [_e(PHJ, nc.sbuf_tensor(f"wsl{i}", [128, 16 * 384], BF16)) for i in range(2)]
                ftile = [[_e(PHJ, nc.sbuf_tensor(f"ft{s}_{i}", [128, 1024], F32)) for i in range(2)]
                         for s in range(3)]
                u2 = _e(PHJ, nc.sbuf_tensor("u2", [128, 2048], F32))
                lamf2 = _e(PHJ, nc.sbuf_tensor("lamf2", [128, 2048], F32))
                og2 = _e(PHJ, nc.sbuf_tensor("og2", [128, 2048], F32))
                ssum = _e(PHJ, nc.sbuf_tensor("ssum", [128, 2048], F32))
                rtmp = _e(PHJ, nc.sbuf_tensor("rtmp", [128, 512], F32))
                rst = _e(PHJ, nc.sbuf_tensor("rst", [128, 512], F32))
                x2b_v = x2b[:].rearrange("p (k t) -> p k t", k=DC)
                wsl_v = [w[:].rearrange("p (k c) -> p k c", k=16) for w in wsl]
                u2_v = u2[:].rearrange("p (m t) -> p m t", m=2)
                lam_v = lamf2[:].rearrange("p (m t) -> p m t", m=2)
                og_v = og2[:].rearrange("p (m t) -> p m t", m=2)
                ss_v = ssum[:].rearrange("p (m t) -> p m t", m=2)
                rt_v = rtmp[:].rearrange("p (m t) -> p m t", m=2)
                rs_v = rst[:].rearrange("p (m t) -> p m t", m=2)

                with nc.Block() as blk, contextlib.ExitStack() as PS:
                    fps = _e(PS, nc.psum_tensor("fps", [128, 1024], F32))
                    rps = [_e(PS, nc.psum_tensor(f"rps{i}", [128, 1024], F32)) for i in range(2)]
                    eps = _e(PS, nc.psum_tensor("eps", [128, 1024], F32))
                    lg0 = sLg.n
                    ld0 = sLd.n

                    c0abs = sSc.n

                    @blk.sync
                    def _(s):
                        s.wait_ge(sSpB.h, sSpB.n)
                        s.dma_start(gmS[:], gammaT).then_inc(sLd.h, 16)
                        s.dma_start(btS[:], betaT).then_inc(sLd.h, 16)
                        s.dma_start(binT_sb[:], binT).then_inc(sLd.h, 16)
                        s.dma_start(boutT_sb[:], boutT).then_inc(sLd.h, 16)
                        for k in range(DC):
                            if k >= 2:
                                s.wait_ge(sSc.h, c0abs + 4 * (k - 1))
                            gv = wfT[:, :].rearrange("(k p) c -> p k c", p=128)[:, k, :]
                            s.dma_start(wfC[k % 2][:, :], gv).then_inc(sWf[k % 2].h, 16)
                            sWf[k % 2].n += 16
                    sLd.n += 64
                    ldc_all = sLd.n

                    p0, v0, c0 = sPe.n, sVe.n, sSc.n
                    pe_feat, pe_rem, pe_ein = {}, {}, {}
                    psim = p0
                    for j in range(DC):
                        for seg in range(3):
                            psim += 1
                            pe_feat[(j, seg)] = psim
                        if j >= 1:
                            psim += 1
                            pe_ein[j - 1] = psim
                        for r in range(6):
                            psim += 1
                            pe_rem[(j, r)] = psim
                    psim += 1
                    pe_ein[DC - 1] = psim
                    sc_feat = {}
                    csim = c0
                    for j in range(DC):
                        for seg in range(3):
                            csim += 1
                            sc_feat[(j, seg)] = csim
                        csim += 1
                    ve_rem, ve_t, ve_oev = {}, {}, {}
                    vsim = v0
                    for j in range(DC):
                        for r in range(6):
                            vsim += 1
                            ve_rem[(j, r)] = vsim
                        vsim += 1
                        ve_t[j] = vsim
                        vsim += 1
                        ve_oev[j] = vsim

                    @blk.gpsimd
                    def _(g):
                        xsp1_v = x_sp[1].rearrange("p (k t) -> p k t", k=DC)
                        g.wait_ge(sSpB.h, sSpB.n)
                        for j in range(2):
                            gv = winTre[:, j * 384:(j + 1) * 384]
                            g.dma_start(wsl_v[j % 2],
                                        gv.rearrange("(k p) c -> p k c", p=128)).then_inc(sWc[j % 2].h, 16)
                            sWc[j % 2].n += 16
                        for k in range(DC):
                            if k:
                                g.wait_ge(sLg.h, lg0 + 16 * k)
                            g.dma_start(x2b_v[:, k, :], xsp1_v[:, k, :]).then_inc(sLg.h, 16)
                        for j in range(2, DC):
                            g.wait_ge(sPe.h, pe_feat[(j - 2, 2)])
                            gv = winTre[:, j * 384:(j + 1) * 384]
                            g.dma_start(wsl_v[j % 2],
                                        gv.rearrange("(k p) c -> p k c", p=128)).then_inc(sWc[j % 2].h, 16)
                            sWc[j % 2].n += 16
                    sLg.n += 16 * DC

                    @blk.tensor
                    def _(te):
                        for j in range(DC):
                            te.wait_ge(sWc[j % 2].h, 16 * (j // 2 + 1))
                            for seg in range(3):
                                fcnt = 3 * j + seg
                                if fcnt >= 1:
                                    te.wait_ge(sSc.h, sc_feat[((fcnt - 1) // 3, (fcnt - 1) % 3)])
                                for k in range(DC):
                                    if j == 0 and seg == 0:
                                        te.wait_ge(sLg.h, lg0 + 16 * (k + 1))
                                    for sc_ in range(2):
                                        i = te.matmul(fps[:, sc_ * 512:sc_ * 512 + 512],
                                                      wsl_v[j % 2][:, k, seg * 128:seg * 128 + 128],
                                                      x2b_v[:, k, sc_ * 512:sc_ * 512 + 512],
                                                      start=(k == 0), stop=(k == DC - 1))
                                sPe.inc(i)
                            if j >= 1:
                                te.wait_ge(sVe.h, ve_t[j - 1])
                                if j >= 2:
                                    te.wait_ge(sVe.h, ve_oev[j - 2])
                                for m in range(2):
                                    for sc_ in range(2):
                                        i = te.matmul(eps[:, sc_ * 512:sc_ * 512 + 512],
                                                      DTC[m][:, :],
                                                      u2_v[:, m, sc_ * 512:sc_ * 512 + 512],
                                                      start=(m == 0), stop=(m == 1))
                                sPe.inc(i)
                            rsrc = [(DTA, 2), (DTB, 0), (DTA, 1)]
                            rcnt = 6 * j
                            for which, (Dm, ft) in enumerate(rsrc):
                                te.wait_ge(sSc.h, sc_feat[(j, ft)])
                                for m in range(2):
                                    r = which * 2 + m
                                    if rcnt + r >= 2:
                                        pj, pr = divmod(rcnt + r - 2, 6)
                                        te.wait_ge(sVe.h, ve_rem[(pj, pr)])
                                    for sc_ in range(2):
                                        i = te.matmul(rps[(rcnt + r) % 2][:, sc_ * 512:sc_ * 512 + 512],
                                                      Dm[m][:, :],
                                                      ftile[ft][j % 2][:, sc_ * 512:sc_ * 512 + 512],
                                                      start=True, stop=True)
                                    sPe.inc(i)
                        te.wait_ge(sVe.h, ve_t[DC - 1])
                        te.wait_ge(sVe.h, ve_oev[DC - 2])
                        for m in range(2):
                            for sc_ in range(2):
                                i = te.matmul(eps[:, sc_ * 512:sc_ * 512 + 512],
                                              DTC[m][:, :],
                                              u2_v[:, m, sc_ * 512:sc_ * 512 + 512],
                                              start=(m == 0), stop=(m == 1))
                        sPe.inc(i)

                    @blk.scalar
                    def _(sc):
                        funcs = [AF.Silu, AF.Sigmoid, AF.Sigmoid]
                        sc.wait_ge(sLd.h, ldc_all)
                        for j in range(DC):
                            for seg in range(3):
                                sc.wait_ge(sPe.h, pe_feat[(j, seg)])
                                if j >= 2:
                                    sc.wait_ge(sPe.h, pe_rem[(j - 2, 5)])
                                i = sc.activation(ftile[seg][j % 2][:, :], fps[:, :],
                                                  funcs[seg], bias=binT_sb[:, 3 * j + seg:3 * j + seg + 1])
                                sSc.inc(i)
                            sc.wait_ge(sWf[j % 2].h, 16 * (j // 2 + 1))
                            sc.drain()
                            i = sc.activation(wfC[j % 2][:, :], wfC[j % 2][:, :], AF.Abs,
                                              accum_out=acols[:, j:j + 1])
                            sSc.inc(i)

                    @blk.vector
                    def _(v):
                        for j in range(DC):
                            for r in range(6):
                                which, m = divmod(r, 2)
                                v.wait_ge(sPe.h, pe_rem[(j, r)])
                                v.drain()
                                if which == 0:
                                    i = v.tensor_copy(lam_v[:, m, :], rps[(6 * j + r) % 2][:, :])
                                elif which == 1:
                                    i = v.scalar_tensor_tensor(u2_v[:, m, :], lam_v[:, m, :], 1.0,
                                                               rps[(6 * j + r) % 2][:, :],
                                                               ALU.subtract, ALU.mult)
                                else:
                                    i = v.tensor_copy(og_v[:, m, :], rps[(6 * j + r) % 2][:, :])
                                sVe.inc(i)
                            v.drain()
                            v.tensor_copy(ss_v[:, :, 0:256], u2_v[:, :, 0:256])
                            for a in range(1, 4):
                                v.drain()
                                v.scalar_tensor_tensor(rt_v[:, :, 0:256], lam_v[:, :, a * 256:a * 256 + 256],
                                                       0.0, ss_v[:, :, (a - 1) * 256:a * 256],
                                                       ALU.add, ALU.mult)
                                v.drain()
                                v.tensor_tensor(ss_v[:, :, a * 256:a * 256 + 256], rt_v[:, :, 0:256],
                                                u2_v[:, :, a * 256:a * 256 + 256], ALU.add)
                            v.drain()
                            v.tensor_tensor(ss_v[:, :, 768:1024], ss_v[:, :, 768:1024],
                                            u2_v[:, :, 768:1024], ALU.add)
                            v.drain()
                            v.tensor_copy(rs_v[:, :, 0:256], u2_v[:, :, 768:1024])
                            for a in (2, 1, 0):
                                v.drain()
                                v.scalar_tensor_tensor(rt_v[:, :, 0:256], lam_v[:, :, a * 256:a * 256 + 256],
                                                       0.0, rs_v[:, :, 0:256], ALU.add, ALU.mult)
                                v.drain()
                                v.tensor_tensor(rs_v[:, :, 0:256], rt_v[:, :, 0:256],
                                                u2_v[:, :, a * 256:a * 256 + 256], ALU.add)
                                v.drain()
                                v.tensor_tensor(ss_v[:, :, a * 256:a * 256 + 256],
                                                ss_v[:, :, a * 256:a * 256 + 256],
                                                rs_v[:, :, 0:256], ALU.add)
                            for m in range(2):
                                v.drain()
                                i = v.tensor_tensor(u2_v[:, m, :], ss_v[:, m, :], og_v[:, m, :], ALU.mult)
                            sVe.inc(i)
                            v.wait_ge(sPe.h, pe_ein[j])
                            v.drain()
                            i = v.tensor_copy(o_v[:, j, :], eps[:, :])
                            sVe.inc(i)

                PHJ.close()

                with contextlib.ExitStack() as PH2:
                    Bvec = _e(PH2, nc.sbuf_tensor("BvecC", [128, 2048], F32))
                    sqbuf = [_e(PH2, nc.sbuf_tensor(f"sqbufC{i}", [128, 1024], F32)) for i in range(2)]
                    o_b = _e(PH2, nc.sbuf_tensor("o_b", [128, DC * TOK], BF16))
                    o_b_v = o_b[:].rearrange("p (k t) -> p k t", k=DC)
                    with nc.Block() as blk, contextlib.ExitStack() as PS:
                        emit_ln(nc, blk, PS, sems, o_v, Bvec, sqbuf, gamma=gmS, beta=btS, out_bf=o_b_v)

                    with contextlib.ExitStack() as PH3:
                        xres = _e(PH3, nc.sbuf_tensor("xresC", [128, DC * TOK], F32))
                        wo = [_e(PH3, nc.sbuf_tensor("wo0", [128, 16 * 512], BF16))]
                        xres_v = xres[:].rearrange("p (k t) -> p k t", k=DC)
                        wo_v = [w[:].rearrange("p (k c) -> p k c", k=16) for w in wo]
                        RGC = [(r0, min(r0 + 4, DC)) for r0 in range(0, DC, 4)]
                        with nc.Block() as blk, contextlib.ExitStack() as PS:
                            psA = [_e(PS, nc.psum_tensor(f"psW{i}", [128, 1024], F32)) for i in range(2)]
                            ld0 = sLd.n
                            lg0 = sLg.n
                            p0, v0 = sPe.n, sVe.n
                            pe_r = {}
                            psim = p0
                            for gi, (r0, r1) in enumerate(RGC):
                                for r in range(r0, r1):
                                    psim += 1
                                    pe_r[r] = psim
                            pe_g = {gi: pe_r[r1 - 1] for gi, (r0, r1) in enumerate(RGC)}

                            @blk.sync
                            def _(s):
                                s.wait_ge(sSpB.h, sSpB.n)
                                s.dma_start(xres_v, x_sp[1].rearrange("p (k t) -> p k t", k=DC)).then_inc(sLd.h, 16)
                            sLd.n += 16

                            @blk.gpsimd
                            def _(g):
                                for gi, (r0, r1) in enumerate(RGC):
                                    if gi >= 1:
                                        g.wait_ge(sPe.h, pe_g[gi - 1])
                                    gv = wout1T[:, r0 * 128:r1 * 128]
                                    g.dma_start(wo_v[0][:, :, 0:(r1 - r0) * 128],
                                                gv.rearrange("(k p) c -> p k c", p=128)).then_inc(sWo.h, 16)
                                    sWo.n += 16

                            @blk.tensor
                            def _(te):
                                for gi, (r0, r1) in enumerate(RGC):
                                    te.wait_ge(sWo.h, 16 * (gi + 1))
                                    for r in range(r0, r1):
                                        lc = r - r0
                                        if r >= 2:
                                            te.wait_ge(sVe.h, v0 + r - 1)
                                        for k in range(DC):
                                            for sc_ in range(2):
                                                i = te.matmul(psA[r % 2][:, sc_ * 512:sc_ * 512 + 512],
                                                              wo_v[0][:, k, lc * 128:lc * 128 + 128],
                                                              o_b_v[:, k, sc_ * 512:sc_ * 512 + 512],
                                                              start=(k == 0), stop=(k == DC - 1))
                                        sPe.inc(i)

                            @blk.vector
                            def _(v):
                                v.wait_ge(sLd.h, ld0 + 16)
                                for r in range(DC):
                                    v.wait_ge(sPe.h, pe_r[r])
                                    i = v.scalar_tensor_tensor(xres_v[:, r, :], psA[r % 2][:, :],
                                                               boutT_sb[:, r:r + 1], xres_v[:, r, :],
                                                               ALU.add, ALU.add)
                                    sVe.inc(i)

                        with nc.Block() as blk, contextlib.ExitStack() as PS:
                            emit_ln(nc, blk, PS, sems, xres_v, Bvec, sqbuf,
                                    spill_to=x_sp[0].rearrange("p (k t) -> p k t", k=DC),
                                    spill_sem=sSpC)

        DTS.close()

        if N_SUB >= 4:
            with contextlib.ExitStack() as PH:
                xres = _e(PH, nc.sbuf_tensor("xresD", [128, DC * TOK], F32))
                cT = _e(PH, nc.sbuf_tensor("cT", [128, DC * D], BF16))
                Kq = _e(PH, nc.sbuf_tensor("Kq", [128, DC * TOK], BF16))
                Bvec = _e(PH, nc.sbuf_tensor("BvecD", [128, 2048], F32))
                sqbuf = [_e(PH, nc.sbuf_tensor(f"sqbufD{i}", [128, 1024], F32)) for i in range(2)]
                biasT_sb = _e(PH, nc.sbuf_tensor("biasT_sb", [128, 16], F32))
                ri32 = _e(PH, nc.sbuf_tensor("ri32", [128, 1024], I32))
                rf32 = _e(PH, nc.sbuf_tensor("rf32", [128, 1024], F32))
                qf32 = _e(PH, nc.sbuf_tensor("qf32", [128, 1024], F32))
                wf = [_e(PH, nc.sbuf_tensor(f"wf{i}", [128, D], F32)) for i in range(2)]
                xres_v = xres[:].rearrange("p (k t) -> p k t", k=DC)
                cT_v = cT[:].rearrange("p (k c) -> p k c", k=DC)
                Kq_v = Kq[:].rearrange("p (k t) -> p k t", k=DC)
                amaxrow = Bvec[0:1, 0:1024]
                a_col = acols[:, 16:17]
                alpha_col = acols[:, 17:18]
                half_col = acols[:, 18:19]
                nhalf_col = acols[:, 19:20]

                with nc.Block() as blk, contextlib.ExitStack() as PS:
                    ps_t = _e(PS, nc.psum_tensor("ps_t", [128, 1024], F32))
                    ps_b = _e(PS, nc.psum_tensor("ps_b", [128, 1024], F32))
                    psA = [_e(PS, nc.psum_tensor(f"psD{i}", [128, 1024], F32)) for i in range(2)]
                    p0, v0, c0 = sPe.n, sVe.n, sSc.n
                    ld0 = sLd.n
                    wf0 = (sWf[0].n, sWf[1].n)
                    V_alpha = v0 + 3
                    V_quant = {k: v0 + 3 + 2 * k + 1 for k in range(DC)}
                    V_fold = {k: v0 + 3 + 2 * k + 2 for k in range(DC)}
                    V_red = {b: v0 + 35 + b + 1 for b in range(8)}
                    V_cp = {b: v0 + 43 + b + 1 for b in range(8)}
                    V_scale = v0 + 52
                    V_K = {k: v0 + 52 + k + 1 for k in range(DC)}
                    V_evac = {r: v0 + 68 + r + 1 for r in range(DC)}
                    P_a1, P_a2 = p0 + 1, p0 + 2
                    P_tr = {b: p0 + 2 + b + 1 for b in range(16)}
                    P_bc = p0 + 19
                    P_r = {r: p0 + 19 + r + 1 for r in range(DC)}

                    @blk.sync
                    def _(s):
                        s.wait_ge(sSpC.h, sSpC.n)
                        s.dma_start(biasT_sb[:], biasT).then_inc(sLd.h, 16)
                        for k in range(DC):
                            if k >= 2:
                                s.wait_ge(sVe.h, V_quant[k - 2])
                            gv = wfT[:, :].rearrange("(k p) c -> p k c", p=128)[:, k, :]
                            s.dma_start(wf[k % 2][:, :], gv).then_inc(sWf[k % 2].h, 16)
                            sWf[k % 2].n += 16
                    sLd.n += 16

                    spC_b = sSpC.n - 16 * DC
                    lx0 = sLx.n

                    @blk.scalar
                    def _(sc):
                        xsp0_v = x_sp[0].rearrange("p (k t) -> p k t", k=DC)
                        for k in range(DC):
                            sc.wait_ge(sSpC.h, spC_b + 16 * (k + 1))
                            if k:
                                sc.wait_ge(sLx.h, lx0 + 16 * k)
                            sc.dma_start(xres_v[:, k, :], xsp0_v[:, k, :]).then_inc(sLx.h, 16)
                            sLx.n += 16
                            sc.wait_ge(sLx.h, lx0 + 16 * (k + 1))
                            if k >= 2:
                                sc.wait_ge(sVe.h, V_fold[k - 2])
                            sc.drain()
                            i = sc.activation(qf32[:, :] if k % 2 == 0 else rf32[:, :],
                                              xres_v[:, k, :], AF.Abs)
                            sSc.inc(i)

                    @blk.tensor
                    def _(te):
                        te.wait_ge(sVe.h, v0 + 1)
                        i = te.matmul(ps_b[0:1, 0:1], onesC[:, :], a_col, start=True, stop=True)
                        sPe.inc(i)
                        te.wait_ge(sVe.h, v0 + 2)
                        i = te.matmul(ps_b[:, 32:33], onesR[:, :], acols[0:1, 20:21],
                                      start=True, stop=True)
                        sPe.inc(i)
                        te.wait_ge(sVe.h, V_fold[DC - 1])
                        for b in range(8):
                            if b >= 2:
                                te.wait_ge(sVe.h, V_red[b - 2])
                            i = te.transpose(ps_t[:, (b % 2) * 512:(b % 2) * 512 + 128],
                                             sqbuf[0][:, b * 128:b * 128 + 128], identF[:, :])
                            sPe.inc(i)
                        for b in range(8):
                            te.wait_ge(sVe.h, V_red[7])
                            if b >= 2:
                                te.wait_ge(sVe.h, V_cp[b - 2])
                            i = te.transpose(ps_t[0:1, (b % 2) * 512 + 256:(b % 2) * 512 + 384],
                                             acols[:, 24 + b:25 + b], identF[:, :])
                            sPe.inc(i)
                        te.wait_ge(sVe.h, V_cp[7])
                        te.matmul(ps_b[:, 0:512], onesR[:, :], amaxrow[0:1, 0:512],
                                  start=True, stop=True)
                        i = te.matmul(ps_b[:, 512:1024], onesR[:, :], amaxrow[0:1, 512:1024],
                                      start=True, stop=True)
                        sPe.inc(i)
                        for r in range(DC):
                            if r >= 2:
                                te.wait_ge(sVe.h, V_evac[r - 2])
                            for k in range(DC):
                                if r == 0:
                                    te.wait_ge(sVe.h, V_K[k])
                                for sc_ in range(2):
                                    i = te.matmul(psA[r % 2][:, sc_ * 512:sc_ * 512 + 512],
                                                  cT_v[:, k, r * 128:r * 128 + 128],
                                                  Kq_v[:, k, sc_ * 512:sc_ * 512 + 512],
                                                  start=(k == 0), stop=(k == DC - 1))
                            sPe.inc(i)

                    @blk.vector
                    def _(v):
                        i = v.tensor_reduce(a_col, acols[:, 0:16], AX.X, ALU.add)
                        sVe.inc(i)
                        v.wait_ge(sPe.h, P_a1)
                        v.drain()
                        i = v.tensor_copy(acols[0:1, 20:21], ps_b[0:1, 0:1])
                        sVe.inc(i)
                        v.wait_ge(sPe.h, P_a2)
                        v.drain()
                        v.tensor_copy(alpha_col, ps_b[:, 32:33])
                        v.drain()
                        v.tensor_scalar(alpha_col, alpha_col, 1.0 / float(D * D), EPS,
                                        ALU.mult, ALU.max)
                        v.drain()
                        v.tensor_scalar(half_col, alpha_col, 0.5, None, ALU.mult)
                        i = v.tensor_scalar(nhalf_col, alpha_col, -0.5, None, ALU.mult)
                        sVe.inc(i)
                        for k in range(DC):
                            v.wait_ge(sWf[k % 2].h, wf0[k % 2] + 16 * (k // 2 + 1))
                            v.drain()
                            v.tensor_scalar(sqbuf[1][:, :], wf[k % 2][:, 0:1024],
                                            nhalf_col, None, ALU.is_le)
                            v.drain()
                            v.scalar_tensor_tensor(cT_v[:, k, 0:1024], wf[k % 2][:, 0:1024],
                                                   half_col, sqbuf[1][:, :], ALU.is_ge, ALU.subtract)
                            v.drain()
                            v.tensor_scalar(sqbuf[1][:, :], wf[k % 2][:, 1024:2048],
                                            nhalf_col, None, ALU.is_le)
                            v.drain()
                            i = v.scalar_tensor_tensor(cT_v[:, k, 1024:2048], wf[k % 2][:, 1024:2048],
                                                       half_col, sqbuf[1][:, :], ALU.is_ge, ALU.subtract)
                            sVe.inc(i)
                            v.wait_ge(sSc.h, c0 + k + 1)
                            v.drain()
                            if k == 0:
                                i = v.tensor_copy(sqbuf[0][:, :], qf32[:, :])
                            else:
                                i = v.tensor_tensor(sqbuf[0][:, :], sqbuf[0][:, :],
                                                    qf32[:, :] if k % 2 == 0 else rf32[:, :], ALU.max)
                            sVe.inc(i)
                        for b in range(8):
                            v.wait_ge(sPe.h, P_tr[b])
                            v.drain()
                            i = v.tensor_reduce(acols[:, 24 + b:25 + b],
                                                ps_t[:, (b % 2) * 512:(b % 2) * 512 + 128],
                                                AX.X, ALU.max)
                            sVe.inc(i)
                        for b in range(8):
                            v.wait_ge(sPe.h, P_tr[8 + b])
                            v.drain()
                            i = v.tensor_copy(amaxrow[0:1, b * 128:b * 128 + 128],
                                              ps_t[0:1, (b % 2) * 512 + 256:(b % 2) * 512 + 384])
                            sVe.inc(i)
                        v.wait_ge(sPe.h, P_bc)
                        v.drain()
                        v.tensor_copy(sqbuf[1][:, :], ps_b[:, 0:1024])
                        v.drain()
                        v.tensor_scalar(Bvec[:, 1024:2048], sqbuf[1][:, :], EPS, 1.0 / 127.0,
                                        ALU.max, ALU.mult)
                        v.drain()
                        v.reciprocal(Bvec[:, 0:1024], Bvec[:, 1024:2048])
                        v.drain()
                        i = v.tensor_scalar(Bvec[:, 1024:2048], Bvec[:, 1024:2048], alpha_col,
                                            None, ALU.mult)
                        sVe.inc(i)
                        for k in range(DC):
                            v.drain()
                            v.tensor_tensor(qf32[:, :], xres_v[:, k, :], Bvec[:, 0:1024], ALU.mult)
                            v.drain()
                            v.tensor_copy(ri32[:, :], qf32[:, :])
                            v.drain()
                            v.tensor_copy(rf32[:, :], ri32[:, :])
                            v.drain()
                            v.tensor_tensor(qf32[:, :], qf32[:, :], rf32[:, :], ALU.subtract)
                            v.drain()
                            v.tensor_scalar(sqbuf[0][:, :], qf32[:, :], 0.5, None, ALU.is_ge)
                            v.drain()
                            v.scalar_tensor_tensor(sqbuf[1][:, :], qf32[:, :], -0.5, sqbuf[0][:, :],
                                                   ALU.is_le, ALU.subtract)
                            v.drain()
                            i = v.tensor_tensor(Kq_v[:, k, :], rf32[:, :], sqbuf[1][:, :], ALU.subtract)
                            sVe.inc(i)
                        v.wait_ge(sLd.h, ld0 + 16)
                        for r in range(DC):
                            v.wait_ge(sPe.h, P_r[r])
                            v.drain()
                            v.tensor_tensor(qf32[:, :], psA[r % 2][:, :], Bvec[:, 1024:2048], ALU.mult)
                            v.drain()
                            i = v.scalar_tensor_tensor(xres_v[:, r, :], qf32[:, :],
                                                       biasT_sb[:, r:r + 1], xres_v[:, r, :],
                                                       ALU.add, ALU.add)
                            sVe.inc(i)

                with nc.Block() as blk, contextlib.ExitStack() as PS:
                    emit_ln(nc, blk, PS, sems, xres_v, Bvec, sqbuf,
                            spill_to=out.transpose([1, 0, 2]), spill_sem=sSpF)

        if N_SUB < 4:
            last = x_sp[0] if N_SUB == 1 else x_sp[1] if N_SUB == 2 else x_sp[0]
            with nc.Block() as blk:
                @blk.sync
                def _(s):
                    s.wait_ge(sSpB.h, sSpB.n)
                    s.wait_ge(sSpC.h, sSpC.n)
                    s.dma_start(out.transpose([1, 0, 2]),
                                last.rearrange("p (k t) -> p k t", k=DC)).then_inc(sSt.h, 16)
                    sSt.n += 16
                    s.wait_ge(sSt.h, sSt.n)
        else:
            with nc.Block() as blk:
                @blk.sync
                def _(s):
                    s.wait_ge(sSpF.h, sSpF.n)

    return nc


_NC_CACHE = None


def kernel(**inputs):
    global _NC_CACHE
    maps = host_prep(inputs)
    if _NC_CACHE is None:
        _NC_CACHE = build_nc()
    nc = _NC_CACHE
    res = run_bass_kernel_spmd(nc, maps, core_ids=list(range(NCORES)))
    full = np.empty((NA, S, D), np.float32)
    for c in range(NCORES):
        shard = np.asarray(res.results[c]["out"], np.float32).reshape(D, NA, 256)
        full[:, 256 * c:256 * c + 256, :] = shard.transpose(1, 2, 0)
    return full
